# version 26
# baseline (speedup 1.0000x reference)
import math
from contextlib import ExitStack

import numpy as np

import concourse.bass as bass
import concourse.mybir as mybir
from concourse.alu_op_type import AluOpType
from concourse.bass_utils import run_bass_kernel_spmd

F32 = mybir.dt.float32
BF16 = mybir.dt.bfloat16
AF = mybir.ActivationFunctionType
AX = mybir.AxisListType.X
MULT = AluOpType.mult
ADD = AluOpType.add
SUB = AluOpType.subtract

D = 2048
S = 16
NDT = 16
RK = 128
PD = 160
HID = 8192
PI = math.pi
EPS_LN = 1e-5
NCORES = 8
TSL = 8192 // NCORES
TC = 512
NH = TSL // TC
NTT = TC // 128
PSLOT = 8
PDEF = 4
KSL = HID // NCORES
NKC = KSL // 128
MMF = 512
RG = [list(range(NCORES))]


class Buf:
    __slots__ = ("writer", "readers", "floor")

    def __init__(self, floor=None):
        self.writer = None
        self.readers = {}
        self.floor = dict(floor) if floor else None


class TK:
    ENGS = ("v", "s", "g", "t", "q")

    def __init__(self, nc):
        self.nc = nc
        self.tick = {e: nc.alloc_semaphore(f"tk_{e}") for e in self.ENGS}
        self.cnt = dict.fromkeys(self.ENGS, 0)
        self.NSD = 12
        self.dsems = [nc.alloc_semaphore(f"dma{i}") for i in range(self.NSD)]
        self.duse = [0] * self.NSD
        self.dix = 0
        self.dcnt = 0
        self.ccs = []
        self.waited = {e: {} for e in self.ENGS}
        self.prog = {e: [] for e in self.ENGS}
        self.floor = {}

    def _emit_wait(self, ce, pkey, val):
        w = self.waited[ce]
        if w.get(pkey, 0) >= val:
            return
        w[pkey] = val
        if pkey in self.tick:
            sem = self.tick[pkey]
        elif isinstance(pkey, tuple) and pkey[0] == "D":
            sem = self.dsems[pkey[1]]
        else:
            sem = self.ccs[pkey[1]]
        self.prog[ce].append(lambda eng, s=sem, v=val: eng.wait_ge(s, v))

    def _deps(self, ce, reads, writes):
        edges = {}

        def add(pk, v):
            if edges.get(pk, 0) < v:
                edges[pk] = v

        for b in reads:
            if b.writer:
                add(*b.writer)
            if b.floor:
                for pk, v in b.floor.items():
                    add(pk, v)
        for b in writes:
            if b.writer:
                add(*b.writer)
            for pk, v in b.readers.items():
                add(pk, v)
            if b.floor:
                for pk, v in b.floor.items():
                    add(pk, v)
        for pk in sorted(edges, key=str):
            self._emit_wait(ce, pk, edges[pk])

    def _note(self, key, val, reads, writes):
        for b in writes:
            b.writer = (key, val)
            b.readers = {}
        for b in reads:
            if b.readers.get(key, 0) < val:
                b.readers[key] = val

    def op(self, ce, emit, reads=(), writes=()):
        self._deps(ce, reads, writes)
        tick = self.tick[ce]
        self.cnt[ce] += 1

        def run(eng, emit=emit, tick=tick):
            emit(eng).then_inc(tick, 1)

        self.prog[ce].append(run)
        self._note(ce, self.cnt[ce], reads, writes)

    def ops(self, ce, emits, reads=(), writes=()):
        self._deps(ce, reads, writes)
        tick = self.tick[ce]
        self.cnt[ce] += 1

        def run(eng, emits=tuple(emits), tick=tick):
            inst = None
            for em in emits:
                inst = em(eng)
            inst.then_inc(tick, 1)

        self.prog[ce].append(run)
        self._note(ce, self.cnt[ce], reads, writes)

    def dma(self, out_ap, in_ap, reads=(), writes=(), ce="q"):
        i_s = self.dix
        self.dix = (self.dix + 1) % self.NSD
        if self.duse[i_s] > 0:
            self._emit_wait(ce, ("D", i_s), 16 * self.duse[i_s])
        self._deps(ce, reads, writes)
        self.duse[i_s] += 1
        self.dcnt += 16
        dsem, val = self.dsems[i_s], 16 * self.duse[i_s]

        def run(eng, o=out_ap, i=in_ap):
            eng.dma_start(out=o, in_=i).then_inc(dsem, 16)

        self.prog[ce].append(run)
        self._note(("D", i_s), val, reads, writes)

    def collective(self, kind, op, ins_ap, outs_ap, reads=(), writes=()):
        ce = "g"
        self._deps(ce, reads, writes)
        sem = self.nc.alloc_semaphore(f"cc_{len(self.ccs)}")
        idx = len(self.ccs)
        self.ccs.append(sem)

        def run(eng, k=kind, o=op, i=ins_ap, u=outs_ap, s=sem):
            eng.collective_compute(k, o, ins=i, outs=u, replica_groups=RG).then_inc(s)

        self.prog[ce].append(run)
        self._note(("c", idx), 1, reads, writes)

    def retire(self, *bufs):
        for b in bufs:
            if b.writer:
                pk, v = b.writer
                if self.floor.get(pk, 0) < v:
                    self.floor[pk] = v
            for pk, v in b.readers.items():
                if self.floor.get(pk, 0) < v:
                    self.floor[pk] = v

    def buf(self):
        return Buf(self.floor)

    def bufs(self, n):
        return [Buf(self.floor) for _ in range(n)]

    def finalize(self, final_wait=True):
        nc = self.nc
        with nc.Block() as blk:
            for ce, deco in (("q", blk.sync), ("v", blk.vector), ("s", blk.scalar),
                             ("g", blk.gpsimd), ("t", blk.tensor)):
                prog = self.prog[ce]
                fw = final_wait if ce == "q" else None

                def body(eng, prog=prog, fw=fw):
                    for f in prog:
                        f(eng)
                    if fw:
                        for i_s in range(self.NSD):
                            if self.duse[i_s]:
                                eng.wait_ge(self.dsems[i_s],
                                            16 * self.duse[i_s])

                deco(body)


def build_program(dbg=False):
    nc = bass.Bass("TRN2", target_bir_lowering=False)
    tk = TK(nc)

    dx = nc.dram_tensor("x", [TSL, D], F32, kind="ExternalInput")
    dxhalo = nc.dram_tensor("xhalo", [3, D], F32, kind="ExternalInput")
    dident = nc.dram_tensor("ident", [128, 128], F32, kind="ExternalInput")
    dgm = nc.dram_tensor("gm", [128, NCORES], F32, kind="ExternalInput")
    dgm2 = nc.dram_tensor("gm2", [128, NCORES], F32, kind="ExternalInput")
    dconvw = nc.dram_tensor("convw", [128, NDT, 4], F32, kind="ExternalInput")
    dconvb = nc.dram_tensor("convb", [128, NDT], F32, kind="ExternalInput")
    dalog = nc.dram_tensor("alog", [128, NDT, S], F32, kind="ExternalInput")
    ddparam = nc.dram_tensor("dparam", [128, NDT], F32, kind="ExternalInput")
    ddtpb = nc.dram_tensor("dtpb", [128, NDT], F32, kind="ExternalInput")
    dxpwT = nc.dram_tensor("xpwT", [128, NDT, PD], F32, kind="ExternalInput")
    ddtpwT = nc.dram_tensor("dtpwT", [128, D], F32, kind="ExternalInput")
    dampw = nc.dram_tensor("ampw_blk", [NDT, 128, NDT, 128], F32,
                           kind="ExternalInput")
    dphw = nc.dram_tensor("phw_blk", [NDT, 128, NDT, 128], F32,
                          kind="ExternalInput")
    dampb = nc.dram_tensor("ampb", [128, NDT], F32, kind="ExternalInput")
    dphb = nc.dram_tensor("phb", [128, NDT], F32, kind="ExternalInput")
    dentwT = nc.dram_tensor("entwT_blk", [2, NDT, 128, NKC, 128], F32,
                            kind="ExternalInput")
    dentpw = nc.dram_tensor("entpw_sl", [2, 4, 128, NKC, MMF], F32,
                            kind="ExternalInput")
    dentb = nc.dram_tensor("entb_sl", [128, 2, NKC], F32, kind="ExternalInput")
    dentpb = nc.dram_tensor("entpb", [128, 2, NDT], F32, kind="ExternalInput")
    dpathw = nc.dram_tensor("pathw_blk", [2, 2, NDT, 128, NDT, 128], F32,
                            kind="ExternalInput")
    dpathb = nc.dram_tensor("pathb", [128, 2, 2, NDT], F32, kind="ExternalInput")
    dphase = nc.dram_tensor("phasec", [128, 2, 2, NDT], F32, kind="ExternalInput")
    dln = {nm: nc.dram_tensor(nm, [128, 2, NDT], F32, kind="ExternalInput")
           for nm in ("ln3g", "ln3b", "ln4g", "ln4b")}
    dout = nc.dram_tensor("out", [TSL, D], F32, kind="ExternalOutput")

    dmin = [nc.dram_tensor(f"m_in{l}", [128, 4, NDT, MMF], BF16,
                           kind="Internal") for l in range(2)]
    dmout = [nc.dram_tensor(f"m_out{l}", [128, 4, NDT, MMF], BF16,
                            kind="Internal") for l in range(2)]
    dbin = [nc.dram_tensor(f"mb_in{l}", [1, D], BF16, kind="Internal")
            for l in range(2)]
    dbout = [nc.dram_tensor(f"mb_out{l}", [1, D], BF16, kind="Internal")
             for l in range(2)]
    dcin = nc.dram_tensor("cin", [128, 2 * NDT * S], F32, kind="Internal")
    dcout = nc.dram_tensor("cout", [128 * NCORES, 2 * NDT * S], F32,
                           kind="Internal")
    dx1st = nc.dram_tensor("x1st", [TSL, D], F32, kind="Internal")
    dbcr = nc.dram_tensor("bcrows", [32, TSL], F32, kind="Internal")
    dspdt = nc.dram_tensor("spill_dt", [128, NDT * TC], F32, kind="Internal")
    dspy = nc.dram_tensor("spill_y", [128, NDT * TC], F32, kind="Internal")

    dbg_outs = {}
    if dbg:
        for nm in ("dbg_xc", "dbg_dt", "dbg_y", "dbg_h2", "dbg_h2p",
                   "dbg_l30", "dbg_l40", "dbg_e1", "dbg_l31", "dbg_p1"):
            dbg_outs[nm] = nc.dram_tensor(nm, [128, NDT, TSL], F32,
                                          kind="ExternalOutput")
    B_dbg = {nm: tk.buf() for nm in dbg_outs}

    A_ = nc.alloc_sbuf_tensor
    P_ = nc.alloc_psum_tensor

    ident = A_("identS", [128, 128], F32)
    identb = A_("identbS", [128, 128], BF16)
    gm = A_("gmS", [128, NCORES], F32)
    gm2 = A_("gm2S", [128, NCORES], F32)
    convw = A_("convwS", [128, NDT, 4], F32)
    convb = A_("convbS", [128, NDT], F32)
    alog = A_("alogS", [128, NDT, S], F32)
    Asb = A_("AsbS", [128, NDT, S], F32)
    dparam = A_("dparamS", [128, NDT], F32)
    dtpb = A_("dtpbS", [128, NDT], F32)
    ampb = A_("ampbS", [128, NDT], F32)
    phb = A_("phbS", [128, NDT], F32)
    entpb = A_("entpbS", [128, 2, NDT], F32)
    pathb = A_("pathbS", [128, 2, 2, NDT], F32)
    phasec = A_("phasecS", [128, 2, 2, NDT], F32)
    modc = A_("modcS", [128, 2, 2, NDT], F32)
    pbmod = A_("pbmodS", [128, 2, 2, NDT], F32)
    lns = {nm: A_(nm + "S", [128, 2, NDT], F32) for nm in dln}
    onescol = A_("onescolS", [128, 1], F32)
    onesrow = A_("onesrowS", [1, 128], F32)
    c_pihalf = A_("cpihalfS", [128, 1], F32)
    c_tiny = A_("ctinyS", [128, 1], F32)
    c_eps = A_("cepsS", [128, 1], F32)
    hlast = A_("hlastS", [128, NDT, S], F32)
    acore = A_("acoreS", [128, NDT * S], F32)
    bcore = A_("bcoreS", [128, NDT * S], F32)
    dts = A_("dtsS", [128, NDT], F32)
    hinit = A_("hinitS", [128, NDT * S], F32)
    lnh = A_("lnhS", [128, NDT * S], F32)
    betac = [A_(f"betacS{l}", [128, NDT], F32) for l in range(2)]
    entbb = A_("entbbS", [128, 2, NKC], BF16)
    statm = A_("statmS", [128, 1], F32)
    statq = A_("statqS", [128, 1], F32)
    statv = A_("statvS", [128, 1], F32)
    statm2 = A_("statm2S", [128, 1], F32)
    statq2 = A_("statq2S", [128, 1], F32)
    statv2 = A_("statv2S", [128, 1], F32)
    scr = A_("scrS", [128, 2048], F32)
    bcflat = A_("bcflatS", [128, TC], F32)
    cdtl = A_("cdtlS", [128, NDT], F32)


    pbk = [P_(f"pb{i}", [128, MMF], F32) for i in range(4)]
    ptf = [P_(f"ptf{i}", [128, MMF], F32) for i in range(2)]
    ptb = [P_(f"ptb{i}", [128, 128], BF16) for i in range(2)]
    pt = [ptf[0][:, 0:128], ptf[1][:, 0:128]]
    pst = [ptf[0][0:1, :], ptf[1][0:1, :]]

    (B_ident, B_identb, B_gm, B_gm2, B_convw, B_convb, B_alog, B_Asb,
     B_dparam, B_dtpb, B_ampb, B_phb, B_entpb, B_pathb, B_phase, B_mod,
     B_pbmod, B_onescol, B_onesrow, B_acore, B_bcore, B_dts, B_hinit,
     B_lnh, B_entbb, B_statm, B_statq, B_statv, B_scr, B_rowm, B_rowr,
     B_mrep, B_rrep, B_bcflat, B_statm2, B_statq2, B_statv2) = tk.bufs(37)
    statR = [(statm, statq, statv, B_statm, B_statq, B_statv),
             (statm2, statq2, statv2, B_statm2, B_statq2, B_statv2)]
    B_lns = {nm: tk.buf() for nm in lns}
    B_hlast = [[tk.buf() for _ in range(S)] for _ in range(NDT)]
    B_betac = tk.bufs(2)
    B_pb = tk.bufs(4)
    B_pt = tk.bufs(2)
    B_ptb = tk.bufs(2)
    B_min = tk.bufs(2)
    B_mout = tk.bufs(2)
    B_bin = tk.bufs(2)
    B_bout = tk.bufs(2)
    B_cin, B_cout, B_x1st, B_bcd = tk.bufs(4)

    tk.dma(ident[:], dident[:], writes=[B_ident])
    tk.dma(gm[:], dgm[:], writes=[B_gm])
    tk.dma(gm2[:], dgm2[:], writes=[B_gm2])
    tk.dma(convw[:], dconvw[:], writes=[B_convw])
    tk.dma(convb[:], dconvb[:], writes=[B_convb])
    tk.dma(alog[:], dalog[:], writes=[B_alog])
    tk.dma(dparam[:], ddparam[:], writes=[B_dparam])
    tk.dma(dtpb[:], ddtpb[:], writes=[B_dtpb])
    tk.dma(ampb[:], dampb[:], writes=[B_ampb])
    tk.dma(phb[:], dphb[:], writes=[B_phb])
    tk.dma(entpb[:], dentpb[:], writes=[B_entpb])
    tk.dma(pathb[:], dpathb[:], writes=[B_pathb])
    tk.dma(phasec[:], dphase[:], writes=[B_phase])
    for nm in lns:
        tk.dma(lns[nm][:], dln[nm][:], writes=[B_lns[nm]])
    tk.op("v", lambda e: e.memset(onescol[:], 1.0), writes=[B_onescol])
    B_cpihalf, B_ctiny, B_ceps = tk.bufs(3)
    tk.op("v", lambda e: e.memset(c_pihalf[:], PI / 2), writes=[B_cpihalf])
    tk.op("v", lambda e: e.memset(c_tiny[:], 1e-38), writes=[B_ctiny])
    tk.op("v", lambda e: e.memset(c_eps[:], EPS_LN), writes=[B_ceps])
    tk.op("v", lambda e: e.memset(onesrow[:], 1.0), writes=[B_onesrow])
    tk.op("v", lambda e: e.tensor_copy(identb[:], ident[:]),
          reads=[B_ident], writes=[B_identb])
    tk.op("s", lambda e: e.activation(Asb[:], alog[:], AF.Exp),
          reads=[B_alog], writes=[B_Asb])
    tk.op("v", lambda e: e.tensor_scalar_mul(Asb[:], Asb[:], -1.0),
          reads=[B_Asb], writes=[B_Asb])
    tk.op("v", lambda e: e.tensor_scalar(modc[:], phasec[:], PI, -2 * PI,
                                         AluOpType.is_gt, MULT),
          reads=[B_phase], writes=[B_mod])
    tk.op("v", lambda e: e.tensor_tensor(modc[:], modc[:], phasec[:], ADD),
          reads=[B_mod, B_phase], writes=[B_mod])
    tk.op("v", lambda e: e.tensor_scalar(pbmod[:], phasec[:], -PI, 2 * PI,
                                         AluOpType.is_lt, MULT),
          reads=[B_phase], writes=[B_pbmod])
    tk.op("v", lambda e: e.tensor_tensor(modc[:], modc[:], pbmod[:], ADD),
          reads=[B_mod, B_pbmod], writes=[B_mod])
    tk.op("s", lambda e: e.activation(pbmod[:], modc[:], AF.Abs),
          reads=[B_mod], writes=[B_pbmod])
    tk.op("s", lambda e: e.activation(modc[:], modc[:], AF.Sin),
          reads=[B_mod], writes=[B_mod])
    tk.op("s", lambda e: e.activation(pbmod[:], pbmod[:], AF.Sin,
                                      bias=c_pihalf[:], scale=-1.0),
          reads=[B_pbmod, B_cpihalf], writes=[B_pbmod])
    tk.op("v", lambda e: e.tensor_tensor(modc[:], modc[:], pbmod[:], ADD),
          reads=[B_mod, B_pbmod], writes=[B_mod])
    tk.op("v", lambda e: e.tensor_tensor(pbmod[:], pathb[:], modc[:], MULT),
          reads=[B_pathb, B_mod], writes=[B_pbmod])

    def mm_group(pbuf, ptile_ap, pieces, reads):
        ems = []
        n = len(pieces)
        for i, (lh, rh) in enumerate(pieces):
            ems.append(lambda e, lh=lh, rh=rh, st=(i == 0), sp=(i == n - 1):
                       e.matmul(ptile_ap, lh, rh, start=st, stop=sp))
        tk.ops("t", ems, reads=list(reads), writes=[pbuf])

    def stats_cols(x_ap, nrows, b_x, eps=EPS_LN, par=0):
        sm, sq, sv, B_sm, B_sq, B_sv = statR[par]
        tk.op("v", lambda e: e.reduce_sum(sm[0:nrows, :], x_ap, axis=AX),
              reads=[b_x], writes=[B_sm])
        tk.op("s", lambda e: e.activation(scr[0:nrows, 0:D], x_ap, AF.Square,
                                          accum_out=sq[0:nrows, :]),
              reads=[b_x], writes=[B_scr, B_sq])
        tk.op("v", lambda e: e.tensor_scalar_mul(sm[0:nrows, :],
                                                 sm[0:nrows, :], 1.0 / D),
              reads=[B_sm], writes=[B_sm])
        tk.op("v", lambda e: e.tensor_tensor(sv[0:nrows, :], sm[0:nrows, :],
                                             sm[0:nrows, :], MULT),
              reads=[B_sm], writes=[B_sv])
        tk.op("v", lambda e: e.scalar_tensor_tensor(sq[0:nrows, :],
                                                    sq[0:nrows, :], 1.0 / D,
                                                    sv[0:nrows, :], MULT, SUB),
              reads=[B_sq, B_sv], writes=[B_sq])
        tk.op("s", lambda e: e.activation(sq[0:nrows, :], sq[0:nrows, :],
                                          AF.Ln, bias=c_eps[0:nrows, :]),
              reads=[B_sq, B_ceps], writes=[B_sq])
        tk.op("s", lambda e: e.activation(sq[0:nrows, :], sq[0:nrows, :],
                                          AF.Exp, scale=-0.5),
              reads=[B_sq], writes=[B_sq])

    es_s1 = ExitStack()
    es_s2 = ExitStack()
    es_dt = ExitStack()
    es_y = ExitStack()

    s1 = lambda n, sh, dt=F32: es_s1.enter_context(nc.sbuf_tensor(n, sh, dt))
    s2 = lambda n, sh, dt=F32: es_s2.enter_context(nc.sbuf_tensor(n, sh, dt))

    yc = es_y.enter_context(nc.sbuf_tensor("yP", [128, NDT, TC], F32))
    dtc = es_dt.enter_context(nc.sbuf_tensor("dtP", [128, NDT, TC], F32))
    dtp = [dtc, dtc]
    yp = [yc, yc]
    crep = [s2(f"crepP{i}", [128, TC]) for i in range(2)]
    abuf = [s2(f"abufP{i}", [128, TC]) for i in range(3)]
    ubuf = [s2(f"ubufP{i}", [128, TC]) for i in range(3)]
    lt = s1("ltP", [128, NDT, TC + 3])
    halosave = s1("halosaveP", [128, NDT, 3])
    xpwb = s1("xpwbP", [128, NDT, PD], BF16)
    dtpwb = s1("dtpwbP", [128, D], BF16)
    dtrawb = s1("dtrawbP", [128, TC], BF16)
    onesTC = s1("onesTCP", [128, TC])
    brep = [s1(f"brepP{i}", [128, TC]) for i in range(2)]
    bbuf = [s1(f"bbufP{i}", [128, TC]) for i in range(3)]
    hbuf = [s1(f"hbufP{i}", [128, TC]) for i in range(2)]
    ylbuf = [s1(f"ylbufP{i}", [128, TC]) for i in range(2)]
    xpl = s1("xplP", [128, 2, D])
    xcbfp = s1("xcbfP", [128, NDT, TC], BF16)

    entwTbM = s1("entwTbM", [128, 2, NKC * 128], BF16)
    B_entwTbM = tk.bufs(2)
    munits = [(j, pc) for j in range(4) for pc in range(NDT)]

    def m_dma_pw(l, j):
        js = j % 2
        jb = [B_xcbf[js * NKC + kc] for kc in range(NKC)]
        tk.dma(xcbfp[:, js * NKC:(js + 1) * NKC, :], dentpw[l, j],
               reads=jb, writes=jb, ce="g")

    def m_dma_wt(l, u):
        j, pc = munits[u]
        iw = u % 2
        tk.dma(entwTbM[:, iw, :], dentwT[l, pc].rearrange("p a f -> p (a f)"),
               reads=[B_entwTbM[iw]], writes=[B_entwTbM[iw]], ce="g")

    def m_product_gen(l):
        if l == 1:
            tk.collective("AllReduce", ADD, [dmin[0].ap().opt()],
                          [dmout[0].ap().opt()], reads=[B_min[0]],
                          writes=[B_mout[0]])
            tk.collective("AllReduce", ADD, [dbin[0].ap().opt()],
                          [dbout[0].ap().opt()], reads=[B_bin[0]],
                          writes=[B_bout[0]])
        tk.dma(entbb[:, l, :], dentb[:, l, :], reads=[B_entbb],
               writes=[B_entbb], ce="g")
        m_dma_pw(l, 0)
        m_dma_wt(l, 0)
        m_dma_wt(l, 1)
        for u, (j, pc) in enumerate(munits):
            iw = u % 2
            js = j % 2
            jb = [B_xcbf[js * NKC + kc] for kc in range(NKC)]
            ptile, bpt = pbk[2 + iw], B_pb[2 + iw]
            pieces = [(entwTbM[:, iw, kc * 128:(kc + 1) * 128],
                       xcbfp[:, js * NKC + kc, :]) for kc in range(NKC)]
            mm_group(bpt, ptile[:], pieces, [B_entwTbM[iw]] + jb)
            if u + 2 < len(munits):
                m_dma_wt(l, u + 2)
            if pc == NKC and j + 1 < 4:
                m_dma_pw(l, j + 1)
            tk.op("s", lambda e, ptile=ptile:
                  e.copy(dtrawb[:], ptile[:]),
                  reads=[bpt], writes=[B_dtrawb])
            tk.dma(dmin[l][:, j, pc, :], dtrawb[:],
                   reads=[B_dtrawb], writes=[B_min[l]])
            if pc == NDT - 1:
                bt, bbt = pst[j % 2], B_pt[j % 2]
                bpieces = [(entbb[:, l, kc:kc + 1],
                            xcbfp[:, js * NKC + kc, :]) for kc in range(NKC)]
                mm_group(bbt, bt, bpieces, [B_entbb] + jb)
                tk.op("s", lambda e, bt=bt: e.copy(dtrawb[0:1, :], bt),
                      reads=[bbt], writes=[B_dtrawb])
                tk.dma(dbin[l][0:1, j * MMF:(j + 1) * MMF], dtrawb[0:1, :],
                       reads=[B_dtrawb], writes=[B_bin[l]])
            yield

    mgen = [m_product_gen(0), m_product_gen(1)]

    B_dtc = [tk.buf() for _ in range(NDT)]
    B_yc = [tk.buf() for _ in range(NDT)]
    B_dtp = [B_dtc, B_dtc]
    B_yp = [B_yc, B_yc]
    B_cdtl = tk.buf()
    B_spdt, B_spy = tk.bufs(2)
    B_lt = [tk.buf() for _ in range(NDT)]
    B_lthalo = [tk.buf() for _ in range(NDT)]
    (B_halosave, B_xpwb, B_dtpwb, B_dtrawb, B_bcsb, B_onesTC) = tk.bufs(6)
    B_brep, B_hb, B_yl = (tk.bufs(2) for _ in range(3))
    B_bb = tk.bufs(3)
    B_crep = tk.bufs(2)
    B_ab, B_ub = tk.bufs(3), tk.bufs(3)
    B_xpl = tk.bufs(2)
    B_xcbf = [tk.buf() for _ in range(NDT)]
    cacc = scr[:, 1024:1024 + TC]
    B_cacc = B_scr

    tk.op("v", lambda e: e.memset(onesTC[:], 1.0), writes=[B_onesTC])
    tk.dma(xpwb[:], dxpwT[:], writes=[B_xpwb], ce="g")
    tk.dma(dtpwb[:], ddtpwT[:], writes=[B_dtpwb], ce="g")

    for h in range(NH):
        t0 = h * TC
        if True:
            if h == 0:
                with nc.sbuf_tensor("haloT", [3, D], F32,
                                    side="right") as haloT:
                    B_halo = tk.buf()
                    tk.dma(haloT[:], dxhalo[:], writes=[B_halo])
                    stats_cols(haloT[:], 3, B_halo)
                    tk.op("v", lambda e: e.tensor_scalar(haloT[:], haloT[:],
                                                         statm[0:3, :],
                                                         statq[0:3, :], SUB, MULT),
                          reads=[B_halo, B_statm, B_statq], writes=[B_halo])
                    for dt in range(NDT):
                        ptile, bpt = pt[dt % 2], B_pt[dt % 2]
                        tk.op("t", lambda e, dt=dt, ptile=ptile:
                              e.transpose(ptile[:, 0:3],
                                          haloT[:, dt * 128:(dt + 1) * 128],
                                          ident[0:3, 0:3]),
                              reads=[B_halo, B_ident], writes=[bpt])
                        tk.op("s", lambda e, dt=dt, ptile=ptile:
                              e.copy(lt[:, dt, 0:3], ptile[:, 0:3]),
                              reads=[bpt], writes=[B_lthalo[dt]])
                    tk.retire(B_halo)
            else:
                for dt in range(NDT):
                    tk.op("v", lambda e, dt=dt:
                          e.tensor_copy(lt[:, dt, 0:3], halosave[:, dt, :]),
                          reads=[B_halosave], writes=[B_lthalo[dt]])

            for i in range(NTT):
                ix = i % 2
                tk.dma(xpl[:, ix, :], dx[t0 + i * 128:t0 + (i + 1) * 128, :],
                       reads=[B_xpl[ix]], writes=[B_xpl[ix]])
                stats_cols(xpl[:, ix, :], 128, B_xpl[ix])
                tk.op("v", lambda e, ix=ix:
                      e.tensor_scalar(xpl[:, ix, :], xpl[:, ix, :], statm[:],
                                      statq[:], SUB, MULT),
                      reads=[B_xpl[ix], B_statm, B_statq], writes=[B_xpl[ix]])
                for dt in range(NDT):
                    ptile, bpt = pt[dt % 2], B_pt[dt % 2]
                    tk.op("t", lambda e, ix=ix, dt=dt, ptile=ptile:
                          e.transpose(ptile, xpl[:, ix, dt * 128:(dt + 1) * 128],
                                      ident[:]),
                          reads=[B_xpl[ix], B_ident], writes=[bpt])
                    tk.op("s", lambda e, i=i, dt=dt, ptile=ptile:
                          e.copy(lt[:, dt, 3 + i * 128:3 + (i + 1) * 128],
                                 ptile),
                          reads=[bpt], writes=[B_lt[dt]])


        if h + 1 < NH:
            for dt in range(NDT):
                tk.op("v", lambda e, dt=dt:
                      e.tensor_copy(halosave[:, dt, :], lt[:, dt, TC:TC + 3]),
                      reads=[B_lt[dt]], writes=[B_halosave])

        if True:
            for dt in range(NDT):
                tk.op("v", lambda e, dt=dt:
                      e.tensor_scalar_mul(cacc, lt[:, dt, 0:TC],
                                          convw[:, dt, 0:1]),
                      reads=[B_lt[dt], B_lthalo[dt], B_convw], writes=[B_cacc])
                for j in range(1, 4):
                    tk.op("v", lambda e, dt=dt, j=j:
                          e.scalar_tensor_tensor(cacc, lt[:, dt, j:j + TC],
                                                 convw[:, dt, j:j + 1], cacc,
                                                 MULT, ADD),
                          reads=[B_lt[dt], B_lthalo[dt], B_convw, B_cacc],
                          writes=[B_cacc])
                tk.op("s", lambda e, dt=dt:
                      e.activation(lt[:, dt, 3:3 + TC], cacc, AF.Silu,
                                   bias=convb[:, dt:dt + 1]),
                      reads=[B_cacc, B_convb], writes=[B_lt[dt]])
                tk.op("v", lambda e, dt=dt:
                      e.tensor_copy(xcbfp[:, dt, :], lt[:, dt, 3:3 + TC]),
                      reads=[B_lt[dt]], writes=[B_xcbf[dt]])
                if dbg:
                    tk.dma(dbg_outs["dbg_xc"][:, dt, t0:t0 + TC],
                           lt[:, dt, 3:3 + TC],
                           reads=[B_lt[dt]], writes=[B_dbg["dbg_xc"]])

            ptile, bpt = pbk[0], B_pb[0]
            pieces = [(xpwb[:, c, 0:128], xcbfp[:, c, :]) for c in range(NDT)]
            mm_group(bpt, ptile[:], pieces, [B_xpwb] + B_xcbf)
            tk.op("s", lambda e, ptile=ptile: e.copy(dtrawb[:], ptile[:]),
                  reads=[bpt], writes=[B_dtrawb])
            ptile2, bpt2 = pbk[1], B_pb[1]
            pieces = [(xpwb[:, c, 128:160], xcbfp[:, c, :]) for c in range(NDT)]
            mm_group(bpt2, ptile2[0:32, :], pieces, [B_xpwb] + B_xcbf)
            tk.op("v", lambda e, ptile2=ptile2:
                  e.tensor_copy(scr[0:32, 0:TC], ptile2[0:32, :]),
                  reads=[bpt2], writes=[B_bcsb, B_scr])
            tk.dma(dbcr[:, t0:t0 + TC], scr[0:32, 0:TC],
                   reads=[B_bcsb, B_scr], writes=[B_bcd])

            for dt in range(NDT):
                ptile, bpt = pbk[2 + dt % 2], B_pb[2 + dt % 2]
                mm_group(bpt, ptile[:],
                         [(dtpwb[:, dt * 128:(dt + 1) * 128], dtrawb[:])],
                         [B_dtpwb, B_dtrawb])
                tk.op("s", lambda e, dt=dt, ptile=ptile:
                      e.activation(scr[:, 0:TC], ptile[:], AF.Exp,
                                   bias=dtpb[:, dt:dt + 1]),
                      reads=[bpt, B_dtpb], writes=[B_scr])
                tk.op("s", lambda e, dt=dt:
                      e.activation(dtp[h][:, dt, :], scr[:, 0:TC], AF.Ln,
                                   bias=onescol[:]),
                      reads=[B_scr, B_onescol], writes=[B_dtp[h][dt]])
                if dbg:
                    tk.dma(dbg_outs["dbg_dt"][:, dt, t0:t0 + TC],
                           dtp[h][:, dt, :],
                           reads=[B_dtp[h][dt]], writes=[B_dbg["dbg_dt"]])
                if h == 0:
                    tk.op("v", lambda e, dt=dt:
                          e.reduce_sum(dts[:, dt:dt + 1], dtp[h][:, dt, :],
                                       axis=AX),
                          reads=[B_dtp[h][dt]], writes=[B_dts])
                else:
                    tk.op("v", lambda e, dt=dt:
                          e.reduce_sum(statv[:], dtp[h][:, dt, :], axis=AX),
                          reads=[B_dtp[h][dt]], writes=[B_statv])
                    tk.op("v", lambda e, dt=dt:
                          e.tensor_tensor(dts[:, dt:dt + 1], dts[:, dt:dt + 1],
                                          statv[:], ADD),
                          reads=[B_dts, B_statv], writes=[B_dts])
                tk.op("v", lambda e, dt=dt, h=h:
                      e.tensor_scalar_mul(yp[h][:, dt, :], lt[:, dt, 3:3 + TC],
                                          dparam[:, dt:dt + 1]),
                      reads=[B_lt[dt], B_dparam], writes=[B_yp[h][dt]])
                tk.op("v", lambda e, dt=dt, h=h:
                      e.tensor_tensor(lt[:, dt, 3:3 + TC], lt[:, dt, 3:3 + TC],
                                      dtp[h][:, dt, :], MULT),
                      reads=[B_lt[dt], B_dtp[h][dt]], writes=[B_lt[dt]])


        def emit_reps1(s, par):
            tk.dma(bcflat[0:1, :], dbcr[s:s + 1, t0:t0 + TC],
                   reads=[B_bcd, B_bcflat], writes=[B_bcflat])
            ptile, bpt = pbk[0], B_pb[0]
            tk.op("t", lambda e, ptile=ptile:
                  e.matmul(ptile[:, 0:TC], onesrow[:], bcflat[0:1, :],
                           start=True, stop=True),
                  reads=[B_onesrow, B_bcflat], writes=[bpt])
            tk.op("s", lambda e, par=par, ptile=ptile:
                  e.copy(brep[par][:], ptile[:, 0:TC]),
                  reads=[bpt], writes=[B_brep[par]])
            tk.dma(bcflat[0:1, :], dbcr[16 + s:17 + s, t0:t0 + TC],
                   reads=[B_bcd, B_bcflat, bpt], writes=[B_bcflat])
            ptile2, bpt2 = pbk[1], B_pb[1]
            tk.op("t", lambda e, ptile2=ptile2:
                  e.matmul(ptile2[:, 0:TC], onesrow[:], bcflat[0:1, :],
                           start=True, stop=True),
                  reads=[B_onesrow, B_bcflat], writes=[bpt2])
            tk.op("s", lambda e, par=par, ptile2=ptile2:
                  e.copy(crep[par][:], ptile2[:, 0:TC]),
                  reads=[bpt2], writes=[B_crep[par]])

        def emit_a1(s, dt, ip, h=h):
            tk.op("s", lambda e, dt=dt, s=s, ip=ip, h=h:
                  e.activation(abuf[ip][:], dtp[h][:, dt, :], AF.Exp,
                               scale=Asb[:, dt, s:s + 1]),
                  reads=[B_dtp[h][dt], B_Asb], writes=[B_ab[ip]])

        iters = [(s, dt) for s in range(S) for dt in range(NDT)]

        def emit_u1(k):
            s, dt = iters[k]
            ip = k % 3
            tk.op("v", lambda e, dt=dt, ip=ip, par=s % 2:
                  e.tensor_tensor(ubuf[ip][:], lt[:, dt, 3:3 + TC],
                                  brep[par][:], MULT),
                  reads=[B_lt[dt], B_brep[s % 2]], writes=[B_ub[ip]])

        def emit_b1(k):
            ip = k % 3
            tk.op("s", lambda e, ip=ip:
                  e.activation(bbuf[ip][:], ubuf[ip][:], AF.Relu),
                  reads=[B_ub[ip]], writes=[B_bb[ip]])

        emit_reps1(0, 0)
        emit_a1(*iters[0], 0)
        emit_u1(0)
        emit_b1(0)
        emit_a1(*iters[1], 1)
        emit_u1(1)
        emit_b1(1)
        for k, (s, dt) in enumerate(iters):
            par = s % 2
            ip = k % 3
            if dt == NDT - 4 and s + 1 < S:
                emit_reps1(s + 1, (s + 1) % 2)
            if k + 2 < len(iters):
                emit_a1(*iters[k + 2], (k + 2) % 3)
                emit_u1(k + 2)
                emit_b1(k + 2)
            if h == 0:
                init, init_reads = 0.0, []
            else:
                init, init_reads = hlast[:, dt, s:s + 1], [B_hlast[dt][s]]
            hb = k % 2
            tk.op("v", lambda e, ip=ip, hb=hb, init=init:
                  e.tensor_tensor_scan(hbuf[hb][:], abuf[ip][:], bbuf[ip][:],
                                       init, MULT, ADD),
                  reads=[B_ab[ip], B_bb[ip]] + init_reads, writes=[B_hb[hb]])
            tk.op("s", lambda e, dt=dt, s=s, hb=hb:
                  e.copy(hlast[:, dt, s:s + 1], hbuf[hb][:, TC - 1:TC]),
                  reads=[B_hb[hb]], writes=[B_hlast[dt][s]])
            tk.op("v", lambda e, hb=hb, par=par:
                  e.tensor_tensor(ylbuf[hb][:], hbuf[hb][:], crep[par][:],
                                  MULT),
                  reads=[B_hb[hb], B_crep[par]], writes=[B_yl[hb]])
            tk.op("v", lambda e, dt=dt, hb=hb, h=h:
                  e.tensor_tensor(yp[h][:, dt, :], yp[h][:, dt, :],
                                  ylbuf[hb][:], ADD),
                  reads=[B_yp[h][dt], B_yl[hb]], writes=[B_yp[h][dt]])
            if k % 3 == 2 and mgen[h] is not None:
                try:
                    next(mgen[h])
                except StopIteration:
                    mgen[h] = None

        for dt in range(NDT):
            if h == 0:
                cinit, creads = 0.0, []
            else:
                cinit = cdtl[:, dt:dt + 1]
                creads = [B_cdtl]
            tk.op("v", lambda e, dt=dt, cinit=cinit, h=h:
                  e.tensor_tensor_scan(dtp[h][:, dt, :], onesTC[:],
                                       dtp[h][:, dt, :], cinit, MULT, ADD),
                  reads=[B_onesTC, B_dtp[h][dt]] + creads, writes=[B_dtp[h][dt]])

        if h == 0:
            for dt in range(NDT):
                tk.op("v", lambda e, dt=dt:
                      e.tensor_copy(cdtl[:, dt:dt + 1], dtc[:, dt, TC - 1:TC]),
                      reads=[B_dtc[dt]], writes=[B_cdtl])
            tk.dma(dspdt[:], dtc[:].rearrange("p a t -> p (a t)"),
                   reads=list(B_dtc), writes=[B_spdt])
            tk.dma(dspy[:], yc[:].rearrange("p a t -> p (a t)"),
                   reads=list(B_yc), writes=[B_spy])

    for h_ in range(2):
        while mgen[h_] is not None:
            try:
                next(mgen[h_])
            except StopIteration:
                mgen[h_] = None

    for dt in range(NDT):
        tk.op("s", lambda e, dt=dt:
              e.activation(acore[:, dt * S:(dt + 1) * S], Asb[:, dt, :], AF.Exp,
                           scale=dts[:, dt:dt + 1]),
              reads=[B_Asb, B_dts], writes=[B_acore])
        tk.op("v", lambda e, dt=dt:
              e.tensor_copy(bcore[:, dt * S:(dt + 1) * S], hlast[:, dt, :]),
              reads=[B_hlast[dt][ss] for ss in range(S)], writes=[B_bcore])
    tk.dma(dcin[:, 0:NDT * S], acore[:], reads=[B_acore], writes=[B_cin])
    tk.dma(dcin[:, NDT * S:], bcore[:], reads=[B_bcore], writes=[B_cin])
    tk.collective("AllGather", AluOpType.bypass, [dcin.ap().opt()],
                  [dcout.ap().opt()], reads=[B_cin], writes=[B_cout])
    tk.collective("AllReduce", ADD, [dmin[1].ap().opt()],
                  [dmout[1].ap().opt()], reads=[B_min[1]], writes=[B_mout[1]])
    tk.collective("AllReduce", ADD, [dbin[1].ap().opt()],
                  [dbout[1].ap().opt()], reads=[B_bin[1]], writes=[B_bout[1]])
    with nc.sbuf_tensor("agb", [128, 2, 2 * NDT * S], F32,
                        side="right") as agb:
        cmp0 = scr[:, 0:NDT * S]
        B_agb = tk.bufs(2)
        B_cmp0 = B_scr
        NS = NDT * S
        tk.dma(agb[:, 0, :], dcout[0:128, :], reads=[B_cout], writes=[B_agb[0]])
        tk.op("v", lambda e: e.memset(hinit[:], 0.0), writes=[B_hinit])
        for r in range(NCORES):
            rb = r % 2
            if r + 1 < NCORES:
                tk.dma(agb[:, 1 - rb, :], dcout[128 * (r + 1):128 * (r + 2), :],
                       reads=[B_cout, B_agb[1 - rb]], writes=[B_agb[1 - rb]])
            tk.op("v", lambda e, r=r, rb=rb:
                  e.tensor_scalar(cmp0, agb[:, rb, 0:NS], gm[:, r:r + 1],
                                  gm2[:, r:r + 1], MULT, ADD),
                  reads=[B_agb[rb], B_gm, B_gm2], writes=[B_cmp0])
            tk.op("v", lambda e: e.tensor_tensor(hinit[:], hinit[:], cmp0, MULT),
                  reads=[B_hinit, B_cmp0], writes=[B_hinit])
            tk.op("v", lambda e, r=r, rb=rb:
                  e.tensor_scalar(cmp0, agb[:, rb, NS:], gm[:, r:r + 1],
                                  None, MULT),
                  reads=[B_agb[rb], B_gm], writes=[B_cmp0])
            tk.op("v", lambda e: e.tensor_tensor(hinit[:], hinit[:], cmp0,
                                                 ADD),
                  reads=[B_hinit, B_cmp0], writes=[B_hinit])
        tk.op("s", lambda e: e.activation(lnh[:], hinit[:], AF.Ln,
                                          bias=c_tiny[:]),
              reads=[B_hinit, B_ctiny], writes=[B_lnh])
        tk.retire(*B_agb)

    for bl in (B_lt, B_lthalo, [B_halosave, B_xpwb, B_dtpwb, B_dtrawb, B_bcsb,
                                B_onesTC], B_brep, B_bb, B_hb, B_yl, B_xpl,
               B_xcbf):
        tk.retire(*bl)
    es_s1.close()

    es_p5 = ExitStack()
    es_p5t = ExitStack()
    es_pre = ExitStack()
    xnT = es_p5.enter_context(nc.sbuf_tensor("xnTP", [128, NDT, TSL], BF16, side="right"))
    B_xnT = [tk.buf() for _ in range(NDT)]
    B_yT = tk.bufs(2)
    B_x0 = tk.bufs(2)
    B_xnb = tk.bufs(2)
    B_awb = tk.bufs(2)
    B_pwbx = tk.bufs(2)

    def emit_crep2(t0, s, par):
        tk.dma(bcflat[0:1, :], dbcr[16 + s:17 + s, t0:t0 + TC],
               reads=[B_bcd, B_bcflat], writes=[B_bcflat])
        ptile2, bpt2 = pbk[1], B_pb[1]
        tk.op("t", lambda e, ptile2=ptile2:
              e.matmul(ptile2[:, 0:TC], onesrow[:], bcflat[0:1, :],
                       start=True, stop=True),
              reads=[B_onesrow, B_bcflat], writes=[bpt2])
        tk.op("s", lambda e, par=par, ptile2=ptile2:
              e.copy(crep[par][:], ptile2[:, 0:TC]),
              reads=[bpt2], writes=[B_crep[par]])

    def emit_pass2(h, crep_pre=False):
        t0 = h * TC

        def emit_w2(s, dt, ip):
            tk.op("s", lambda e, dt=dt, s=s, ip=ip:
                  e.activation(abuf[ip][:], dtc[:, dt, :], AF.Exp,
                               scale=Asb[:, dt, s:s + 1],
                               bias=lnh[:, dt * S + s:dt * S + s + 1]),
                  reads=[B_dtc[dt], B_Asb, B_lnh], writes=[B_ab[ip]])

        iters = [(s, dt) for s in range(S) for dt in range(NDT)]
        if not crep_pre:
            emit_crep2(t0, 0, 0)
        emit_w2(0, 0, 0)
        emit_w2(*iters[1], 1)
        for k, (s, dt) in enumerate(iters):
            par = s % 2
            ip = k % 3
            if k + 2 < len(iters):
                s2, dt2 = iters[k + 2]
                emit_w2(s2, dt2, (k + 2) % 3)
                if dt == NDT - 8 and s + 1 < S:
                    emit_crep2(t0, s + 1, (s + 1) % 2)
            tk.op("v", lambda e, ip=ip, par=par:
                  e.tensor_tensor(ubuf[ip][:], abuf[ip][:], crep[par][:],
                                  MULT),
                  reads=[B_ab[ip], B_crep[par]], writes=[B_ub[ip]])
            tk.op("v", lambda e, dt=dt, ip=ip:
                  e.tensor_tensor(yc[:, dt, :], yc[:, dt, :],
                                  ubuf[ip][:], ADD),
                  reads=[B_yc[dt], B_ub[ip]], writes=[B_yc[dt]])
        if dbg:
            for dt in range(NDT):
                tk.dma(dbg_outs["dbg_y"][:, dt, t0:t0 + TC], yc[:, dt, :],
                       reads=[B_yc[dt]], writes=[B_dbg["dbg_y"]])

    def emit_p5(h):
        t0 = h * TC

        def phase1(i):
            ib = i % 2
            tk.dma(x0t[:, ib, :], dx[t0 + i * 128:t0 + (i + 1) * 128, :],
                   reads=[B_x0[ib]], writes=[B_x0[ib]], ce="s")
            for dt in range(NDT):
                ptile, bpt = pt[dt % 2], B_pt[dt % 2]
                tk.op("t", lambda e, i=i, dt=dt, ptile=ptile:
                      e.transpose(ptile, yc[:, dt, i * 128:(i + 1) * 128],
                                  ident[:]),
                      reads=[B_yc[dt], B_ident], writes=[bpt])
                tk.op("s", lambda e, dt=dt, ib=ib, ptile=ptile:
                      e.copy(yTt[:, ib, dt * 128:(dt + 1) * 128], ptile),
                      reads=[bpt], writes=[B_yT[ib]])

        phase1(0)
        phase1(1)
        for i in range(NTT):
            ib = i % 2
            sm, sq, _, B_sm, B_sq, _ = statR[ib]
            stats_cols(yTt[:, ib, :], 128, B_yT[ib], par=ib)
            tk.op("v", lambda e, ib=ib, sm=sm, sq=sq:
                  e.tensor_scalar(yTt[:, ib, :], yTt[:, ib, :], sm[:],
                                  sq[:], SUB, MULT),
                  reads=[B_yT[ib], B_sm, B_sq], writes=[B_yT[ib]])
            tk.op("v", lambda e, ib=ib:
                  e.tensor_tensor(yTt[:, ib, :], yTt[:, ib, :],
                                  x0t[:, ib, :], ADD),
                  reads=[B_yT[ib], B_x0[ib]], writes=[B_yT[ib]])
            tk.dma(dx1st[t0 + i * 128:t0 + (i + 1) * 128, :], yTt[:, ib, :],
                   reads=[B_yT[ib]], writes=[B_x1st], ce="s")
            stats_cols(yTt[:, ib, :], 128, B_yT[ib], par=ib)
            tk.op("v", lambda e, ib=ib, sm=sm, sq=sq:
                  e.tensor_scalar(xnbft[:, ib, :], yTt[:, ib, :], sm[:],
                                  sq[:], SUB, MULT),
                  reads=[B_yT[ib], B_sm, B_sq], writes=[B_xnb[ib]])
            for dt in range(NDT):
                ptile, bpt = ptb[dt % 2], B_ptb[dt % 2]
                tk.op("t", lambda e, dt=dt, ib=ib, ptile=ptile:
                      e.transpose(ptile[:],
                                  xnbft[:, ib, dt * 128:(dt + 1) * 128],
                                  identb[:]),
                      reads=[B_xnb[ib], B_identb], writes=[bpt])
                tk.op("s", lambda e, i=i, dt=dt, t0=t0, ptile=ptile:
                      e.copy(xnT[:, dt, t0 + i * 128:t0 + (i + 1) * 128],
                             ptile[:]),
                      reads=[bpt], writes=[B_xnT[dt]])
            if i + 2 < NTT:
                phase1(i + 2)

    emit_pass2(1)
    awb = es_pre.enter_context(nc.sbuf_tensor("awbP", [128, 2, NDT, 128],
                                              BF16, side="right"))
    pwbx = es_pre.enter_context(nc.sbuf_tensor("pwbxP", [128, 2, NDT, 128],
                                               BF16, side="right"))
    yTt = es_p5t.enter_context(nc.sbuf_tensor("yTtP", [128, 2, D], F32,
                                              side="right"))
    x0t = es_p5t.enter_context(nc.sbuf_tensor("x0tP", [128, 2, D], F32,
                                              side="right"))
    xnbft = es_p5t.enter_context(nc.sbuf_tensor("xnbftP", [128, 2, D], BF16,
                                                side="right"))
    for pg in range(2):
        tk.dma(awb[:, pg, :, :], dampw[pg],
               reads=[B_awb[pg]], writes=[B_awb[pg]], ce="g")
        tk.dma(pwbx[:, pg, :, :], dphw[pg],
               reads=[B_pwbx[pg]], writes=[B_pwbx[pg]], ce="g")
    emit_p5(1)

    emit_crep2(0, 0, 0)
    for dt in range(NDT):
        tk.dma(dtc[:, dt, :], dspdt[:, dt * TC:(dt + 1) * TC],
               reads=[B_spdt, B_dtc[dt]], writes=[B_dtc[dt]])
        tk.dma(yc[:, dt, :], dspy[:, dt * TC:(dt + 1) * TC],
               reads=[B_spy, B_yc[dt]], writes=[B_yc[dt]])

    emit_pass2(0, crep_pre=True)
    emit_p5(0)

    tk.retire(*B_yT, *B_x0, *B_xnb)
    es_p5t.close()
    tk.retire(*B_dtc)
    for bl in (B_crep, B_ab, B_ub):
        tk.retire(*bl)
    es_s1_dummy = None
    es_s2.close()
    es_dt.close()
    tk.retire(*B_yc)
    es_y.close()

    es_ffn = ExitStack()
    h2 = es_ffn.enter_context(nc.sbuf_tensor("h2F", [128, NDT, TSL], F32))
    h2bf = es_ffn.enter_context(nc.sbuf_tensor("h2bfF", [128, NDT, TSL], BF16))
    B_h2 = [tk.bufs(2) for _ in range(NDT)]
    B_h2bf = [tk.bufs(2) for _ in range(NDT)]
    B_h2a = [b for bb in B_h2 for b in bb]
    B_h2bfa = [b for bb in B_h2bf for b in bb]

    with nc.sbuf_tensor("ampch", [128, 2, NH, MMF], F32) as ampch, \
         nc.sbuf_tensor("thch", [128, 2, NH, MMF], F32) as thch, \
         nc.sbuf_tensor("sctP", [128, MMF], F32, side="right") as sct:
        B_ach = [[tk.buf() for _ in range(NH)] for _ in range(2)]
        B_tch = [[tk.buf() for _ in range(NH)] for _ in range(2)]
        B_sct = tk.buf()
        for pq in range(8):
            for pp in range(2):
                pg = pq * 2 + pp
                wb = pg % 2
                for tc in range(NH):
                    fs = slice(tc * MMF, (tc + 1) * MMF)
                    ptile, bpt = pbk[tc % 2], B_pb[tc % 2]
                    pieces = [(awb[:, wb, c, :], xnT[:, c, fs])
                              for c in range(NDT)]
                    mm_group(bpt, ptile[:], pieces, [B_awb[wb]] + B_xnT)
                    tk.op("s", lambda e, pp=pp, tc=tc, pg=pg, ptile=ptile:
                          e.activation(ampch[:, pp, tc, :], ptile[:], AF.Sigmoid,
                                       bias=ampb[:, pg:pg + 1]),
                          reads=[bpt, B_ampb], writes=[B_ach[pp][tc]])
                    ptile2, bpt2 = pbk[2 + tc % 2], B_pb[2 + tc % 2]
                    pieces = [(pwbx[:, wb, c, :], xnT[:, c, fs])
                              for c in range(NDT)]
                    mm_group(bpt2, ptile2[:], pieces, [B_pwbx[wb]] + B_xnT)
                    tk.op("s", lambda e, pp=pp, tc=tc, pg=pg, ptile2=ptile2:
                          e.activation(thch[:, pp, tc, :], ptile2[:], AF.Tanh,
                                       bias=phb[:, pg:pg + 1]),
                          reads=[bpt2, B_phb], writes=[B_tch[pp][tc]])
                if pg + 2 < 16:
                    tk.dma(awb[:, wb, :, :], dampw[pg + 2],
                           reads=[B_awb[wb]], writes=[B_awb[wb]], ce="g")
                    tk.dma(pwbx[:, wb, :, :], dphw[pg + 2],
                           reads=[B_pwbx[wb]], writes=[B_pwbx[wb]], ce="g")
            for pp in range(2):
                pg = pq * 2 + pp
                for tc in range(NH):
                    fs = slice(tc * MMF, (tc + 1) * MMF)
                    tk.op("s", lambda e, pp=pp, tc=tc:
                          e.activation(sct[:], thch[:, pp, tc, :], AF.Abs),
                          reads=[B_tch[pp][tc]], writes=[B_sct])
                    tk.op("s", lambda e, pp=pp, tc=tc:
                          e.activation(thch[:, pp, tc, :], thch[:, pp, tc, :],
                                       AF.Sin, scale=PI),
                          reads=[B_tch[pp][tc]], writes=[B_tch[pp][tc]])
                    tk.op("s", lambda e: e.activation(sct[:], sct[:], AF.Sin,
                                                      bias=c_pihalf[:], scale=-PI),
                          reads=[B_sct, B_cpihalf], writes=[B_sct])
                    tk.op("v", lambda e, pp=pp, tc=tc:
                          e.tensor_tensor(thch[:, pp, tc, :],
                                          thch[:, pp, tc, :], sct[:], ADD),
                          reads=[B_tch[pp][tc], B_sct], writes=[B_tch[pp][tc]])
                    tk.op("v", lambda e, pp=pp, tc=tc, pg=pg, fs=fs:
                          e.tensor_tensor(h2[:, pg, fs], thch[:, pp, tc, :],
                                          ampch[:, pp, tc, :], MULT),
                          reads=[B_tch[pp][tc], B_ach[pp][tc]],
                          writes=[B_h2[pg][tc]])
                    tk.op("v", lambda e, pg=pg, fs=fs, tc=tc:
                          e.tensor_copy(h2bf[:, pg, fs], h2[:, pg, fs]),
                          reads=[B_h2[pg][tc]], writes=[B_h2bf[pg][tc]])
        for pp in range(2):
            tk.retire(*B_ach[pp], *B_tch[pp])
        tk.retire(*B_awb, *B_pwbx, B_sct)
    if dbg:
        for dt in range(NDT):
            tk.dma(dbg_outs["dbg_h2p"][:, dt, :], h2[:, dt, :],
                   reads=B_h2[dt], writes=[B_dbg["dbg_h2p"]])
    es_pre.close()
    tk.retire(*B_xnT)
    es_p5.close()

    def ffn_ln_tc(l, gname, bname, tc):
        if True:
            fs = slice(tc * MMF, (tc + 1) * MMF)
            em1 = []
            for jc in range(NDT):
                em1.append(lambda e, jc=jc, fs=fs, st=(jc == 0),
                           sp=(jc == NDT - 1):
                           e.matmul(pst[0], onescol[:], h2[:, jc, fs],
                                    start=st, stop=sp))
            tk.ops("t", em1, reads=[B_onescol] + [B_h2[jc][tc]
                                                  for jc in range(NDT)],
                   writes=[B_pt[0]])
            sq_bufs = tk.bufs(4)
            for jc in range(min(3, NDT)):
                sc = scr[:, (jc % 4) * MMF:(jc % 4) * MMF + MMF]
                tk.op("s", lambda e, jc=jc, fs=fs, sc=sc:
                      e.activation(sc, h2[:, jc, fs], AF.Square),
                      reads=[B_h2[jc][tc]], writes=[sq_bufs[jc % 4]])
            for jc in range(NDT):
                sc = scr[:, (jc % 4) * MMF:(jc % 4) * MMF + MMF]
                if jc + 3 < NDT:
                    jn = jc + 3
                    sn = scr[:, (jn % 4) * MMF:(jn % 4) * MMF + MMF]
                    tk.op("s", lambda e, jn=jn, fs=fs, sn=sn:
                          e.activation(sn, h2[:, jn, fs], AF.Square),
                          reads=[B_h2[jn][tc]], writes=[sq_bufs[jn % 4]])
                tk.op("t", lambda e, jc=jc, sc=sc, st=(jc == 0),
                      sp=(jc == NDT - 1):
                      e.matmul(pst[1], onescol[:], sc, start=st, stop=sp),
                      reads=[B_onescol, sq_bufs[jc % 4]], writes=[B_pt[1]])
            tk.op("v", lambda e: e.tensor_scalar_mul(rowm[:], pst[0], 1.0 / D),
                  reads=[B_pt[0]], writes=[B_rowm])
            tk.op("v", lambda e: e.tensor_scalar_mul(rowr[:], pst[1], 1.0 / D),
                  reads=[B_pt[1]], writes=[B_rowr])
            tk.op("v", lambda e: e.tensor_tensor(scr[0:1, 0:MMF], rowm[:],
                                                 rowm[:], MULT),
                  reads=[B_rowm], writes=[B_scr])
            tk.op("v", lambda e: e.tensor_tensor(rowr[:], rowr[:],
                                                 scr[0:1, 0:MMF], SUB),
                  reads=[B_rowr, B_scr], writes=[B_rowr])
            tk.op("s", lambda e: e.activation(rowr[:], rowr[:], AF.Ln,
                                              bias=c_eps[0:1, :]),
                  reads=[B_rowr, B_ceps], writes=[B_rowr])
            tk.op("s", lambda e: e.activation(rowr[:], rowr[:], AF.Exp,
                                              scale=-0.5),
                  reads=[B_rowr], writes=[B_rowr])
            tk.op("t", lambda e: e.matmul(pbk[0][:], onesrow[:], rowm[:],
                                          start=True, stop=True),
                  reads=[B_onesrow, B_rowm], writes=[B_pb[0]])
            tk.op("s", lambda e: e.copy(mrep[:], pbk[0][:]),
                  reads=[B_pb[0]], writes=[B_mrep])
            tk.op("t", lambda e: e.matmul(pbk[1][:], onesrow[:], rowr[:],
                                          start=True, stop=True),
                  reads=[B_onesrow, B_rowr], writes=[B_pb[1]])
            tk.op("s", lambda e: e.copy(rrep[:], pbk[1][:]),
                  reads=[B_pb[1]], writes=[B_rrep])
            nb = tk.bufs(4)
            for jc in range(NDT):
                sc = scr[:, (jc % 4) * MMF:(jc % 4) * MMF + MMF]
                bsc = nb[jc % 4]
                tk.op("v", lambda e, jc=jc, fs=fs, sc=sc:
                      e.tensor_tensor(sc, h2[:, jc, fs], mrep[:], SUB),
                      reads=[B_h2[jc][tc], B_mrep], writes=[bsc])
                tk.op("v", lambda e, sc=sc: e.tensor_tensor(sc, sc, rrep[:],
                                                            MULT),
                      reads=[bsc, B_rrep], writes=[bsc])
                tk.op("v", lambda e, jc=jc, fs=fs, sc=sc, l=l:
                      e.tensor_scalar(h2[:, jc, fs], sc,
                                      lns[gname][:, l, jc:jc + 1],
                                      lns[bname][:, l, jc:jc + 1], MULT, ADD),
                      reads=[bsc, B_lns[gname], B_lns[bname]],
                      writes=[B_h2[jc][tc]])
                tk.op("v", lambda e, jc=jc, fs=fs:
                      e.tensor_copy(h2bf[:, jc, fs], h2[:, jc, fs]),
                      reads=[B_h2[jc][tc]], writes=[B_h2bf[jc][tc]])
            tk.retire(*nb, *sq_bufs)

    with nc.sbuf_tensor("rowmF", [1, MMF], F32) as rowm, \
         nc.sbuf_tensor("rowrF", [1, MMF], F32) as rowr, \
         nc.sbuf_tensor("mrepF", [128, MMF], F32) as mrep, \
         nc.sbuf_tensor("rrepF", [128, MMF], F32) as rrep, \
         nc.sbuf_tensor("mblkF", [128, 2, NDT, MMF], BF16) as mblk, \
         nc.sbuf_tensor("enttF", [128, 2, MMF], F32) as entt, \
         nc.sbuf_tensor("browF", [1, D], BF16) as brow, \
         nc.sbuf_tensor("pwb2F", [128, PSLOT, NDT, 128], BF16) as pwb2, \
         nc.sbuf_tensor("tptF", [128, 2, MMF], F32) as tpt:
        B_mblk = tk.bufs(2)
        B_entt = tk.bufs(2)
        B_brow = tk.buf()
        B_pwb2 = tk.bufs(PSLOT)
        B_tpt = tk.bufs(2)
        for l in range(2):
            tk.dma(brow[:], dbout[l][0:1, :], reads=[B_bout[l], B_brow],
                   writes=[B_brow])
            for jc in range(NDT):
                ptile, bpt = ptb[jc % 2], B_ptb[jc % 2]
                tk.op("t", lambda e, jc=jc, ptile=ptile:
                      e.transpose(ptile[:, 0:1],
                                  brow[:, jc * 128:(jc + 1) * 128],
                                  identb[0:1, 0:1]),
                      reads=[B_brow, B_identb], writes=[bpt])
                tk.op("s", lambda e, jc=jc, l=l, ptile=ptile:
                      e.copy(betac[l][:, jc:jc + 1], ptile[:, 0:1]),
                      reads=[bpt], writes=[B_betac[l]])
            tk.op("v", lambda e, l=l:
                  e.tensor_tensor(betac[l][:], betac[l][:], entpb[:, l, :], ADD),
                  reads=[B_betac[l], B_entpb], writes=[B_betac[l]])

            for jb in range(4):
                ib = jb % 2
                tk.dma(mblk[:, ib, :, :], dmout[l][:, jb],
                       reads=[B_mout[l], B_mblk[ib]], writes=[B_mblk[ib]])
                tc = 0
                fs = slice(tc * MMF, (tc + 1) * MMF)
                for pj in range(4):
                    pjg = jb * 4 + pj
                    ptile, bpt = pbk[pj % 4], B_pb[pj % 4]
                    pieces = [(mblk[:, ib, c, pj * 128:(pj + 1) * 128],
                               h2bf[:, c, fs]) for c in range(NDT)]
                    mm_group(bpt, ptile[:], pieces,
                             [B_mblk[ib]] + [B_h2bf[c][tc]
                                             for c in range(NDT)])
                    ie = pj % 2
                    tk.op("s", lambda e, l=l, pjg=pjg, ie=ie, ptile=ptile:
                          e.activation(entt[:, ie, :], ptile[:], AF.Identity,
                                       bias=betac[l][:, pjg:pjg + 1]),
                          reads=[bpt, B_betac[l]], writes=[B_entt[ie]])
                    tk.op("v", lambda e, pjg=pjg, fs=fs, ie=ie:
                          e.tensor_tensor(h2[:, pjg, fs], h2[:, pjg, fs],
                                          entt[:, ie, :], ADD),
                          reads=[B_h2[pjg][tc], B_entt[ie]],
                          writes=[B_h2[pjg][tc]])
            if dbg and l == 1:
                for dt in range(NDT):
                    tk.dma(dbg_outs["dbg_e1"][:, dt, :], h2[:, dt, :],
                           reads=B_h2[dt], writes=[B_dbg["dbg_e1"]])
            ffn_ln_tc(l, "ln3g", "ln3b", 0)
            for jb in range(4):
                ib = jb % 2
                tk.dma(mblk[:, ib, :, :], dmout[l][:, jb],
                       reads=[B_mout[l], B_mblk[ib]], writes=[B_mblk[ib]])
                tc = 1
                fs = slice(tc * MMF, (tc + 1) * MMF)
                for pj in range(4):
                    pjg = jb * 4 + pj
                    ptile, bpt = pbk[pj % 4], B_pb[pj % 4]
                    pieces = [(mblk[:, ib, c, pj * 128:(pj + 1) * 128],
                               h2bf[:, c, fs]) for c in range(NDT)]
                    mm_group(bpt, ptile[:], pieces,
                             [B_mblk[ib]] + [B_h2bf[c][tc]
                                             for c in range(NDT)])
                    ie = pj % 2
                    tk.op("s", lambda e, l=l, pjg=pjg, ie=ie, ptile=ptile:
                          e.activation(entt[:, ie, :], ptile[:], AF.Identity,
                                       bias=betac[l][:, pjg:pjg + 1]),
                          reads=[bpt, B_betac[l]], writes=[B_entt[ie]])
                    tk.op("v", lambda e, pjg=pjg, fs=fs, ie=ie:
                          e.tensor_tensor(h2[:, pjg, fs], h2[:, pjg, fs],
                                          entt[:, ie, :], ADD),
                          reads=[B_h2[pjg][tc], B_entt[ie]],
                          writes=[B_h2[pjg][tc]])
            ffn_ln_tc(l, "ln3g", "ln3b", 1)
            if dbg and l == 1:
                for dt in range(NDT):
                    tk.dma(dbg_outs["dbg_l31"][:, dt, :], h2[:, dt, :],
                           reads=B_h2[dt], writes=[B_dbg["dbg_l31"]])
            if dbg and l == 0:
                for dt in range(NDT):
                    tk.dma(dbg_outs["dbg_l30"][:, dt, :], h2[:, dt, :],
                           reads=B_h2[dt], writes=[B_dbg["dbg_l30"]])

            steps = [(p, pjg) for p in range(2) for pjg in range(NDT)]
            gc = [0]

            def path_group(p, pjg, islot, tc, l=l):
                fs = slice(tc * MMF, (tc + 1) * MMF)
                bk, ie = gc[0] % 4, gc[0] % 2
                gc[0] += 1
                ptile, bpt = pbk[bk], B_pb[bk]
                pieces = [(pwb2[:, islot, c, :], h2bf[:, c, fs])
                          for c in range(NDT)]
                mm_group(bpt, ptile[:], pieces,
                         [B_pwb2[islot]] + [B_h2bf[c][tc]
                                            for c in range(NDT)])
                tk.op("s", lambda e, l=l, p=p, pjg=pjg, ie=ie, ptile=ptile:
                      e.activation(tpt[:, ie, :], ptile[:], AF.Identity,
                                   scale=modc[:, l, p, pjg:pjg + 1],
                                   bias=pbmod[:, l, p, pjg:pjg + 1]),
                      reads=[bpt, B_mod, B_pbmod], writes=[B_tpt[ie]])
                tk.op("v", lambda e, pjg=pjg, fs=fs, ie=ie:
                      e.tensor_tensor(h2[:, pjg, fs], h2[:, pjg, fs],
                                      tpt[:, ie, :], ADD),
                      reads=[B_h2[pjg][tc], B_tpt[ie]],
                      writes=[B_h2[pjg][tc]])

            for idx, (p, pjg) in enumerate(steps):
                if idx >= PDEF:
                    pd, pjd = steps[idx - PDEF]
                    path_group(pd, pjd, (idx - PDEF) % PSLOT, 1)
                islot = idx % PSLOT
                tk.dma(pwb2[:, islot, :, :], dpathw[l, p, pjg],
                       reads=[B_pwb2[islot]], writes=[B_pwb2[islot]], ce="g")
                path_group(p, pjg, islot, 0)
            if dbg and l == 1:
                for dt in range(NDT):
                    tk.dma(dbg_outs["dbg_p1"][:, dt, 0:MMF],
                           h2[:, dt, 0:MMF],
                           reads=[B_h2[dt][0]], writes=[B_dbg["dbg_p1"]])
            ffn_ln_tc(l, "ln4g", "ln4b", 0)
            for idx in range(len(steps) - PDEF, len(steps)):
                pd, pjd = steps[idx]
                path_group(pd, pjd, idx % PSLOT, 1)
            if dbg and l == 1:
                for dt in range(NDT):
                    tk.dma(dbg_outs["dbg_p1"][:, dt, MMF:TSL],
                           h2[:, dt, MMF:TSL],
                           reads=[B_h2[dt][1]], writes=[B_dbg["dbg_p1"]])
            ffn_ln_tc(l, "ln4g", "ln4b", 1)
            if dbg and l == 0:
                for dt in range(NDT):
                    tk.dma(dbg_outs["dbg_l40"][:, dt, :], h2[:, dt, :],
                           reads=[B_h2[dt]], writes=[B_dbg["dbg_l40"]])
            if dbg and l == 1:
                for dt in range(NDT):
                    tk.dma(dbg_outs["dbg_h2"][:, dt, :], h2[:, dt, :],
                           reads=B_h2[dt], writes=[B_dbg["dbg_h2"]])
        tk.retire(*B_mblk, *B_entt, B_brow, *B_pwb2, *B_tpt,
                  B_rowm, B_rowr, B_mrep, B_rrep)

    with nc.sbuf_tensor("h2TtF", [128, 2, D], F32) as h2Tt, \
         nc.sbuf_tensor("x1tF", [128, 2, D], F32) as x1t:
        B_h2T = tk.bufs(2)
        B_x1t = tk.bufs(2)
        def fphase1(it):
            ib = it % 2
            tk.dma(x1t[:, ib, :], dx1st[it * 128:(it + 1) * 128, :],
                   reads=[B_x1st, B_x1t[ib]], writes=[B_x1t[ib]], ce="s")
            for dt in range(NDT):
                ptile, bpt = pbk[dt % 2][:, 0:128], B_pb[dt % 2]
                tk.op("t", lambda e, it=it, dt=dt, ptile=ptile:
                      e.transpose(ptile, h2[:, dt, it * 128:(it + 1) * 128],
                                  ident[:]),
                      reads=[B_h2[dt][it // 4], B_ident], writes=[bpt])
                tk.op("s", lambda e, dt=dt, ib=ib, ptile=ptile:
                      e.copy(h2Tt[:, ib, dt * 128:(dt + 1) * 128], ptile),
                      reads=[bpt], writes=[B_h2T[ib]])

        fphase1(0)
        fphase1(1)
        for it in range(TSL // 128):
            ib = it % 2
            sm, sq, _, B_sm, B_sq, _ = statR[ib]
            stats_cols(h2Tt[:, ib, :], 128, B_h2T[ib], par=ib)
            tk.op("v", lambda e, ib=ib, sm=sm, sq=sq:
                  e.tensor_scalar(h2Tt[:, ib, :], h2Tt[:, ib, :], sm[:],
                                  sq[:], SUB, MULT),
                  reads=[B_h2T[ib], B_sm, B_sq], writes=[B_h2T[ib]])
            tk.op("v", lambda e, ib=ib:
                  e.tensor_tensor(x1t[:, ib, :], x1t[:, ib, :], h2Tt[:, ib, :],
                                  ADD),
                  reads=[B_x1t[ib], B_h2T[ib]], writes=[B_x1t[ib]])
            tk.dma(dout[it * 128:(it + 1) * 128, :], x1t[:, ib, :],
                   reads=[B_x1t[ib]], ce=("s" if it % 2 else "q"))
            if it + 2 < TSL // 128:
                fphase1(it + 2)

    es_ffn.close()
    tk.finalize()
    return nc


_PROG_CACHE = {}


def _get_program(dbg=False):
    if dbg not in _PROG_CACHE:
        _PROG_CACHE[dbg] = build_program(dbg=dbg)
    return _PROG_CACHE[dbg]


def make_in_maps(inputs):
    f = np.float32
    asf = lambda a: np.ascontiguousarray(np.asarray(a, dtype=f))
    x = np.asarray(inputs["L0_attn_x"], dtype=f)[0]
    colmaj = lambda a: asf(np.asarray(a, f).reshape(NDT, 128).T)

    shared = {
        "ident": np.eye(128, dtype=f),
        "convw": asf(np.asarray(inputs["L0_attn_conv_w"], f)[:, 0, :]
                     .reshape(NDT, 128, 4).transpose(1, 0, 2)),
        "convb": colmaj(inputs["L0_attn_conv_b"]),
        "alog": asf(np.asarray(inputs["L0_attn_A_log"], f)
                    .reshape(NDT, 128, S).transpose(1, 0, 2)),
        "dparam": colmaj(inputs["L0_attn_D_param"]),
        "dtpb": colmaj(inputs["L0_attn_dt_proj_b"]),
        "xpwT": asf(np.asarray(inputs["L0_attn_x_proj_w"], f).T
                    .reshape(NDT, 128, PD).transpose(1, 0, 2)),
        "dtpwT": asf(np.asarray(inputs["L0_attn_dt_proj_w"], f).T),
        "ampw_blk": asf(np.asarray(inputs["L0_ffn_amp_w"], f)
                        .reshape(NDT, 128, NDT, 128).transpose(2, 1, 0, 3)),
        "phw_blk": asf(np.asarray(inputs["L0_ffn_ph_w"], f)
                       .reshape(NDT, 128, NDT, 128).transpose(2, 1, 0, 3)),
        "ampb": colmaj(inputs["L0_ffn_amp_b"]),
        "phb": colmaj(inputs["L0_ffn_ph_b"]),
        "entpb": asf(np.asarray(inputs["L0_ffn_ent_pb"], f)
                     .reshape(2, NDT, 128).transpose(2, 0, 1)),
        "pathw_blk": asf(np.asarray(inputs["L0_ffn_path_w"], f)
                         .reshape(2, 2, NDT, 128, NDT, 128)
                         .transpose(0, 1, 4, 3, 2, 5)),
        "pathb": asf(np.asarray(inputs["L0_ffn_path_b"], f)
                     .reshape(2, 2, NDT, 128).transpose(3, 0, 1, 2)),
        "phasec": asf(np.asarray(inputs["L0_ffn_phase"], f)
                      .reshape(2, 2, NDT, 128).transpose(3, 0, 1, 2)),
    }
    for nm, key in (("ln3g", "L0_ffn_ln3_g"), ("ln3b", "L0_ffn_ln3_b"),
                    ("ln4g", "L0_ffn_ln4_g"), ("ln4b", "L0_ffn_ln4_b")):
        shared[nm] = asf(np.asarray(inputs[key], f)
                         .reshape(2, NDT, 128).transpose(2, 0, 1))

    entw = np.asarray(inputs["L0_ffn_ent_w"], f)
    entpw = np.asarray(inputs["L0_ffn_ent_pw"], f)
    entb = np.asarray(inputs["L0_ffn_ent_b"], f)

    in_maps = []
    for c in range(NCORES):
        ks = slice(c * KSL, (c + 1) * KSL)
        g = np.zeros((128, NCORES), f)
        g[:, :c] = 1.0
        m = dict(shared)
        m["x"] = asf(x[c * TSL:(c + 1) * TSL])
        m["xhalo"] = (np.zeros((3, D), f) if c == 0
                      else asf(x[c * TSL - 3:c * TSL]))
        m["gm"] = g
        m["gm2"] = asf(1.0 - g)
        m["entwT_blk"] = asf(entw[:, :, ks].reshape(2, NDT, 128, NKC, 128)
                             .transpose(0, 1, 4, 3, 2))
        m["entpw_sl"] = asf(entpw[:, ks, :].reshape(2, NKC, 128, 4, MMF)
                            .transpose(0, 3, 2, 1, 4))
        m["entb_sl"] = asf(entb[:, ks].reshape(2, NKC, 128).transpose(2, 0, 1))
        in_maps.append(m)
    return in_maps


def kernel(**inputs):
    nc = _get_program(dbg=False)
    in_maps = make_in_maps(inputs)
    res = run_bass_kernel_spmd(nc, in_maps, core_ids=list(range(NCORES)))
    out = np.concatenate([np.asarray(res.results[c]["out"])
                          for c in range(NCORES)], axis=0)
    return out[None].astype(np.float32)


# version 27
# speedup vs baseline: 1.1475x; 1.1475x over previous
import math
from contextlib import ExitStack

import numpy as np

import concourse.bass as bass
import concourse.mybir as mybir
from concourse.alu_op_type import AluOpType
from concourse.bass_utils import run_bass_kernel_spmd

F32 = mybir.dt.float32
BF16 = mybir.dt.bfloat16
AF = mybir.ActivationFunctionType
AX = mybir.AxisListType.X
MULT = AluOpType.mult
ADD = AluOpType.add
SUB = AluOpType.subtract

D = 2048
S = 16
NDT = 16
RK = 128
PD = 160
HID = 8192
PI = math.pi
EPS_LN = 1e-5
NCORES = 8
TSL = 8192 // NCORES
TC = 512
NH = TSL // TC
NTT = TC // 128
PSLOT = 8
PDEF = 4
KSL = HID // NCORES
NKC = KSL // 128
MMF = 512
RG = [list(range(NCORES))]


class Buf:
    __slots__ = ("writer", "readers", "floor")

    def __init__(self, floor=None):
        self.writer = None
        self.readers = {}
        self.floor = dict(floor) if floor else None


class TK:
    ENGS = ("v", "s", "g", "t", "q")

    def __init__(self, nc):
        self.nc = nc
        self.tick = {e: nc.alloc_semaphore(f"tk_{e}") for e in self.ENGS}
        self.cnt = dict.fromkeys(self.ENGS, 0)
        self.NSD = 12
        self.dsems = [nc.alloc_semaphore(f"dma{i}") for i in range(self.NSD)]
        self.duse = [0] * self.NSD
        self.dix = 0
        self.dcnt = 0
        self.ccs = []
        self.waited = {e: {} for e in self.ENGS}
        self.prog = {e: [] for e in self.ENGS}
        self.floor = {}

    def _emit_wait(self, ce, pkey, val):
        w = self.waited[ce]
        if w.get(pkey, 0) >= val:
            return
        w[pkey] = val
        if pkey in self.tick:
            sem = self.tick[pkey]
        elif isinstance(pkey, tuple) and pkey[0] == "D":
            sem = self.dsems[pkey[1]]
        else:
            sem = self.ccs[pkey[1]]
        self.prog[ce].append(lambda eng, s=sem, v=val: eng.wait_ge(s, v))

    def _deps(self, ce, reads, writes):
        edges = {}

        def add(pk, v):
            if edges.get(pk, 0) < v:
                edges[pk] = v

        for b in reads:
            if b.writer:
                add(*b.writer)
            if b.floor:
                for pk, v in b.floor.items():
                    add(pk, v)
        for b in writes:
            if b.writer:
                add(*b.writer)
            for pk, v in b.readers.items():
                add(pk, v)
            if b.floor:
                for pk, v in b.floor.items():
                    add(pk, v)
        for pk in sorted(edges, key=str):
            self._emit_wait(ce, pk, edges[pk])

    def _note(self, key, val, reads, writes):
        for b in writes:
            b.writer = (key, val)
            b.readers = {}
        for b in reads:
            if b.readers.get(key, 0) < val:
                b.readers[key] = val

    def op(self, ce, emit, reads=(), writes=()):
        self._deps(ce, reads, writes)
        tick = self.tick[ce]
        self.cnt[ce] += 1

        def run(eng, emit=emit, tick=tick):
            emit(eng).then_inc(tick, 1)

        self.prog[ce].append(run)
        self._note(ce, self.cnt[ce], reads, writes)

    def ops(self, ce, emits, reads=(), writes=()):
        self._deps(ce, reads, writes)
        tick = self.tick[ce]
        self.cnt[ce] += 1

        def run(eng, emits=tuple(emits), tick=tick):
            inst = None
            for em in emits:
                inst = em(eng)
            inst.then_inc(tick, 1)

        self.prog[ce].append(run)
        self._note(ce, self.cnt[ce], reads, writes)

    def dma(self, out_ap, in_ap, reads=(), writes=(), ce="q"):
        i_s = self.dix
        self.dix = (self.dix + 1) % self.NSD
        if self.duse[i_s] > 0:
            self._emit_wait(ce, ("D", i_s), 16 * self.duse[i_s])
        self._deps(ce, reads, writes)
        self.duse[i_s] += 1
        self.dcnt += 16
        dsem, val = self.dsems[i_s], 16 * self.duse[i_s]

        def run(eng, o=out_ap, i=in_ap):
            eng.dma_start(out=o, in_=i).then_inc(dsem, 16)

        self.prog[ce].append(run)
        self._note(("D", i_s), val, reads, writes)

    def collective(self, kind, op, ins_ap, outs_ap, reads=(), writes=()):
        ce = "g"
        self._deps(ce, reads, writes)
        sem = self.nc.alloc_semaphore(f"cc_{len(self.ccs)}")
        idx = len(self.ccs)
        self.ccs.append(sem)

        def run(eng, k=kind, o=op, i=ins_ap, u=outs_ap, s=sem):
            eng.collective_compute(k, o, ins=i, outs=u, replica_groups=RG).then_inc(s)

        self.prog[ce].append(run)
        self._note(("c", idx), 1, reads, writes)

    def retire(self, *bufs):
        for b in bufs:
            if b.writer:
                pk, v = b.writer
                if self.floor.get(pk, 0) < v:
                    self.floor[pk] = v
            for pk, v in b.readers.items():
                if self.floor.get(pk, 0) < v:
                    self.floor[pk] = v

    def buf(self):
        return Buf(self.floor)

    def bufs(self, n):
        return [Buf(self.floor) for _ in range(n)]

    def finalize(self, final_wait=True):
        nc = self.nc
        with nc.Block() as blk:
            for ce, deco in (("q", blk.sync), ("v", blk.vector), ("s", blk.scalar),
                             ("g", blk.gpsimd), ("t", blk.tensor)):
                prog = self.prog[ce]
                fw = final_wait if ce == "q" else None

                def body(eng, prog=prog, fw=fw):
                    for f in prog:
                        f(eng)
                    if fw:
                        for i_s in range(self.NSD):
                            if self.duse[i_s]:
                                eng.wait_ge(self.dsems[i_s],
                                            16 * self.duse[i_s])

                deco(body)


def build_program(dbg=False):
    nc = bass.Bass("TRN2", target_bir_lowering=False)
    tk = TK(nc)

    dx = nc.dram_tensor("x", [TSL, D], F32, kind="ExternalInput")
    dxhalo = nc.dram_tensor("xhalo", [3, D], F32, kind="ExternalInput")
    dident = nc.dram_tensor("ident", [128, 128], F32, kind="ExternalInput")
    dgm = nc.dram_tensor("gm", [128, NCORES], F32, kind="ExternalInput")
    dgm2 = nc.dram_tensor("gm2", [128, NCORES], F32, kind="ExternalInput")
    dconvw = nc.dram_tensor("convw", [128, NDT, 4], F32, kind="ExternalInput")
    dconvb = nc.dram_tensor("convb", [128, NDT], F32, kind="ExternalInput")
    dalog = nc.dram_tensor("alog", [128, NDT, S], F32, kind="ExternalInput")
    ddparam = nc.dram_tensor("dparam", [128, NDT], F32, kind="ExternalInput")
    ddtpb = nc.dram_tensor("dtpb", [128, NDT], F32, kind="ExternalInput")
    dxpwT = nc.dram_tensor("xpwT", [128, NDT, PD], F32, kind="ExternalInput")
    ddtpwT = nc.dram_tensor("dtpwT", [128, D], F32, kind="ExternalInput")
    dampw = nc.dram_tensor("ampw_blk", [NDT, 128, NDT, 128], F32,
                           kind="ExternalInput")
    dphw = nc.dram_tensor("phw_blk", [NDT, 128, NDT, 128], F32,
                          kind="ExternalInput")
    dampb = nc.dram_tensor("ampb", [128, NDT], F32, kind="ExternalInput")
    dphb = nc.dram_tensor("phb", [128, NDT], F32, kind="ExternalInput")
    dentwT = nc.dram_tensor("entwT_blk", [2, NDT, 128, NKC, 128], F32,
                            kind="ExternalInput")
    dentpw = nc.dram_tensor("entpw_sl", [2, 4, 128, NKC, MMF], F32,
                            kind="ExternalInput")
    dentb = nc.dram_tensor("entb_sl", [128, 2, NKC], F32, kind="ExternalInput")
    dentpb = nc.dram_tensor("entpb", [128, 2, NDT], F32, kind="ExternalInput")
    dpathw = nc.dram_tensor("pathw_blk", [2, 2, NDT, 128, NDT, 128], F32,
                            kind="ExternalInput")
    dpathb = nc.dram_tensor("pathb", [128, 2, 2, NDT], F32, kind="ExternalInput")
    dphase = nc.dram_tensor("phasec", [128, 2, 2, NDT], F32, kind="ExternalInput")
    dln = {nm: nc.dram_tensor(nm, [128, 2, NDT], F32, kind="ExternalInput")
           for nm in ("ln3g", "ln3b", "ln4g", "ln4b")}
    dout = nc.dram_tensor("out", [TSL, D], F32, kind="ExternalOutput")

    dmin = [nc.dram_tensor(f"m_in{l}", [128, 4, NDT, MMF], BF16,
                           kind="Internal") for l in range(2)]
    dmout = [nc.dram_tensor(f"m_out{l}", [128, 4, NDT, MMF], BF16,
                            kind="Internal") for l in range(2)]
    dbin = [nc.dram_tensor(f"mb_in{l}", [1, D], BF16, kind="Internal")
            for l in range(2)]
    dbout = [nc.dram_tensor(f"mb_out{l}", [1, D], BF16, kind="Internal")
             for l in range(2)]
    dcin = nc.dram_tensor("cin", [128, 2 * NDT * S], F32, kind="Internal")
    dcout = nc.dram_tensor("cout", [128 * NCORES, 2 * NDT * S], F32,
                           kind="Internal")
    dx1st = nc.dram_tensor("x1st", [TSL, D], F32, kind="Internal")
    dbcr = nc.dram_tensor("bcrows", [32, TSL], F32, kind="Internal")
    dspdt = nc.dram_tensor("spill_dt", [128, NDT * TC], F32, kind="Internal")
    dspy = nc.dram_tensor("spill_y", [128, NDT * TC], F32, kind="Internal")

    dbg_outs = {}
    if dbg:
        for nm in ("dbg_xc", "dbg_dt", "dbg_y", "dbg_h2", "dbg_h2p",
                   "dbg_l30", "dbg_l40", "dbg_e1", "dbg_l31", "dbg_p1"):
            dbg_outs[nm] = nc.dram_tensor(nm, [128, NDT, TSL], F32,
                                          kind="ExternalOutput")
    B_dbg = {nm: tk.buf() for nm in dbg_outs}

    A_ = nc.alloc_sbuf_tensor
    P_ = nc.alloc_psum_tensor

    ident = A_("identS", [128, 128], F32)
    identb = A_("identbS", [128, 128], BF16)
    gm = A_("gmS", [128, NCORES], F32)
    gm2 = A_("gm2S", [128, NCORES], F32)
    convw = A_("convwS", [128, NDT, 4], F32)
    convb = A_("convbS", [128, NDT], F32)
    alog = A_("alogS", [128, NDT, S], F32)
    Asb = A_("AsbS", [128, NDT, S], F32)
    dparam = A_("dparamS", [128, NDT], F32)
    dtpb = A_("dtpbS", [128, NDT], F32)
    ampb = A_("ampbS", [128, NDT], F32)
    phb = A_("phbS", [128, NDT], F32)
    entpb = A_("entpbS", [128, 2, NDT], F32)
    pathb = A_("pathbS", [128, 2, 2, NDT], F32)
    phasec = A_("phasecS", [128, 2, 2, NDT], F32)
    modc = A_("modcS", [128, 2, 2, NDT], F32)
    pbmod = A_("pbmodS", [128, 2, 2, NDT], F32)
    lns = {nm: A_(nm + "S", [128, 2, NDT], F32) for nm in dln}
    onescol = A_("onescolS", [128, 1], F32)
    onesrow = A_("onesrowS", [1, 128], F32)
    c_pihalf = A_("cpihalfS", [128, 1], F32)
    c_tiny = A_("ctinyS", [128, 1], F32)
    c_eps = A_("cepsS", [128, 1], F32)
    hlast = A_("hlastS", [128, NDT, S], F32)
    acore = A_("acoreS", [128, NDT * S], F32)
    bcore = A_("bcoreS", [128, NDT * S], F32)
    dts = A_("dtsS", [128, NDT], F32)
    hinit = A_("hinitS", [128, NDT * S], F32)
    lnh = A_("lnhS", [128, NDT * S], F32)
    betac = [A_(f"betacS{l}", [128, NDT], F32) for l in range(2)]
    entbb = A_("entbbS", [128, 2, NKC], BF16)
    statm = A_("statmS", [128, 1], F32)
    statq = A_("statqS", [128, 1], F32)
    statv = A_("statvS", [128, 1], F32)
    statm2 = A_("statm2S", [128, 1], F32)
    statq2 = A_("statq2S", [128, 1], F32)
    statv2 = A_("statv2S", [128, 1], F32)
    scr = A_("scrS", [128, 2048], F32)
    bcflat = A_("bcflatS", [128, TC], F32)
    cdtl = A_("cdtlS", [128, NDT], F32)


    pbk = [P_(f"pb{i}", [128, MMF], F32) for i in range(4)]
    ptf = [P_(f"ptf{i}", [128, MMF], F32) for i in range(2)]
    ptb = [P_(f"ptb{i}", [128, 128], BF16) for i in range(2)]
    pt = [ptf[0][:, 0:128], ptf[1][:, 0:128]]
    pst = [ptf[0][0:1, :], ptf[1][0:1, :]]

    (B_ident, B_identb, B_gm, B_gm2, B_convw, B_convb, B_alog, B_Asb,
     B_dparam, B_dtpb, B_ampb, B_phb, B_entpb, B_pathb, B_phase, B_mod,
     B_pbmod, B_onescol, B_onesrow, B_acore, B_bcore, B_dts, B_hinit,
     B_lnh, B_entbb, B_statm, B_statq, B_statv, B_scr, B_rowm, B_rowr,
     B_mrep, B_rrep, B_bcflat, B_statm2, B_statq2, B_statv2) = tk.bufs(37)
    statR = [(statm, statq, statv, B_statm, B_statq, B_statv),
             (statm2, statq2, statv2, B_statm2, B_statq2, B_statv2)]
    B_lns = {nm: tk.buf() for nm in lns}
    B_hlast = [[tk.buf() for _ in range(S)] for _ in range(NDT)]
    B_betac = tk.bufs(2)
    B_pb = tk.bufs(4)
    B_pt = tk.bufs(2)
    B_ptb = tk.bufs(2)
    B_min = tk.bufs(2)
    B_mout = tk.bufs(2)
    B_bin = tk.bufs(2)
    B_bout = tk.bufs(2)
    B_cin, B_cout, B_x1st, B_bcd = tk.bufs(4)

    tk.dma(ident[:], dident[:], writes=[B_ident])
    tk.dma(gm[:], dgm[:], writes=[B_gm])
    tk.dma(gm2[:], dgm2[:], writes=[B_gm2])
    tk.dma(convw[:], dconvw[:], writes=[B_convw])
    tk.dma(convb[:], dconvb[:], writes=[B_convb])
    tk.dma(alog[:], dalog[:], writes=[B_alog])
    tk.dma(dparam[:], ddparam[:], writes=[B_dparam])
    tk.dma(dtpb[:], ddtpb[:], writes=[B_dtpb])
    tk.dma(ampb[:], dampb[:], writes=[B_ampb])
    tk.dma(phb[:], dphb[:], writes=[B_phb])
    tk.dma(entpb[:], dentpb[:], writes=[B_entpb])
    tk.dma(pathb[:], dpathb[:], writes=[B_pathb])
    tk.dma(phasec[:], dphase[:], writes=[B_phase])
    for nm in lns:
        tk.dma(lns[nm][:], dln[nm][:], writes=[B_lns[nm]])
    tk.op("v", lambda e: e.memset(onescol[:], 1.0), writes=[B_onescol])
    B_cpihalf, B_ctiny, B_ceps = tk.bufs(3)
    tk.op("v", lambda e: e.memset(c_pihalf[:], PI / 2), writes=[B_cpihalf])
    tk.op("v", lambda e: e.memset(c_tiny[:], 1e-38), writes=[B_ctiny])
    tk.op("v", lambda e: e.memset(c_eps[:], EPS_LN), writes=[B_ceps])
    tk.op("v", lambda e: e.memset(onesrow[:], 1.0), writes=[B_onesrow])
    tk.op("v", lambda e: e.tensor_copy(identb[:], ident[:]),
          reads=[B_ident], writes=[B_identb])
    tk.op("s", lambda e: e.activation(Asb[:], alog[:], AF.Exp),
          reads=[B_alog], writes=[B_Asb])
    tk.op("v", lambda e: e.tensor_scalar_mul(Asb[:], Asb[:], -1.0),
          reads=[B_Asb], writes=[B_Asb])
    tk.op("v", lambda e: e.tensor_scalar(modc[:], phasec[:], PI, -2 * PI,
                                         AluOpType.is_gt, MULT),
          reads=[B_phase], writes=[B_mod])
    tk.op("v", lambda e: e.tensor_tensor(modc[:], modc[:], phasec[:], ADD),
          reads=[B_mod, B_phase], writes=[B_mod])
    tk.op("v", lambda e: e.tensor_scalar(pbmod[:], phasec[:], -PI, 2 * PI,
                                         AluOpType.is_lt, MULT),
          reads=[B_phase], writes=[B_pbmod])
    tk.op("v", lambda e: e.tensor_tensor(modc[:], modc[:], pbmod[:], ADD),
          reads=[B_mod, B_pbmod], writes=[B_mod])
    tk.op("s", lambda e: e.activation(pbmod[:], modc[:], AF.Abs),
          reads=[B_mod], writes=[B_pbmod])
    tk.op("s", lambda e: e.activation(modc[:], modc[:], AF.Sin),
          reads=[B_mod], writes=[B_mod])
    tk.op("s", lambda e: e.activation(pbmod[:], pbmod[:], AF.Sin,
                                      bias=c_pihalf[:], scale=-1.0),
          reads=[B_pbmod, B_cpihalf], writes=[B_pbmod])
    tk.op("v", lambda e: e.tensor_tensor(modc[:], modc[:], pbmod[:], ADD),
          reads=[B_mod, B_pbmod], writes=[B_mod])
    tk.op("v", lambda e: e.tensor_tensor(pbmod[:], pathb[:], modc[:], MULT),
          reads=[B_pathb, B_mod], writes=[B_pbmod])

    def mm_group(pbuf, ptile_ap, pieces, reads):
        ems = []
        n = len(pieces)
        for i, (lh, rh) in enumerate(pieces):
            ems.append(lambda e, lh=lh, rh=rh, st=(i == 0), sp=(i == n - 1):
                       e.matmul(ptile_ap, lh, rh, start=st, stop=sp))
        tk.ops("t", ems, reads=list(reads), writes=[pbuf])

    def stats_cols(x_ap, nrows, b_x, eps=EPS_LN, par=0):
        sm, sq, sv, B_sm, B_sq, B_sv = statR[par]
        tk.op("v", lambda e: e.reduce_sum(sm[0:nrows, :], x_ap, axis=AX),
              reads=[b_x], writes=[B_sm])
        tk.op("s", lambda e: e.activation(scr[0:nrows, 0:D], x_ap, AF.Square,
                                          accum_out=sq[0:nrows, :]),
              reads=[b_x], writes=[B_scr, B_sq])
        tk.op("v", lambda e: e.tensor_scalar_mul(sm[0:nrows, :],
                                                 sm[0:nrows, :], 1.0 / D),
              reads=[B_sm], writes=[B_sm])
        tk.op("v", lambda e: e.tensor_tensor(sv[0:nrows, :], sm[0:nrows, :],
                                             sm[0:nrows, :], MULT),
              reads=[B_sm], writes=[B_sv])
        tk.op("v", lambda e: e.scalar_tensor_tensor(sq[0:nrows, :],
                                                    sq[0:nrows, :], 1.0 / D,
                                                    sv[0:nrows, :], MULT, SUB),
              reads=[B_sq, B_sv], writes=[B_sq])
        tk.op("s", lambda e: e.activation(sq[0:nrows, :], sq[0:nrows, :],
                                          AF.Ln, bias=c_eps[0:nrows, :]),
              reads=[B_sq, B_ceps], writes=[B_sq])
        tk.op("s", lambda e: e.activation(sq[0:nrows, :], sq[0:nrows, :],
                                          AF.Exp, scale=-0.5),
              reads=[B_sq], writes=[B_sq])

    es_s1 = ExitStack()
    es_s2 = ExitStack()
    es_dt = ExitStack()
    es_y = ExitStack()

    s1 = lambda n, sh, dt=F32: es_s1.enter_context(nc.sbuf_tensor(n, sh, dt))
    s2 = lambda n, sh, dt=F32: es_s2.enter_context(nc.sbuf_tensor(n, sh, dt))

    yc = es_y.enter_context(nc.sbuf_tensor("yP", [128, NDT, TC], F32))
    dtc = es_dt.enter_context(nc.sbuf_tensor("dtP", [128, NDT, TC], F32))
    dtp = [dtc, dtc]
    yp = [yc, yc]
    crep = [s2(f"crepP{i}", [128, TC]) for i in range(2)]
    abuf = [s2(f"abufP{i}", [128, TC]) for i in range(3)]
    ubuf = [s2(f"ubufP{i}", [128, TC]) for i in range(3)]
    lt = s1("ltP", [128, NDT, TC + 3])
    halosave = s1("halosaveP", [128, NDT, 3])
    xpwb = s1("xpwbP", [128, NDT, PD], BF16)
    dtpwb = s1("dtpwbP", [128, D], BF16)
    dtrawb = s1("dtrawbP", [128, TC], BF16)
    onesTC = s1("onesTCP", [128, TC])
    brep = [s1(f"brepP{i}", [128, TC]) for i in range(2)]
    bbuf = [s1(f"bbufP{i}", [128, TC]) for i in range(3)]
    hbuf = [s1(f"hbufP{i}", [128, TC]) for i in range(2)]
    ylbuf = [s1(f"ylbufP{i}", [128, TC]) for i in range(2)]
    xpl = s1("xplP", [128, 2, D])
    xcbfp = s1("xcbfP", [128, NDT, TC], BF16)

    entwTbM = s1("entwTbM", [128, 2, NKC * 128], BF16)
    B_entwTbM = tk.bufs(2)
    munits = [(j, pc) for j in range(4) for pc in range(NDT)]

    def m_dma_pw(l, j):
        js = j % 2
        jb = [B_xcbf[js * NKC + kc] for kc in range(NKC)]
        tk.dma(xcbfp[:, js * NKC:(js + 1) * NKC, :], dentpw[l, j],
               reads=jb, writes=jb, ce="g")

    def m_dma_wt(l, u):
        j, pc = munits[u]
        iw = u % 2
        tk.dma(entwTbM[:, iw, :], dentwT[l, pc].rearrange("p a f -> p (a f)"),
               reads=[B_entwTbM[iw]], writes=[B_entwTbM[iw]], ce="g")

    def m_product_gen(l):
        if l == 1:
            tk.collective("AllReduce", ADD, [dmin[0].ap().opt()],
                          [dmout[0].ap().opt()], reads=[B_min[0]],
                          writes=[B_mout[0]])
            tk.collective("AllReduce", ADD, [dbin[0].ap().opt()],
                          [dbout[0].ap().opt()], reads=[B_bin[0]],
                          writes=[B_bout[0]])
        tk.dma(entbb[:, l, :], dentb[:, l, :], reads=[B_entbb],
               writes=[B_entbb], ce="g")
        m_dma_pw(l, 0)
        m_dma_wt(l, 0)
        m_dma_wt(l, 1)
        for u, (j, pc) in enumerate(munits):
            iw = u % 2
            js = j % 2
            jb = [B_xcbf[js * NKC + kc] for kc in range(NKC)]
            ptile, bpt = pbk[2 + iw], B_pb[2 + iw]
            pieces = [(entwTbM[:, iw, kc * 128:(kc + 1) * 128],
                       xcbfp[:, js * NKC + kc, :]) for kc in range(NKC)]
            mm_group(bpt, ptile[:], pieces, [B_entwTbM[iw]] + jb)
            if u + 2 < len(munits):
                m_dma_wt(l, u + 2)
            if pc == NKC and j + 1 < 4:
                m_dma_pw(l, j + 1)
            tk.op("s", lambda e, ptile=ptile:
                  e.copy(dtrawb[:], ptile[:]),
                  reads=[bpt], writes=[B_dtrawb])
            tk.dma(dmin[l][:, j, pc, :], dtrawb[:],
                   reads=[B_dtrawb], writes=[B_min[l]])
            if pc == NDT - 1:
                bt, bbt = pst[j % 2], B_pt[j % 2]
                bpieces = [(entbb[:, l, kc:kc + 1],
                            xcbfp[:, js * NKC + kc, :]) for kc in range(NKC)]
                mm_group(bbt, bt, bpieces, [B_entbb] + jb)
                tk.op("s", lambda e, bt=bt: e.copy(dtrawb[0:1, :], bt),
                      reads=[bbt], writes=[B_dtrawb])
                tk.dma(dbin[l][0:1, j * MMF:(j + 1) * MMF], dtrawb[0:1, :],
                       reads=[B_dtrawb], writes=[B_bin[l]])
            yield

    mgen = [m_product_gen(0), m_product_gen(1)]

    B_dtc = [tk.buf() for _ in range(NDT)]
    B_yc = [tk.buf() for _ in range(NDT)]
    B_dtp = [B_dtc, B_dtc]
    B_yp = [B_yc, B_yc]
    B_cdtl = tk.buf()
    B_spdt, B_spy = tk.bufs(2)
    B_lt = [tk.buf() for _ in range(NDT)]
    B_lthalo = [tk.buf() for _ in range(NDT)]
    (B_halosave, B_xpwb, B_dtpwb, B_dtrawb, B_bcsb, B_onesTC) = tk.bufs(6)
    B_brep, B_hb, B_yl = (tk.bufs(2) for _ in range(3))
    B_bb = tk.bufs(3)
    B_crep = tk.bufs(2)
    B_ab, B_ub = tk.bufs(3), tk.bufs(3)
    B_xpl = tk.bufs(2)
    B_xcbf = [tk.buf() for _ in range(NDT)]
    cacc = scr[:, 1024:1024 + TC]
    B_cacc = B_scr

    tk.op("v", lambda e: e.memset(onesTC[:], 1.0), writes=[B_onesTC])
    tk.dma(xpwb[:], dxpwT[:], writes=[B_xpwb], ce="g")
    tk.dma(dtpwb[:], ddtpwT[:], writes=[B_dtpwb], ce="g")

    for h in range(NH):
        t0 = h * TC
        if True:
            if h == 0:
                with nc.sbuf_tensor("haloT", [3, D], F32,
                                    side="right") as haloT:
                    B_halo = tk.buf()
                    tk.dma(haloT[:], dxhalo[:], writes=[B_halo])
                    stats_cols(haloT[:], 3, B_halo)
                    tk.op("v", lambda e: e.tensor_scalar(haloT[:], haloT[:],
                                                         statm[0:3, :],
                                                         statq[0:3, :], SUB, MULT),
                          reads=[B_halo, B_statm, B_statq], writes=[B_halo])
                    for dt in range(NDT):
                        ptile, bpt = pt[dt % 2], B_pt[dt % 2]
                        tk.op("t", lambda e, dt=dt, ptile=ptile:
                              e.transpose(ptile[:, 0:3],
                                          haloT[:, dt * 128:(dt + 1) * 128],
                                          ident[0:3, 0:3]),
                              reads=[B_halo, B_ident], writes=[bpt])
                        tk.op("s", lambda e, dt=dt, ptile=ptile:
                              e.copy(lt[:, dt, 0:3], ptile[:, 0:3]),
                              reads=[bpt], writes=[B_lthalo[dt]])
                    tk.retire(B_halo)
            else:
                for dt in range(NDT):
                    tk.op("v", lambda e, dt=dt:
                          e.tensor_copy(lt[:, dt, 0:3], halosave[:, dt, :]),
                          reads=[B_halosave], writes=[B_lthalo[dt]])

            for i in range(NTT):
                ix = i % 2
                tk.dma(xpl[:, ix, :], dx[t0 + i * 128:t0 + (i + 1) * 128, :],
                       reads=[B_xpl[ix]], writes=[B_xpl[ix]])
                stats_cols(xpl[:, ix, :], 128, B_xpl[ix])
                tk.op("v", lambda e, ix=ix:
                      e.tensor_scalar(xpl[:, ix, :], xpl[:, ix, :], statm[:],
                                      statq[:], SUB, MULT),
                      reads=[B_xpl[ix], B_statm, B_statq], writes=[B_xpl[ix]])
                for dt in range(NDT):
                    ptile, bpt = pt[dt % 2], B_pt[dt % 2]
                    tk.op("t", lambda e, ix=ix, dt=dt, ptile=ptile:
                          e.transpose(ptile, xpl[:, ix, dt * 128:(dt + 1) * 128],
                                      ident[:]),
                          reads=[B_xpl[ix], B_ident], writes=[bpt])
                    tk.op("s", lambda e, i=i, dt=dt, ptile=ptile:
                          e.copy(lt[:, dt, 3 + i * 128:3 + (i + 1) * 128],
                                 ptile),
                          reads=[bpt], writes=[B_lt[dt]])


        if h + 1 < NH:
            for dt in range(NDT):
                tk.op("v", lambda e, dt=dt:
                      e.tensor_copy(halosave[:, dt, :], lt[:, dt, TC:TC + 3]),
                      reads=[B_lt[dt]], writes=[B_halosave])

        if True:
            for dt in range(NDT):
                tk.op("v", lambda e, dt=dt:
                      e.tensor_scalar_mul(cacc, lt[:, dt, 0:TC],
                                          convw[:, dt, 0:1]),
                      reads=[B_lt[dt], B_lthalo[dt], B_convw], writes=[B_cacc])
                for j in range(1, 4):
                    tk.op("v", lambda e, dt=dt, j=j:
                          e.scalar_tensor_tensor(cacc, lt[:, dt, j:j + TC],
                                                 convw[:, dt, j:j + 1], cacc,
                                                 MULT, ADD),
                          reads=[B_lt[dt], B_lthalo[dt], B_convw, B_cacc],
                          writes=[B_cacc])
                tk.op("s", lambda e, dt=dt:
                      e.activation(lt[:, dt, 3:3 + TC], cacc, AF.Silu,
                                   bias=convb[:, dt:dt + 1]),
                      reads=[B_cacc, B_convb], writes=[B_lt[dt]])
                tk.op("v", lambda e, dt=dt:
                      e.tensor_copy(xcbfp[:, dt, :], lt[:, dt, 3:3 + TC]),
                      reads=[B_lt[dt]], writes=[B_xcbf[dt]])
                if dbg:
                    tk.dma(dbg_outs["dbg_xc"][:, dt, t0:t0 + TC],
                           lt[:, dt, 3:3 + TC],
                           reads=[B_lt[dt]], writes=[B_dbg["dbg_xc"]])

            ptile, bpt = pbk[0], B_pb[0]
            pieces = [(xpwb[:, c, 0:128], xcbfp[:, c, :]) for c in range(NDT)]
            mm_group(bpt, ptile[:], pieces, [B_xpwb] + B_xcbf)
            tk.op("s", lambda e, ptile=ptile: e.copy(dtrawb[:], ptile[:]),
                  reads=[bpt], writes=[B_dtrawb])
            ptile2, bpt2 = pbk[1], B_pb[1]
            pieces = [(xpwb[:, c, 128:160], xcbfp[:, c, :]) for c in range(NDT)]
            mm_group(bpt2, ptile2[0:32, :], pieces, [B_xpwb] + B_xcbf)
            tk.op("v", lambda e, ptile2=ptile2:
                  e.tensor_copy(scr[0:32, 0:TC], ptile2[0:32, :]),
                  reads=[bpt2], writes=[B_bcsb, B_scr])
            tk.dma(dbcr[:, t0:t0 + TC], scr[0:32, 0:TC],
                   reads=[B_bcsb, B_scr], writes=[B_bcd])

            for dt in range(NDT):
                ptile, bpt = pbk[2 + dt % 2], B_pb[2 + dt % 2]
                mm_group(bpt, ptile[:],
                         [(dtpwb[:, dt * 128:(dt + 1) * 128], dtrawb[:])],
                         [B_dtpwb, B_dtrawb])
                tk.op("s", lambda e, dt=dt, ptile=ptile:
                      e.activation(scr[:, 0:TC], ptile[:], AF.Exp,
                                   bias=dtpb[:, dt:dt + 1]),
                      reads=[bpt, B_dtpb], writes=[B_scr])
                tk.op("s", lambda e, dt=dt:
                      e.activation(dtp[h][:, dt, :], scr[:, 0:TC], AF.Ln,
                                   bias=onescol[:]),
                      reads=[B_scr, B_onescol], writes=[B_dtp[h][dt]])
                if dbg:
                    tk.dma(dbg_outs["dbg_dt"][:, dt, t0:t0 + TC],
                           dtp[h][:, dt, :],
                           reads=[B_dtp[h][dt]], writes=[B_dbg["dbg_dt"]])
                if h == 0:
                    tk.op("v", lambda e, dt=dt:
                          e.reduce_sum(dts[:, dt:dt + 1], dtp[h][:, dt, :],
                                       axis=AX),
                          reads=[B_dtp[h][dt]], writes=[B_dts])
                else:
                    tk.op("v", lambda e, dt=dt:
                          e.reduce_sum(statv[:], dtp[h][:, dt, :], axis=AX),
                          reads=[B_dtp[h][dt]], writes=[B_statv])
                    tk.op("v", lambda e, dt=dt:
                          e.tensor_tensor(dts[:, dt:dt + 1], dts[:, dt:dt + 1],
                                          statv[:], ADD),
                          reads=[B_dts, B_statv], writes=[B_dts])
                tk.op("v", lambda e, dt=dt, h=h:
                      e.tensor_scalar_mul(yp[h][:, dt, :], lt[:, dt, 3:3 + TC],
                                          dparam[:, dt:dt + 1]),
                      reads=[B_lt[dt], B_dparam], writes=[B_yp[h][dt]])
                tk.op("v", lambda e, dt=dt, h=h:
                      e.tensor_tensor(lt[:, dt, 3:3 + TC], lt[:, dt, 3:3 + TC],
                                      dtp[h][:, dt, :], MULT),
                      reads=[B_lt[dt], B_dtp[h][dt]], writes=[B_lt[dt]])


        def emit_reps1(s, par):
            tk.dma(bcflat[0:1, :], dbcr[s:s + 1, t0:t0 + TC],
                   reads=[B_bcd, B_bcflat], writes=[B_bcflat])
            ptile, bpt = pbk[0], B_pb[0]
            tk.op("t", lambda e, ptile=ptile:
                  e.matmul(ptile[:, 0:TC], onesrow[:], bcflat[0:1, :],
                           start=True, stop=True),
                  reads=[B_onesrow, B_bcflat], writes=[bpt])
            tk.op("s", lambda e, par=par, ptile=ptile:
                  e.copy(brep[par][:], ptile[:, 0:TC]),
                  reads=[bpt], writes=[B_brep[par]])
            tk.dma(bcflat[0:1, :], dbcr[16 + s:17 + s, t0:t0 + TC],
                   reads=[B_bcd, B_bcflat, bpt], writes=[B_bcflat])
            ptile2, bpt2 = pbk[1], B_pb[1]
            tk.op("t", lambda e, ptile2=ptile2:
                  e.matmul(ptile2[:, 0:TC], onesrow[:], bcflat[0:1, :],
                           start=True, stop=True),
                  reads=[B_onesrow, B_bcflat], writes=[bpt2])
            tk.op("s", lambda e, par=par, ptile2=ptile2:
                  e.copy(crep[par][:], ptile2[:, 0:TC]),
                  reads=[bpt2], writes=[B_crep[par]])

        def emit_a1(s, dt, ip, h=h):
            tk.op("s", lambda e, dt=dt, s=s, ip=ip, h=h:
                  e.activation(abuf[ip][:], dtp[h][:, dt, :], AF.Exp,
                               scale=Asb[:, dt, s:s + 1]),
                  reads=[B_dtp[h][dt], B_Asb], writes=[B_ab[ip]])

        iters = [(s, dt) for s in range(S) for dt in range(NDT)]

        def emit_u1(k):
            s, dt = iters[k]
            ip = k % 3
            tk.op("v", lambda e, dt=dt, ip=ip, par=s % 2:
                  e.tensor_tensor(ubuf[ip][:], lt[:, dt, 3:3 + TC],
                                  brep[par][:], MULT),
                  reads=[B_lt[dt], B_brep[s % 2]], writes=[B_ub[ip]])

        def emit_b1(k):
            ip = k % 3
            tk.op("s", lambda e, ip=ip:
                  e.activation(bbuf[ip][:], ubuf[ip][:], AF.Relu),
                  reads=[B_ub[ip]], writes=[B_bb[ip]])

        emit_reps1(0, 0)
        emit_a1(*iters[0], 0)
        emit_u1(0)
        emit_b1(0)
        emit_a1(*iters[1], 1)
        emit_u1(1)
        emit_b1(1)
        for k, (s, dt) in enumerate(iters):
            par = s % 2
            ip = k % 3
            if dt == NDT - 8 and s + 1 < S:
                emit_reps1(s + 1, (s + 1) % 2)
            if k + 2 < len(iters):
                emit_a1(*iters[k + 2], (k + 2) % 3)
                emit_u1(k + 2)
                emit_b1(k + 2)
            if h == 0:
                init, init_reads = 0.0, []
            else:
                init, init_reads = hlast[:, dt, s:s + 1], [B_hlast[dt][s]]
            hb = k % 2
            tk.op("v", lambda e, ip=ip, hb=hb, init=init:
                  e.tensor_tensor_scan(hbuf[hb][:], abuf[ip][:], bbuf[ip][:],
                                       init, MULT, ADD),
                  reads=[B_ab[ip], B_bb[ip]] + init_reads, writes=[B_hb[hb]])
            tk.op("s", lambda e, dt=dt, s=s, hb=hb:
                  e.copy(hlast[:, dt, s:s + 1], hbuf[hb][:, TC - 1:TC]),
                  reads=[B_hb[hb]], writes=[B_hlast[dt][s]])
            tk.op("v", lambda e, hb=hb, par=par:
                  e.tensor_tensor(ylbuf[hb][:], hbuf[hb][:], crep[par][:],
                                  MULT),
                  reads=[B_hb[hb], B_crep[par]], writes=[B_yl[hb]])
            tk.op("v", lambda e, dt=dt, hb=hb, h=h:
                  e.tensor_tensor(yp[h][:, dt, :], yp[h][:, dt, :],
                                  ylbuf[hb][:], ADD),
                  reads=[B_yp[h][dt], B_yl[hb]], writes=[B_yp[h][dt]])
            if k % 2 == 1 and mgen[h] is not None:
                try:
                    next(mgen[h])
                except StopIteration:
                    mgen[h] = None

        for dt in range(NDT):
            if h == 0:
                cinit, creads = 0.0, []
            else:
                cinit = cdtl[:, dt:dt + 1]
                creads = [B_cdtl]
            tk.op("v", lambda e, dt=dt, cinit=cinit, h=h:
                  e.tensor_tensor_scan(dtp[h][:, dt, :], onesTC[:],
                                       dtp[h][:, dt, :], cinit, MULT, ADD),
                  reads=[B_onesTC, B_dtp[h][dt]] + creads, writes=[B_dtp[h][dt]])

        if h == 0:
            for dt in range(NDT):
                tk.op("v", lambda e, dt=dt:
                      e.tensor_copy(cdtl[:, dt:dt + 1], dtc[:, dt, TC - 1:TC]),
                      reads=[B_dtc[dt]], writes=[B_cdtl])
            tk.dma(dspdt[:], dtc[:].rearrange("p a t -> p (a t)"),
                   reads=list(B_dtc), writes=[B_spdt])
            tk.dma(dspy[:], yc[:].rearrange("p a t -> p (a t)"),
                   reads=list(B_yc), writes=[B_spy])

    for h_ in range(2):
        while mgen[h_] is not None:
            try:
                next(mgen[h_])
            except StopIteration:
                mgen[h_] = None

    def emit_crep2(t0, s, par):
        tk.dma(bcflat[0:1, :], dbcr[16 + s:17 + s, t0:t0 + TC],
               reads=[B_bcd, B_bcflat], writes=[B_bcflat])
        ptile2, bpt2 = pbk[1], B_pb[1]
        tk.op("t", lambda e, ptile2=ptile2:
              e.matmul(ptile2[:, 0:TC], onesrow[:], bcflat[0:1, :],
                       start=True, stop=True),
              reads=[B_onesrow, B_bcflat], writes=[bpt2])
        tk.op("s", lambda e, par=par, ptile2=ptile2:
              e.copy(crep[par][:], ptile2[:, 0:TC]),
              reads=[bpt2], writes=[B_crep[par]])


    for dt in range(NDT):
        tk.op("s", lambda e, dt=dt:
              e.activation(acore[:, dt * S:(dt + 1) * S], Asb[:, dt, :], AF.Exp,
                           scale=dts[:, dt:dt + 1]),
              reads=[B_Asb, B_dts], writes=[B_acore])
        tk.op("v", lambda e, dt=dt:
              e.tensor_copy(bcore[:, dt * S:(dt + 1) * S], hlast[:, dt, :]),
              reads=[B_hlast[dt][ss] for ss in range(S)], writes=[B_bcore])
    tk.dma(dcin[:, 0:NDT * S], acore[:], reads=[B_acore], writes=[B_cin])
    tk.dma(dcin[:, NDT * S:], bcore[:], reads=[B_bcore], writes=[B_cin])
    tk.collective("AllGather", AluOpType.bypass, [dcin.ap().opt()],
                  [dcout.ap().opt()], reads=[B_cin], writes=[B_cout])
    tk.collective("AllReduce", ADD, [dmin[1].ap().opt()],
                  [dmout[1].ap().opt()], reads=[B_min[1]], writes=[B_mout[1]])
    tk.collective("AllReduce", ADD, [dbin[1].ap().opt()],
                  [dbout[1].ap().opt()], reads=[B_bin[1]], writes=[B_bout[1]])
    emit_crep2(TC, 0, 0)
    with nc.sbuf_tensor("agb", [128, 2, 2 * NDT * S], F32,
                        side="right") as agb:
        cmp0 = scr[:, 0:NDT * S]
        B_agb = tk.bufs(2)
        B_cmp0 = B_scr
        NS = NDT * S
        tk.dma(agb[:, 0, :], dcout[0:128, :], reads=[B_cout], writes=[B_agb[0]])
        tk.op("v", lambda e: e.memset(hinit[:], 0.0), writes=[B_hinit])
        for r in range(NCORES):
            rb = r % 2
            if r + 1 < NCORES:
                tk.dma(agb[:, 1 - rb, :], dcout[128 * (r + 1):128 * (r + 2), :],
                       reads=[B_cout, B_agb[1 - rb]], writes=[B_agb[1 - rb]])
            tk.op("v", lambda e, r=r, rb=rb:
                  e.tensor_scalar(cmp0, agb[:, rb, 0:NS], gm[:, r:r + 1],
                                  gm2[:, r:r + 1], MULT, ADD),
                  reads=[B_agb[rb], B_gm, B_gm2], writes=[B_cmp0])
            tk.op("v", lambda e: e.tensor_tensor(hinit[:], hinit[:], cmp0, MULT),
                  reads=[B_hinit, B_cmp0], writes=[B_hinit])
            tk.op("v", lambda e, r=r, rb=rb:
                  e.tensor_scalar(cmp0, agb[:, rb, NS:], gm[:, r:r + 1],
                                  None, MULT),
                  reads=[B_agb[rb], B_gm], writes=[B_cmp0])
            tk.op("v", lambda e: e.tensor_tensor(hinit[:], hinit[:], cmp0,
                                                 ADD),
                  reads=[B_hinit, B_cmp0], writes=[B_hinit])
        tk.op("s", lambda e: e.activation(lnh[:], hinit[:], AF.Ln,
                                          bias=c_tiny[:]),
              reads=[B_hinit, B_ctiny], writes=[B_lnh])
        tk.retire(*B_agb)

    for bl in (B_lt, B_lthalo, [B_halosave, B_xpwb, B_dtpwb, B_dtrawb, B_bcsb,
                                B_onesTC], B_brep, B_bb, B_hb, B_yl, B_xpl,
               B_xcbf):
        tk.retire(*bl)
    es_s1.close()

    es_p5 = ExitStack()
    es_p5t = ExitStack()
    es_pre = ExitStack()
    xnT = es_p5.enter_context(nc.sbuf_tensor("xnTP", [128, NDT, TSL], BF16, side="right"))
    B_xnT = [tk.buf() for _ in range(NDT)]
    B_yT = tk.bufs(2)
    B_x0 = tk.bufs(2)
    B_xnb = tk.bufs(2)
    B_awb = tk.bufs(2)
    B_pwbx = tk.bufs(2)

    def emit_pass2(h, crep_pre=False):
        t0 = h * TC

        def emit_w2(s, dt, ip):
            tk.op("s", lambda e, dt=dt, s=s, ip=ip:
                  e.activation(abuf[ip][:], dtc[:, dt, :], AF.Exp,
                               scale=Asb[:, dt, s:s + 1],
                               bias=lnh[:, dt * S + s:dt * S + s + 1]),
                  reads=[B_dtc[dt], B_Asb, B_lnh], writes=[B_ab[ip]])

        iters = [(s, dt) for s in range(S) for dt in range(NDT)]
        if not crep_pre:
            emit_crep2(t0, 0, 0)
        emit_w2(0, 0, 0)
        emit_w2(*iters[1], 1)
        for k, (s, dt) in enumerate(iters):
            par = s % 2
            ip = k % 3
            if k + 2 < len(iters):
                s2, dt2 = iters[k + 2]
                emit_w2(s2, dt2, (k + 2) % 3)
                if dt == NDT - 4 and s + 1 < S:
                    emit_crep2(t0, s + 1, (s + 1) % 2)
            tk.op("v", lambda e, ip=ip, par=par:
                  e.tensor_tensor(ubuf[ip][:], abuf[ip][:], crep[par][:],
                                  MULT),
                  reads=[B_ab[ip], B_crep[par]], writes=[B_ub[ip]])
            tk.op("v", lambda e, dt=dt, ip=ip:
                  e.tensor_tensor(yc[:, dt, :], yc[:, dt, :],
                                  ubuf[ip][:], ADD),
                  reads=[B_yc[dt], B_ub[ip]], writes=[B_yc[dt]])
        if dbg:
            for dt in range(NDT):
                tk.dma(dbg_outs["dbg_y"][:, dt, t0:t0 + TC], yc[:, dt, :],
                       reads=[B_yc[dt]], writes=[B_dbg["dbg_y"]])

    def emit_p5(h):
        t0 = h * TC

        def phase1(i):
            ib = i % 2
            tk.dma(x0t[:, ib, :], dx[t0 + i * 128:t0 + (i + 1) * 128, :],
                   reads=[B_x0[ib]], writes=[B_x0[ib]], ce="s")
            for dt in range(NDT):
                ptile, bpt = pt[dt % 2], B_pt[dt % 2]
                tk.op("t", lambda e, i=i, dt=dt, ptile=ptile:
                      e.transpose(ptile, yc[:, dt, i * 128:(i + 1) * 128],
                                  ident[:]),
                      reads=[B_yc[dt], B_ident], writes=[bpt])
                tk.op("s", lambda e, dt=dt, ib=ib, ptile=ptile:
                      e.copy(yTt[:, ib, dt * 128:(dt + 1) * 128], ptile),
                      reads=[bpt], writes=[B_yT[ib]])

        phase1(0)
        phase1(1)
        for i in range(NTT):
            ib = i % 2
            sm, sq, _, B_sm, B_sq, _ = statR[ib]
            stats_cols(yTt[:, ib, :], 128, B_yT[ib], par=ib)
            tk.op("v", lambda e, ib=ib, sm=sm, sq=sq:
                  e.tensor_scalar(yTt[:, ib, :], yTt[:, ib, :], sm[:],
                                  sq[:], SUB, MULT),
                  reads=[B_yT[ib], B_sm, B_sq], writes=[B_yT[ib]])
            tk.op("v", lambda e, ib=ib:
                  e.tensor_tensor(yTt[:, ib, :], yTt[:, ib, :],
                                  x0t[:, ib, :], ADD),
                  reads=[B_yT[ib], B_x0[ib]], writes=[B_yT[ib]])
            tk.dma(dx1st[t0 + i * 128:t0 + (i + 1) * 128, :], yTt[:, ib, :],
                   reads=[B_yT[ib]], writes=[B_x1st], ce="s")
            stats_cols(yTt[:, ib, :], 128, B_yT[ib], par=ib)
            tk.op("v", lambda e, ib=ib, sm=sm, sq=sq:
                  e.tensor_scalar(xnbft[:, ib, :], yTt[:, ib, :], sm[:],
                                  sq[:], SUB, MULT),
                  reads=[B_yT[ib], B_sm, B_sq], writes=[B_xnb[ib]])
            for dt in range(NDT):
                ptile, bpt = ptb[dt % 2], B_ptb[dt % 2]
                tk.op("t", lambda e, dt=dt, ib=ib, ptile=ptile:
                      e.transpose(ptile[:],
                                  xnbft[:, ib, dt * 128:(dt + 1) * 128],
                                  identb[:]),
                      reads=[B_xnb[ib], B_identb], writes=[bpt])
                tk.op("s", lambda e, i=i, dt=dt, t0=t0, ptile=ptile:
                      e.copy(xnT[:, dt, t0 + i * 128:t0 + (i + 1) * 128],
                             ptile[:]),
                      reads=[bpt], writes=[B_xnT[dt]])
            if i + 2 < NTT:
                phase1(i + 2)

    emit_pass2(1, crep_pre=True)
    awb = es_pre.enter_context(nc.sbuf_tensor("awbP", [128, 2, NDT, 128],
                                              BF16, side="right"))
    pwbx = es_pre.enter_context(nc.sbuf_tensor("pwbxP", [128, 2, NDT, 128],
                                               BF16, side="right"))
    yTt = es_p5t.enter_context(nc.sbuf_tensor("yTtP", [128, 2, D], F32,
                                              side="right"))
    x0t = es_p5t.enter_context(nc.sbuf_tensor("x0tP", [128, 2, D], F32,
                                              side="right"))
    xnbft = es_p5t.enter_context(nc.sbuf_tensor("xnbftP", [128, 2, D], BF16,
                                                side="right"))
    for pg in range(2):
        tk.dma(awb[:, pg, :, :], dampw[pg],
               reads=[B_awb[pg]], writes=[B_awb[pg]], ce="g")
        tk.dma(pwbx[:, pg, :, :], dphw[pg],
               reads=[B_pwbx[pg]], writes=[B_pwbx[pg]], ce="g")
    emit_p5(1)

    emit_crep2(0, 0, 0)
    for dt in range(NDT):
        tk.dma(dtc[:, dt, :], dspdt[:, dt * TC:(dt + 1) * TC],
               reads=[B_spdt, B_dtc[dt]], writes=[B_dtc[dt]])
        tk.dma(yc[:, dt, :], dspy[:, dt * TC:(dt + 1) * TC],
               reads=[B_spy, B_yc[dt]], writes=[B_yc[dt]])

    emit_pass2(0, crep_pre=True)
    emit_p5(0)

    tk.retire(*B_yT, *B_x0, *B_xnb)
    es_p5t.close()
    tk.retire(*B_dtc)
    for bl in (B_crep, B_ab, B_ub):
        tk.retire(*bl)
    es_s1_dummy = None
    es_s2.close()
    es_dt.close()
    tk.retire(*B_yc)
    es_y.close()

    es_ffn = ExitStack()
    h2 = es_ffn.enter_context(nc.sbuf_tensor("h2F", [128, NDT, TSL], F32))
    h2bf = es_ffn.enter_context(nc.sbuf_tensor("h2bfF", [128, NDT, TSL], BF16))
    B_h2 = [tk.bufs(2) for _ in range(NDT)]
    B_h2bf = [tk.bufs(2) for _ in range(NDT)]
    B_h2a = [b for bb in B_h2 for b in bb]
    B_h2bfa = [b for bb in B_h2bf for b in bb]

    with nc.sbuf_tensor("ampch", [128, 2, NH, MMF], F32) as ampch, \
         nc.sbuf_tensor("thch", [128, 2, NH, MMF], F32) as thch, \
         nc.sbuf_tensor("sctP", [128, MMF], F32, side="right") as sct:
        B_ach = [[tk.buf() for _ in range(NH)] for _ in range(2)]
        B_tch = [[tk.buf() for _ in range(NH)] for _ in range(2)]
        B_sct = tk.buf()
        for pq in range(8):
            for pp in range(2):
                pg = pq * 2 + pp
                wb = pg % 2
                for tc in range(NH):
                    fs = slice(tc * MMF, (tc + 1) * MMF)
                    ptile, bpt = pbk[tc % 2], B_pb[tc % 2]
                    pieces = [(awb[:, wb, c, :], xnT[:, c, fs])
                              for c in range(NDT)]
                    mm_group(bpt, ptile[:], pieces, [B_awb[wb]] + B_xnT)
                    tk.op("s", lambda e, pp=pp, tc=tc, pg=pg, ptile=ptile:
                          e.activation(ampch[:, pp, tc, :], ptile[:], AF.Sigmoid,
                                       bias=ampb[:, pg:pg + 1]),
                          reads=[bpt, B_ampb], writes=[B_ach[pp][tc]])
                    ptile2, bpt2 = pbk[2 + tc % 2], B_pb[2 + tc % 2]
                    pieces = [(pwbx[:, wb, c, :], xnT[:, c, fs])
                              for c in range(NDT)]
                    mm_group(bpt2, ptile2[:], pieces, [B_pwbx[wb]] + B_xnT)
                    tk.op("s", lambda e, pp=pp, tc=tc, pg=pg, ptile2=ptile2:
                          e.activation(thch[:, pp, tc, :], ptile2[:], AF.Tanh,
                                       bias=phb[:, pg:pg + 1]),
                          reads=[bpt2, B_phb], writes=[B_tch[pp][tc]])
                if pg + 2 < 16:
                    tk.dma(awb[:, wb, :, :], dampw[pg + 2],
                           reads=[B_awb[wb]], writes=[B_awb[wb]], ce="g")
                    tk.dma(pwbx[:, wb, :, :], dphw[pg + 2],
                           reads=[B_pwbx[wb]], writes=[B_pwbx[wb]], ce="g")
            for pp in range(2):
                pg = pq * 2 + pp
                for tc in range(NH):
                    fs = slice(tc * MMF, (tc + 1) * MMF)
                    tk.op("s", lambda e, pp=pp, tc=tc:
                          e.activation(sct[:], thch[:, pp, tc, :], AF.Abs),
                          reads=[B_tch[pp][tc]], writes=[B_sct])
                    tk.op("s", lambda e, pp=pp, tc=tc:
                          e.activation(thch[:, pp, tc, :], thch[:, pp, tc, :],
                                       AF.Sin, scale=PI),
                          reads=[B_tch[pp][tc]], writes=[B_tch[pp][tc]])
                    tk.op("s", lambda e: e.activation(sct[:], sct[:], AF.Sin,
                                                      bias=c_pihalf[:], scale=-PI),
                          reads=[B_sct, B_cpihalf], writes=[B_sct])
                    tk.op("v", lambda e, pp=pp, tc=tc:
                          e.tensor_tensor(thch[:, pp, tc, :],
                                          thch[:, pp, tc, :], sct[:], ADD),
                          reads=[B_tch[pp][tc], B_sct], writes=[B_tch[pp][tc]])
                    tk.op("v", lambda e, pp=pp, tc=tc, pg=pg, fs=fs:
                          e.tensor_tensor(h2[:, pg, fs], thch[:, pp, tc, :],
                                          ampch[:, pp, tc, :], MULT),
                          reads=[B_tch[pp][tc], B_ach[pp][tc]],
                          writes=[B_h2[pg][tc]])
                    tk.op("v", lambda e, pg=pg, fs=fs, tc=tc:
                          e.tensor_copy(h2bf[:, pg, fs], h2[:, pg, fs]),
                          reads=[B_h2[pg][tc]], writes=[B_h2bf[pg][tc]])
        for pp in range(2):
            tk.retire(*B_ach[pp], *B_tch[pp])
        tk.retire(*B_awb, *B_pwbx, B_sct)
    if dbg:
        for dt in range(NDT):
            tk.dma(dbg_outs["dbg_h2p"][:, dt, :], h2[:, dt, :],
                   reads=B_h2[dt], writes=[B_dbg["dbg_h2p"]])
    es_pre.close()
    tk.retire(*B_xnT)
    es_p5.close()

    def ffn_ln_tc(l, gname, bname, tc):
        if True:
            fs = slice(tc * MMF, (tc + 1) * MMF)
            em1 = []
            for jc in range(NDT):
                em1.append(lambda e, jc=jc, fs=fs, st=(jc == 0),
                           sp=(jc == NDT - 1):
                           e.matmul(pst[0], onescol[:], h2[:, jc, fs],
                                    start=st, stop=sp))
            tk.ops("t", em1, reads=[B_onescol] + [B_h2[jc][tc]
                                                  for jc in range(NDT)],
                   writes=[B_pt[0]])
            sq_bufs = tk.bufs(4)
            for jc in range(min(3, NDT)):
                sc = scr[:, (jc % 4) * MMF:(jc % 4) * MMF + MMF]
                tk.op("s", lambda e, jc=jc, fs=fs, sc=sc:
                      e.activation(sc, h2[:, jc, fs], AF.Square),
                      reads=[B_h2[jc][tc]], writes=[sq_bufs[jc % 4]])
            for jc in range(NDT):
                sc = scr[:, (jc % 4) * MMF:(jc % 4) * MMF + MMF]
                if jc + 3 < NDT:
                    jn = jc + 3
                    sn = scr[:, (jn % 4) * MMF:(jn % 4) * MMF + MMF]
                    tk.op("s", lambda e, jn=jn, fs=fs, sn=sn:
                          e.activation(sn, h2[:, jn, fs], AF.Square),
                          reads=[B_h2[jn][tc]], writes=[sq_bufs[jn % 4]])
                tk.op("t", lambda e, jc=jc, sc=sc, st=(jc == 0),
                      sp=(jc == NDT - 1):
                      e.matmul(pst[1], onescol[:], sc, start=st, stop=sp),
                      reads=[B_onescol, sq_bufs[jc % 4]], writes=[B_pt[1]])
            tk.op("v", lambda e: e.tensor_scalar_mul(rowm[:], pst[0], 1.0 / D),
                  reads=[B_pt[0]], writes=[B_rowm])
            tk.op("v", lambda e: e.tensor_scalar_mul(rowr[:], pst[1], 1.0 / D),
                  reads=[B_pt[1]], writes=[B_rowr])
            tk.op("v", lambda e: e.tensor_tensor(scr[0:1, 0:MMF], rowm[:],
                                                 rowm[:], MULT),
                  reads=[B_rowm], writes=[B_scr])
            tk.op("v", lambda e: e.tensor_tensor(rowr[:], rowr[:],
                                                 scr[0:1, 0:MMF], SUB),
                  reads=[B_rowr, B_scr], writes=[B_rowr])
            tk.op("s", lambda e: e.activation(rowr[:], rowr[:], AF.Ln,
                                              bias=c_eps[0:1, :]),
                  reads=[B_rowr, B_ceps], writes=[B_rowr])
            tk.op("s", lambda e: e.activation(rowr[:], rowr[:], AF.Exp,
                                              scale=-0.5),
                  reads=[B_rowr], writes=[B_rowr])
            tk.op("t", lambda e: e.matmul(pbk[0][:], onesrow[:], rowm[:],
                                          start=True, stop=True),
                  reads=[B_onesrow, B_rowm], writes=[B_pb[0]])
            tk.op("s", lambda e: e.copy(mrep[:], pbk[0][:]),
                  reads=[B_pb[0]], writes=[B_mrep])
            tk.op("t", lambda e: e.matmul(pbk[1][:], onesrow[:], rowr[:],
                                          start=True, stop=True),
                  reads=[B_onesrow, B_rowr], writes=[B_pb[1]])
            tk.op("s", lambda e: e.copy(rrep[:], pbk[1][:]),
                  reads=[B_pb[1]], writes=[B_rrep])
            nb = tk.bufs(4)
            for jc in range(NDT):
                sc = scr[:, (jc % 4) * MMF:(jc % 4) * MMF + MMF]
                bsc = nb[jc % 4]
                tk.op("v", lambda e, jc=jc, fs=fs, sc=sc:
                      e.tensor_tensor(sc, h2[:, jc, fs], mrep[:], SUB),
                      reads=[B_h2[jc][tc], B_mrep], writes=[bsc])
                tk.op("v", lambda e, sc=sc: e.tensor_tensor(sc, sc, rrep[:],
                                                            MULT),
                      reads=[bsc, B_rrep], writes=[bsc])
                tk.op("v", lambda e, jc=jc, fs=fs, sc=sc, l=l:
                      e.tensor_scalar(h2[:, jc, fs], sc,
                                      lns[gname][:, l, jc:jc + 1],
                                      lns[bname][:, l, jc:jc + 1], MULT, ADD),
                      reads=[bsc, B_lns[gname], B_lns[bname]],
                      writes=[B_h2[jc][tc]])
                tk.op("v", lambda e, jc=jc, fs=fs:
                      e.tensor_copy(h2bf[:, jc, fs], h2[:, jc, fs]),
                      reads=[B_h2[jc][tc]], writes=[B_h2bf[jc][tc]])
            tk.retire(*nb, *sq_bufs)

    with nc.sbuf_tensor("rowmF", [1, MMF], F32) as rowm, \
         nc.sbuf_tensor("rowrF", [1, MMF], F32) as rowr, \
         nc.sbuf_tensor("mrepF", [128, MMF], F32) as mrep, \
         nc.sbuf_tensor("rrepF", [128, MMF], F32) as rrep, \
         nc.sbuf_tensor("mblkF", [128, 2, NDT, MMF], BF16) as mblk, \
         nc.sbuf_tensor("enttF", [128, 2, MMF], F32) as entt, \
         nc.sbuf_tensor("browF", [1, D], BF16) as brow, \
         nc.sbuf_tensor("pwb2F", [128, PSLOT, NDT, 128], BF16) as pwb2, \
         nc.sbuf_tensor("tptF", [128, 2, MMF], F32) as tpt:
        B_mblk = tk.bufs(2)
        B_entt = tk.bufs(2)
        B_brow = tk.buf()
        B_pwb2 = tk.bufs(PSLOT)
        B_tpt = tk.bufs(2)
        for l in range(2):
            tk.dma(brow[:], dbout[l][0:1, :], reads=[B_bout[l], B_brow],
                   writes=[B_brow])
            for jc in range(NDT):
                ptile, bpt = ptb[jc % 2], B_ptb[jc % 2]
                tk.op("t", lambda e, jc=jc, ptile=ptile:
                      e.transpose(ptile[:, 0:1],
                                  brow[:, jc * 128:(jc + 1) * 128],
                                  identb[0:1, 0:1]),
                      reads=[B_brow, B_identb], writes=[bpt])
                tk.op("s", lambda e, jc=jc, l=l, ptile=ptile:
                      e.copy(betac[l][:, jc:jc + 1], ptile[:, 0:1]),
                      reads=[bpt], writes=[B_betac[l]])
            tk.op("v", lambda e, l=l:
                  e.tensor_tensor(betac[l][:], betac[l][:], entpb[:, l, :], ADD),
                  reads=[B_betac[l], B_entpb], writes=[B_betac[l]])

            for jb in range(4):
                ib = jb % 2
                tk.dma(mblk[:, ib, :, :], dmout[l][:, jb],
                       reads=[B_mout[l], B_mblk[ib]], writes=[B_mblk[ib]])
                tc = 0
                fs = slice(tc * MMF, (tc + 1) * MMF)
                for pj in range(4):
                    pjg = jb * 4 + pj
                    ptile, bpt = pbk[pj % 4], B_pb[pj % 4]
                    pieces = [(mblk[:, ib, c, pj * 128:(pj + 1) * 128],
                               h2bf[:, c, fs]) for c in range(NDT)]
                    mm_group(bpt, ptile[:], pieces,
                             [B_mblk[ib]] + [B_h2bf[c][tc]
                                             for c in range(NDT)])
                    ie = pj % 2
                    tk.op("s", lambda e, l=l, pjg=pjg, ie=ie, ptile=ptile:
                          e.activation(entt[:, ie, :], ptile[:], AF.Identity,
                                       bias=betac[l][:, pjg:pjg + 1]),
                          reads=[bpt, B_betac[l]], writes=[B_entt[ie]])
                    tk.op("v", lambda e, pjg=pjg, fs=fs, ie=ie:
                          e.tensor_tensor(h2[:, pjg, fs], h2[:, pjg, fs],
                                          entt[:, ie, :], ADD),
                          reads=[B_h2[pjg][tc], B_entt[ie]],
                          writes=[B_h2[pjg][tc]])
            if dbg and l == 1:
                for dt in range(NDT):
                    tk.dma(dbg_outs["dbg_e1"][:, dt, :], h2[:, dt, :],
                           reads=B_h2[dt], writes=[B_dbg["dbg_e1"]])
            ffn_ln_tc(l, "ln3g", "ln3b", 0)
            for jb in range(4):
                ib = jb % 2
                tk.dma(mblk[:, ib, :, :], dmout[l][:, jb],
                       reads=[B_mout[l], B_mblk[ib]], writes=[B_mblk[ib]])
                tc = 1
                fs = slice(tc * MMF, (tc + 1) * MMF)
                for pj in range(4):
                    pjg = jb * 4 + pj
                    ptile, bpt = pbk[pj % 4], B_pb[pj % 4]
                    pieces = [(mblk[:, ib, c, pj * 128:(pj + 1) * 128],
                               h2bf[:, c, fs]) for c in range(NDT)]
                    mm_group(bpt, ptile[:], pieces,
                             [B_mblk[ib]] + [B_h2bf[c][tc]
                                             for c in range(NDT)])
                    ie = pj % 2
                    tk.op("s", lambda e, l=l, pjg=pjg, ie=ie, ptile=ptile:
                          e.activation(entt[:, ie, :], ptile[:], AF.Identity,
                                       bias=betac[l][:, pjg:pjg + 1]),
                          reads=[bpt, B_betac[l]], writes=[B_entt[ie]])
                    tk.op("v", lambda e, pjg=pjg, fs=fs, ie=ie:
                          e.tensor_tensor(h2[:, pjg, fs], h2[:, pjg, fs],
                                          entt[:, ie, :], ADD),
                          reads=[B_h2[pjg][tc], B_entt[ie]],
                          writes=[B_h2[pjg][tc]])
            ffn_ln_tc(l, "ln3g", "ln3b", 1)
            if dbg and l == 1:
                for dt in range(NDT):
                    tk.dma(dbg_outs["dbg_l31"][:, dt, :], h2[:, dt, :],
                           reads=B_h2[dt], writes=[B_dbg["dbg_l31"]])
            if dbg and l == 0:
                for dt in range(NDT):
                    tk.dma(dbg_outs["dbg_l30"][:, dt, :], h2[:, dt, :],
                           reads=B_h2[dt], writes=[B_dbg["dbg_l30"]])

            steps = [(p, pjg) for p in range(2) for pjg in range(NDT)]
            gc = [0]

            def path_group(p, pjg, islot, tc, l=l):
                fs = slice(tc * MMF, (tc + 1) * MMF)
                bk, ie = gc[0] % 4, gc[0] % 2
                gc[0] += 1
                ptile, bpt = pbk[bk], B_pb[bk]
                pieces = [(pwb2[:, islot, c, :], h2bf[:, c, fs])
                          for c in range(NDT)]
                mm_group(bpt, ptile[:], pieces,
                         [B_pwb2[islot]] + [B_h2bf[c][tc]
                                            for c in range(NDT)])
                tk.op("s", lambda e, l=l, p=p, pjg=pjg, ie=ie, ptile=ptile:
                      e.activation(tpt[:, ie, :], ptile[:], AF.Identity,
                                   scale=modc[:, l, p, pjg:pjg + 1],
                                   bias=pbmod[:, l, p, pjg:pjg + 1]),
                      reads=[bpt, B_mod, B_pbmod], writes=[B_tpt[ie]])
                tk.op("v", lambda e, pjg=pjg, fs=fs, ie=ie:
                      e.tensor_tensor(h2[:, pjg, fs], h2[:, pjg, fs],
                                      tpt[:, ie, :], ADD),
                      reads=[B_h2[pjg][tc], B_tpt[ie]],
                      writes=[B_h2[pjg][tc]])

            for idx, (p, pjg) in enumerate(steps):
                if idx >= PDEF:
                    pd, pjd = steps[idx - PDEF]
                    path_group(pd, pjd, (idx - PDEF) % PSLOT, 1)
                islot = idx % PSLOT
                tk.dma(pwb2[:, islot, :, :], dpathw[l, p, pjg],
                       reads=[B_pwb2[islot]], writes=[B_pwb2[islot]], ce="g")
                path_group(p, pjg, islot, 0)
            if dbg and l == 1:
                for dt in range(NDT):
                    tk.dma(dbg_outs["dbg_p1"][:, dt, 0:MMF],
                           h2[:, dt, 0:MMF],
                           reads=[B_h2[dt][0]], writes=[B_dbg["dbg_p1"]])
            ffn_ln_tc(l, "ln4g", "ln4b", 0)
            for idx in range(len(steps) - PDEF, len(steps)):
                pd, pjd = steps[idx]
                path_group(pd, pjd, idx % PSLOT, 1)
            if dbg and l == 1:
                for dt in range(NDT):
                    tk.dma(dbg_outs["dbg_p1"][:, dt, MMF:TSL],
                           h2[:, dt, MMF:TSL],
                           reads=[B_h2[dt][1]], writes=[B_dbg["dbg_p1"]])
            ffn_ln_tc(l, "ln4g", "ln4b", 1)
            if dbg and l == 0:
                for dt in range(NDT):
                    tk.dma(dbg_outs["dbg_l40"][:, dt, :], h2[:, dt, :],
                           reads=[B_h2[dt]], writes=[B_dbg["dbg_l40"]])
            if dbg and l == 1:
                for dt in range(NDT):
                    tk.dma(dbg_outs["dbg_h2"][:, dt, :], h2[:, dt, :],
                           reads=B_h2[dt], writes=[B_dbg["dbg_h2"]])
        tk.retire(*B_mblk, *B_entt, B_brow, *B_pwb2, *B_tpt,
                  B_rowm, B_rowr, B_mrep, B_rrep)

    with nc.sbuf_tensor("h2TtF", [128, 2, D], F32) as h2Tt, \
         nc.sbuf_tensor("x1tF", [128, 2, D], F32) as x1t:
        B_h2T = tk.bufs(2)
        B_x1t = tk.bufs(2)
        def fphase1(it):
            ib = it % 2
            tk.dma(x1t[:, ib, :], dx1st[it * 128:(it + 1) * 128, :],
                   reads=[B_x1st, B_x1t[ib]], writes=[B_x1t[ib]], ce="s")
            for dt in range(NDT):
                ptile, bpt = pbk[dt % 2][:, 0:128], B_pb[dt % 2]
                tk.op("t", lambda e, it=it, dt=dt, ptile=ptile:
                      e.transpose(ptile, h2[:, dt, it * 128:(it + 1) * 128],
                                  ident[:]),
                      reads=[B_h2[dt][it // 4], B_ident], writes=[bpt])
                tk.op("s", lambda e, dt=dt, ib=ib, ptile=ptile:
                      e.copy(h2Tt[:, ib, dt * 128:(dt + 1) * 128], ptile),
                      reads=[bpt], writes=[B_h2T[ib]])

        fphase1(0)
        fphase1(1)
        for it in range(TSL // 128):
            ib = it % 2
            sm, sq, _, B_sm, B_sq, _ = statR[ib]
            stats_cols(h2Tt[:, ib, :], 128, B_h2T[ib], par=ib)
            tk.op("v", lambda e, ib=ib, sm=sm, sq=sq:
                  e.tensor_scalar(h2Tt[:, ib, :], h2Tt[:, ib, :], sm[:],
                                  sq[:], SUB, MULT),
                  reads=[B_h2T[ib], B_sm, B_sq], writes=[B_h2T[ib]])
            tk.op("v", lambda e, ib=ib:
                  e.tensor_tensor(x1t[:, ib, :], x1t[:, ib, :], h2Tt[:, ib, :],
                                  ADD),
                  reads=[B_x1t[ib], B_h2T[ib]], writes=[B_x1t[ib]])
            tk.dma(dout[it * 128:(it + 1) * 128, :], x1t[:, ib, :],
                   reads=[B_x1t[ib]])
            if it + 2 < TSL // 128:
                fphase1(it + 2)

    es_ffn.close()
    tk.finalize()
    return nc


_PROG_CACHE = {}


def _get_program(dbg=False):
    if dbg not in _PROG_CACHE:
        _PROG_CACHE[dbg] = build_program(dbg=dbg)
    return _PROG_CACHE[dbg]


def make_in_maps(inputs):
    f = np.float32
    asf = lambda a: np.ascontiguousarray(np.asarray(a, dtype=f))
    x = np.asarray(inputs["L0_attn_x"], dtype=f)[0]
    colmaj = lambda a: asf(np.asarray(a, f).reshape(NDT, 128).T)

    shared = {
        "ident": np.eye(128, dtype=f),
        "convw": asf(np.asarray(inputs["L0_attn_conv_w"], f)[:, 0, :]
                     .reshape(NDT, 128, 4).transpose(1, 0, 2)),
        "convb": colmaj(inputs["L0_attn_conv_b"]),
        "alog": asf(np.asarray(inputs["L0_attn_A_log"], f)
                    .reshape(NDT, 128, S).transpose(1, 0, 2)),
        "dparam": colmaj(inputs["L0_attn_D_param"]),
        "dtpb": colmaj(inputs["L0_attn_dt_proj_b"]),
        "xpwT": asf(np.asarray(inputs["L0_attn_x_proj_w"], f).T
                    .reshape(NDT, 128, PD).transpose(1, 0, 2)),
        "dtpwT": asf(np.asarray(inputs["L0_attn_dt_proj_w"], f).T),
        "ampw_blk": asf(np.asarray(inputs["L0_ffn_amp_w"], f)
                        .reshape(NDT, 128, NDT, 128).transpose(2, 1, 0, 3)),
        "phw_blk": asf(np.asarray(inputs["L0_ffn_ph_w"], f)
                       .reshape(NDT, 128, NDT, 128).transpose(2, 1, 0, 3)),
        "ampb": colmaj(inputs["L0_ffn_amp_b"]),
        "phb": colmaj(inputs["L0_ffn_ph_b"]),
        "entpb": asf(np.asarray(inputs["L0_ffn_ent_pb"], f)
                     .reshape(2, NDT, 128).transpose(2, 0, 1)),
        "pathw_blk": asf(np.asarray(inputs["L0_ffn_path_w"], f)
                         .reshape(2, 2, NDT, 128, NDT, 128)
                         .transpose(0, 1, 4, 3, 2, 5)),
        "pathb": asf(np.asarray(inputs["L0_ffn_path_b"], f)
                     .reshape(2, 2, NDT, 128).transpose(3, 0, 1, 2)),
        "phasec": asf(np.asarray(inputs["L0_ffn_phase"], f)
                      .reshape(2, 2, NDT, 128).transpose(3, 0, 1, 2)),
    }
    for nm, key in (("ln3g", "L0_ffn_ln3_g"), ("ln3b", "L0_ffn_ln3_b"),
                    ("ln4g", "L0_ffn_ln4_g"), ("ln4b", "L0_ffn_ln4_b")):
        shared[nm] = asf(np.asarray(inputs[key], f)
                         .reshape(2, NDT, 128).transpose(2, 0, 1))

    entw = np.asarray(inputs["L0_ffn_ent_w"], f)
    entpw = np.asarray(inputs["L0_ffn_ent_pw"], f)
    entb = np.asarray(inputs["L0_ffn_ent_b"], f)

    in_maps = []
    for c in range(NCORES):
        ks = slice(c * KSL, (c + 1) * KSL)
        g = np.zeros((128, NCORES), f)
        g[:, :c] = 1.0
        m = dict(shared)
        m["x"] = asf(x[c * TSL:(c + 1) * TSL])
        m["xhalo"] = (np.zeros((3, D), f) if c == 0
                      else asf(x[c * TSL - 3:c * TSL]))
        m["gm"] = g
        m["gm2"] = asf(1.0 - g)
        m["entwT_blk"] = asf(entw[:, :, ks].reshape(2, NDT, 128, NKC, 128)
                             .transpose(0, 1, 4, 3, 2))
        m["entpw_sl"] = asf(entpw[:, ks, :].reshape(2, NKC, 128, 4, MMF)
                            .transpose(0, 3, 2, 1, 4))
        m["entb_sl"] = asf(entb[:, ks].reshape(2, NKC, 128).transpose(2, 0, 1))
        in_maps.append(m)
    return in_maps


def kernel(**inputs):
    nc = _get_program(dbg=False)
    in_maps = make_in_maps(inputs)
    res = run_bass_kernel_spmd(nc, in_maps, core_ids=list(range(NCORES)))
    out = np.concatenate([np.asarray(res.results[c]["out"])
                          for c in range(NCORES)], axis=0)
    return out[None].astype(np.float32)


# version 28
# speedup vs baseline: 1.1625x; 1.0131x over previous
import math
from contextlib import ExitStack

import numpy as np

import concourse.bass as bass
import concourse.mybir as mybir
from concourse.alu_op_type import AluOpType
from concourse.bass_utils import run_bass_kernel_spmd

F32 = mybir.dt.float32
BF16 = mybir.dt.bfloat16
AF = mybir.ActivationFunctionType
AX = mybir.AxisListType.X
MULT = AluOpType.mult
ADD = AluOpType.add
SUB = AluOpType.subtract

D = 2048
S = 16
NDT = 16
RK = 128
PD = 160
HID = 8192
PI = math.pi
EPS_LN = 1e-5
NCORES = 8
TSL = 8192 // NCORES
TC = 512
NH = TSL // TC
NTT = TC // 128
PSLOT = 8
PDEF = 6
KSL = HID // NCORES
NKC = KSL // 128
MMF = 512
RG = [list(range(NCORES))]


class Buf:
    __slots__ = ("writer", "readers", "floor")

    def __init__(self, floor=None):
        self.writer = None
        self.readers = {}
        self.floor = dict(floor) if floor else None


class TK:
    ENGS = ("v", "s", "g", "t", "q")

    def __init__(self, nc):
        self.nc = nc
        self.tick = {e: nc.alloc_semaphore(f"tk_{e}") for e in self.ENGS}
        self.cnt = dict.fromkeys(self.ENGS, 0)
        self.NSD = 12
        self.dsems = [nc.alloc_semaphore(f"dma{i}") for i in range(self.NSD)]
        self.duse = [0] * self.NSD
        self.dix = 0
        self.dcnt = 0
        self.ccs = []
        self.waited = {e: {} for e in self.ENGS}
        self.prog = {e: [] for e in self.ENGS}
        self.floor = {}

    def _emit_wait(self, ce, pkey, val):
        w = self.waited[ce]
        if w.get(pkey, 0) >= val:
            return
        w[pkey] = val
        if pkey in self.tick:
            sem = self.tick[pkey]
        elif isinstance(pkey, tuple) and pkey[0] == "D":
            sem = self.dsems[pkey[1]]
        else:
            sem = self.ccs[pkey[1]]
        self.prog[ce].append(lambda eng, s=sem, v=val: eng.wait_ge(s, v))

    def _deps(self, ce, reads, writes):
        edges = {}

        def add(pk, v):
            if edges.get(pk, 0) < v:
                edges[pk] = v

        for b in reads:
            if b.writer:
                add(*b.writer)
            if b.floor:
                for pk, v in b.floor.items():
                    add(pk, v)
        for b in writes:
            if b.writer:
                add(*b.writer)
            for pk, v in b.readers.items():
                add(pk, v)
            if b.floor:
                for pk, v in b.floor.items():
                    add(pk, v)
        for pk in sorted(edges, key=str):
            self._emit_wait(ce, pk, edges[pk])

    def _note(self, key, val, reads, writes):
        for b in writes:
            b.writer = (key, val)
            b.readers = {}
        for b in reads:
            if b.readers.get(key, 0) < val:
                b.readers[key] = val

    def op(self, ce, emit, reads=(), writes=()):
        self._deps(ce, reads, writes)
        tick = self.tick[ce]
        self.cnt[ce] += 1

        def run(eng, emit=emit, tick=tick):
            emit(eng).then_inc(tick, 1)

        self.prog[ce].append(run)
        self._note(ce, self.cnt[ce], reads, writes)

    def ops(self, ce, emits, reads=(), writes=()):
        self._deps(ce, reads, writes)
        tick = self.tick[ce]
        self.cnt[ce] += 1

        def run(eng, emits=tuple(emits), tick=tick):
            inst = None
            for em in emits:
                inst = em(eng)
            inst.then_inc(tick, 1)

        self.prog[ce].append(run)
        self._note(ce, self.cnt[ce], reads, writes)

    def dma(self, out_ap, in_ap, reads=(), writes=(), ce="q"):
        i_s = self.dix
        self.dix = (self.dix + 1) % self.NSD
        if self.duse[i_s] > 0:
            self._emit_wait(ce, ("D", i_s), 16 * self.duse[i_s])
        self._deps(ce, reads, writes)
        self.duse[i_s] += 1
        self.dcnt += 16
        dsem, val = self.dsems[i_s], 16 * self.duse[i_s]

        def run(eng, o=out_ap, i=in_ap):
            eng.dma_start(out=o, in_=i).then_inc(dsem, 16)

        self.prog[ce].append(run)
        self._note(("D", i_s), val, reads, writes)

    def collective(self, kind, op, ins_ap, outs_ap, reads=(), writes=()):
        ce = "g"
        self._deps(ce, reads, writes)
        sem = self.nc.alloc_semaphore(f"cc_{len(self.ccs)}")
        idx = len(self.ccs)
        self.ccs.append(sem)

        def run(eng, k=kind, o=op, i=ins_ap, u=outs_ap, s=sem):
            eng.collective_compute(k, o, ins=i, outs=u, replica_groups=RG).then_inc(s)

        self.prog[ce].append(run)
        self._note(("c", idx), 1, reads, writes)

    def retire(self, *bufs):
        for b in bufs:
            if b.writer:
                pk, v = b.writer
                if self.floor.get(pk, 0) < v:
                    self.floor[pk] = v
            for pk, v in b.readers.items():
                if self.floor.get(pk, 0) < v:
                    self.floor[pk] = v

    def buf(self):
        return Buf(self.floor)

    def bufs(self, n):
        return [Buf(self.floor) for _ in range(n)]

    def finalize(self, final_wait=True):
        nc = self.nc
        with nc.Block() as blk:
            for ce, deco in (("q", blk.sync), ("v", blk.vector), ("s", blk.scalar),
                             ("g", blk.gpsimd), ("t", blk.tensor)):
                prog = self.prog[ce]
                fw = final_wait if ce == "q" else None

                def body(eng, prog=prog, fw=fw):
                    for f in prog:
                        f(eng)
                    if fw:
                        for i_s in range(self.NSD):
                            if self.duse[i_s]:
                                eng.wait_ge(self.dsems[i_s],
                                            16 * self.duse[i_s])

                deco(body)


def build_program(dbg=False):
    nc = bass.Bass("TRN2", target_bir_lowering=False)
    tk = TK(nc)

    dx = nc.dram_tensor("x", [TSL, D], F32, kind="ExternalInput")
    dxhalo = nc.dram_tensor("xhalo", [3, D], F32, kind="ExternalInput")
    dident = nc.dram_tensor("ident", [128, 128], F32, kind="ExternalInput")
    dgm = nc.dram_tensor("gm", [128, NCORES], F32, kind="ExternalInput")
    dgm2 = nc.dram_tensor("gm2", [128, NCORES], F32, kind="ExternalInput")
    dconvw = nc.dram_tensor("convw", [128, NDT, 4], F32, kind="ExternalInput")
    dconvb = nc.dram_tensor("convb", [128, NDT], F32, kind="ExternalInput")
    dalog = nc.dram_tensor("alog", [128, NDT, S], F32, kind="ExternalInput")
    ddparam = nc.dram_tensor("dparam", [128, NDT], F32, kind="ExternalInput")
    ddtpb = nc.dram_tensor("dtpb", [128, NDT], F32, kind="ExternalInput")
    dxpwT = nc.dram_tensor("xpwT", [128, NDT, PD], F32, kind="ExternalInput")
    ddtpwT = nc.dram_tensor("dtpwT", [128, D], F32, kind="ExternalInput")
    dampw = nc.dram_tensor("ampw_blk", [NDT, 128, NDT, 128], F32,
                           kind="ExternalInput")
    dphw = nc.dram_tensor("phw_blk", [NDT, 128, NDT, 128], F32,
                          kind="ExternalInput")
    dampb = nc.dram_tensor("ampb", [128, NDT], F32, kind="ExternalInput")
    dphb = nc.dram_tensor("phb", [128, NDT], F32, kind="ExternalInput")
    dentwT = nc.dram_tensor("entwT_blk", [2, NDT, 128, NKC, 128], F32,
                            kind="ExternalInput")
    dentpw = nc.dram_tensor("entpw_sl", [2, 4, 128, NKC, MMF], F32,
                            kind="ExternalInput")
    dentb = nc.dram_tensor("entb_sl", [128, 2, NKC], F32, kind="ExternalInput")
    dentpb = nc.dram_tensor("entpb", [128, 2, NDT], F32, kind="ExternalInput")
    dpathw = nc.dram_tensor("pathw_blk", [2, 2, NDT, 128, NDT, 128], F32,
                            kind="ExternalInput")
    dpathb = nc.dram_tensor("pathb", [128, 2, 2, NDT], F32, kind="ExternalInput")
    dphase = nc.dram_tensor("phasec", [128, 2, 2, NDT], F32, kind="ExternalInput")
    dln = {nm: nc.dram_tensor(nm, [128, 2, NDT], F32, kind="ExternalInput")
           for nm in ("ln3g", "ln3b", "ln4g", "ln4b")}
    dout = nc.dram_tensor("out", [TSL, D], F32, kind="ExternalOutput")

    dmin = [nc.dram_tensor(f"m_in{l}", [128, 4, NDT, MMF], BF16,
                           kind="Internal") for l in range(2)]
    dmout = [nc.dram_tensor(f"m_out{l}", [128, 4, NDT, MMF], BF16,
                            kind="Internal") for l in range(2)]
    dbin = [nc.dram_tensor(f"mb_in{l}", [1, D], BF16, kind="Internal")
            for l in range(2)]
    dbout = [nc.dram_tensor(f"mb_out{l}", [1, D], BF16, kind="Internal")
             for l in range(2)]
    dcin = nc.dram_tensor("cin", [128, 2 * NDT * S], F32, kind="Internal")
    dcout = nc.dram_tensor("cout", [128 * NCORES, 2 * NDT * S], F32,
                           kind="Internal")
    dx1st = nc.dram_tensor("x1st", [TSL, D], F32, kind="Internal")
    dbcr = nc.dram_tensor("bcrows", [32, TSL], F32, kind="Internal")
    dspdt = nc.dram_tensor("spill_dt", [128, NDT * TC], F32, kind="Internal")
    dspy = nc.dram_tensor("spill_y", [128, NDT * TC], F32, kind="Internal")

    dbg_outs = {}
    if dbg:
        for nm in ("dbg_xc", "dbg_dt", "dbg_y", "dbg_h2", "dbg_h2p",
                   "dbg_l30", "dbg_l40", "dbg_e1", "dbg_l31", "dbg_p1"):
            dbg_outs[nm] = nc.dram_tensor(nm, [128, NDT, TSL], F32,
                                          kind="ExternalOutput")
    B_dbg = {nm: tk.buf() for nm in dbg_outs}

    A_ = nc.alloc_sbuf_tensor
    P_ = nc.alloc_psum_tensor

    ident = A_("identS", [128, 128], F32)
    identb = A_("identbS", [128, 128], BF16)
    gm = A_("gmS", [128, NCORES], F32)
    gm2 = A_("gm2S", [128, NCORES], F32)
    convw = A_("convwS", [128, NDT, 4], F32)
    convb = A_("convbS", [128, NDT], F32)
    alog = A_("alogS", [128, NDT, S], F32)
    Asb = A_("AsbS", [128, NDT, S], F32)
    dparam = A_("dparamS", [128, NDT], F32)
    dtpb = A_("dtpbS", [128, NDT], F32)
    ampb = A_("ampbS", [128, NDT], F32)
    phb = A_("phbS", [128, NDT], F32)
    entpb = A_("entpbS", [128, 2, NDT], F32)
    pathb = A_("pathbS", [128, 2, 2, NDT], F32)
    phasec = A_("phasecS", [128, 2, 2, NDT], F32)
    modc = A_("modcS", [128, 2, 2, NDT], F32)
    pbmod = A_("pbmodS", [128, 2, 2, NDT], F32)
    lns = {nm: A_(nm + "S", [128, 2, NDT], F32) for nm in dln}
    onescol = A_("onescolS", [128, 1], F32)
    onesrow = A_("onesrowS", [1, 128], F32)
    c_pihalf = A_("cpihalfS", [128, 1], F32)
    c_tiny = A_("ctinyS", [128, 1], F32)
    c_eps = A_("cepsS", [128, 1], F32)
    hlast = A_("hlastS", [128, NDT, S], F32)
    acore = A_("acoreS", [128, NDT * S], F32)
    bcore = A_("bcoreS", [128, NDT * S], F32)
    dts = A_("dtsS", [128, NDT], F32)
    hinit = A_("hinitS", [128, NDT * S], F32)
    lnh = A_("lnhS", [128, NDT * S], F32)
    betac = [A_(f"betacS{l}", [128, NDT], F32) for l in range(2)]
    entbb = A_("entbbS", [128, 2, NKC], BF16)
    statm = A_("statmS", [128, 1], F32)
    statq = A_("statqS", [128, 1], F32)
    statv = A_("statvS", [128, 1], F32)
    statm2 = A_("statm2S", [128, 1], F32)
    statq2 = A_("statq2S", [128, 1], F32)
    statv2 = A_("statv2S", [128, 1], F32)
    scr = A_("scrS", [128, 2048], F32)
    bcflat = A_("bcflatS", [128, TC], F32)
    cdtl = A_("cdtlS", [128, NDT], F32)


    pbk = [P_(f"pb{i}", [128, MMF], F32) for i in range(4)]
    ptf = [P_(f"ptf{i}", [128, MMF], F32) for i in range(2)]
    ptb = [P_(f"ptb{i}", [128, 128], BF16) for i in range(2)]
    pt = [ptf[0][:, 0:128], ptf[1][:, 0:128]]
    pst = [ptf[0][0:1, :], ptf[1][0:1, :]]

    (B_ident, B_identb, B_gm, B_gm2, B_convw, B_convb, B_alog, B_Asb,
     B_dparam, B_dtpb, B_ampb, B_phb, B_entpb, B_pathb, B_phase, B_mod,
     B_pbmod, B_onescol, B_onesrow, B_acore, B_bcore, B_dts, B_hinit,
     B_lnh, B_entbb, B_statm, B_statq, B_statv, B_scr, B_rowm, B_rowr,
     B_mrep, B_rrep, B_bcflat, B_statm2, B_statq2, B_statv2) = tk.bufs(37)
    statR = [(statm, statq, statv, B_statm, B_statq, B_statv),
             (statm2, statq2, statv2, B_statm2, B_statq2, B_statv2)]
    B_lns = {nm: tk.buf() for nm in lns}
    B_hlast = [[tk.buf() for _ in range(S)] for _ in range(NDT)]
    B_betac = tk.bufs(2)
    B_pb = tk.bufs(4)
    B_pt = tk.bufs(2)
    B_ptb = tk.bufs(2)
    B_min = tk.bufs(2)
    B_mout = tk.bufs(2)
    B_bin = tk.bufs(2)
    B_bout = tk.bufs(2)
    B_cin, B_cout, B_x1st, B_bcd = tk.bufs(4)

    tk.dma(ident[:], dident[:], writes=[B_ident])
    tk.dma(gm[:], dgm[:], writes=[B_gm])
    tk.dma(gm2[:], dgm2[:], writes=[B_gm2])
    tk.dma(convw[:], dconvw[:], writes=[B_convw])
    tk.dma(convb[:], dconvb[:], writes=[B_convb])
    tk.dma(alog[:], dalog[:], writes=[B_alog])
    tk.dma(dparam[:], ddparam[:], writes=[B_dparam])
    tk.dma(dtpb[:], ddtpb[:], writes=[B_dtpb])
    tk.dma(ampb[:], dampb[:], writes=[B_ampb])
    tk.dma(phb[:], dphb[:], writes=[B_phb])
    tk.dma(entpb[:], dentpb[:], writes=[B_entpb])
    tk.dma(pathb[:], dpathb[:], writes=[B_pathb])
    tk.dma(phasec[:], dphase[:], writes=[B_phase])
    for nm in lns:
        tk.dma(lns[nm][:], dln[nm][:], writes=[B_lns[nm]])
    tk.op("v", lambda e: e.memset(onescol[:], 1.0), writes=[B_onescol])
    B_cpihalf, B_ctiny, B_ceps = tk.bufs(3)
    tk.op("v", lambda e: e.memset(c_pihalf[:], PI / 2), writes=[B_cpihalf])
    tk.op("v", lambda e: e.memset(c_tiny[:], 1e-38), writes=[B_ctiny])
    tk.op("v", lambda e: e.memset(c_eps[:], EPS_LN), writes=[B_ceps])
    tk.op("v", lambda e: e.memset(onesrow[:], 1.0), writes=[B_onesrow])
    tk.op("v", lambda e: e.tensor_copy(identb[:], ident[:]),
          reads=[B_ident], writes=[B_identb])
    tk.op("s", lambda e: e.activation(Asb[:], alog[:], AF.Exp),
          reads=[B_alog], writes=[B_Asb])
    tk.op("v", lambda e: e.tensor_scalar_mul(Asb[:], Asb[:], -1.0),
          reads=[B_Asb], writes=[B_Asb])
    tk.op("v", lambda e: e.tensor_scalar(modc[:], phasec[:], PI, -2 * PI,
                                         AluOpType.is_gt, MULT),
          reads=[B_phase], writes=[B_mod])
    tk.op("v", lambda e: e.tensor_tensor(modc[:], modc[:], phasec[:], ADD),
          reads=[B_mod, B_phase], writes=[B_mod])
    tk.op("v", lambda e: e.tensor_scalar(pbmod[:], phasec[:], -PI, 2 * PI,
                                         AluOpType.is_lt, MULT),
          reads=[B_phase], writes=[B_pbmod])
    tk.op("v", lambda e: e.tensor_tensor(modc[:], modc[:], pbmod[:], ADD),
          reads=[B_mod, B_pbmod], writes=[B_mod])
    tk.op("s", lambda e: e.activation(pbmod[:], modc[:], AF.Abs),
          reads=[B_mod], writes=[B_pbmod])
    tk.op("s", lambda e: e.activation(modc[:], modc[:], AF.Sin),
          reads=[B_mod], writes=[B_mod])
    tk.op("s", lambda e: e.activation(pbmod[:], pbmod[:], AF.Sin,
                                      bias=c_pihalf[:], scale=-1.0),
          reads=[B_pbmod, B_cpihalf], writes=[B_pbmod])
    tk.op("v", lambda e: e.tensor_tensor(modc[:], modc[:], pbmod[:], ADD),
          reads=[B_mod, B_pbmod], writes=[B_mod])
    tk.op("v", lambda e: e.tensor_tensor(pbmod[:], pathb[:], modc[:], MULT),
          reads=[B_pathb, B_mod], writes=[B_pbmod])

    def mm_group(pbuf, ptile_ap, pieces, reads):
        ems = []
        n = len(pieces)
        for i, (lh, rh) in enumerate(pieces):
            ems.append(lambda e, lh=lh, rh=rh, st=(i == 0), sp=(i == n - 1):
                       e.matmul(ptile_ap, lh, rh, start=st, stop=sp))
        tk.ops("t", ems, reads=list(reads), writes=[pbuf])

    def stats_cols(x_ap, nrows, b_x, eps=EPS_LN, par=0):
        sm, sq, sv, B_sm, B_sq, B_sv = statR[par]
        tk.op("v", lambda e: e.reduce_sum(sm[0:nrows, :], x_ap, axis=AX),
              reads=[b_x], writes=[B_sm])
        tk.op("s", lambda e: e.activation(scr[0:nrows, 0:D], x_ap, AF.Square,
                                          accum_out=sq[0:nrows, :]),
              reads=[b_x], writes=[B_scr, B_sq])
        tk.op("v", lambda e: e.tensor_scalar_mul(sm[0:nrows, :],
                                                 sm[0:nrows, :], 1.0 / D),
              reads=[B_sm], writes=[B_sm])
        tk.op("v", lambda e: e.tensor_tensor(sv[0:nrows, :], sm[0:nrows, :],
                                             sm[0:nrows, :], MULT),
              reads=[B_sm], writes=[B_sv])
        tk.op("v", lambda e: e.scalar_tensor_tensor(sq[0:nrows, :],
                                                    sq[0:nrows, :], 1.0 / D,
                                                    sv[0:nrows, :], MULT, SUB),
              reads=[B_sq, B_sv], writes=[B_sq])
        tk.op("s", lambda e: e.activation(sq[0:nrows, :], sq[0:nrows, :],
                                          AF.Ln, bias=c_eps[0:nrows, :]),
              reads=[B_sq, B_ceps], writes=[B_sq])
        tk.op("s", lambda e: e.activation(sq[0:nrows, :], sq[0:nrows, :],
                                          AF.Exp, scale=-0.5),
              reads=[B_sq], writes=[B_sq])

    es_s1 = ExitStack()
    es_s2 = ExitStack()
    es_dt = ExitStack()
    es_y = ExitStack()

    s1 = lambda n, sh, dt=F32: es_s1.enter_context(nc.sbuf_tensor(n, sh, dt))
    s2 = lambda n, sh, dt=F32: es_s2.enter_context(nc.sbuf_tensor(n, sh, dt))

    yc = es_y.enter_context(nc.sbuf_tensor("yP", [128, NDT, TC], F32))
    dtc = es_dt.enter_context(nc.sbuf_tensor("dtP", [128, NDT, TC], F32))
    dtp = [dtc, dtc]
    yp = [yc, yc]
    crep = [s2(f"crepP{i}", [128, TC]) for i in range(2)]
    abuf = [s2(f"abufP{i}", [128, TC]) for i in range(3)]
    ubuf = [s2(f"ubufP{i}", [128, TC]) for i in range(3)]
    lt = s1("ltP", [128, NDT, TC + 3])
    halosave = s1("halosaveP", [128, NDT, 3])
    xpwb = s1("xpwbP", [128, NDT, PD], BF16)
    dtpwb = s1("dtpwbP", [128, D], BF16)
    dtrawb = s1("dtrawbP", [128, TC], BF16)
    onesTC = s1("onesTCP", [128, TC])
    brep = [s1(f"brepP{i}", [128, TC]) for i in range(2)]
    bbuf = [s1(f"bbufP{i}", [128, TC]) for i in range(3)]
    hbuf = [s1(f"hbufP{i}", [128, TC]) for i in range(2)]
    ylbuf = [s1(f"ylbufP{i}", [128, TC]) for i in range(2)]
    xpl = s1("xplP", [128, 2, D])
    xcbfp = s1("xcbfP", [128, NDT, TC], BF16)

    entwTbM = s1("entwTbM", [128, 2, NKC * 128], BF16)
    B_entwTbM = tk.bufs(2)
    munits = [(j, pc) for j in range(4) for pc in range(NDT)]

    def m_dma_pw(l, j):
        js = j % 2
        jb = [B_xcbf[js * NKC + kc] for kc in range(NKC)]
        tk.dma(xcbfp[:, js * NKC:(js + 1) * NKC, :], dentpw[l, j],
               reads=jb, writes=jb, ce="g")

    def m_dma_wt(l, u):
        j, pc = munits[u]
        iw = u % 2
        tk.dma(entwTbM[:, iw, :], dentwT[l, pc].rearrange("p a f -> p (a f)"),
               reads=[B_entwTbM[iw]], writes=[B_entwTbM[iw]], ce="g")

    def m_product_gen(l):
        if l == 1:
            tk.collective("AllReduce", ADD, [dmin[0].ap().opt()],
                          [dmout[0].ap().opt()], reads=[B_min[0]],
                          writes=[B_mout[0]])
            tk.collective("AllReduce", ADD, [dbin[0].ap().opt()],
                          [dbout[0].ap().opt()], reads=[B_bin[0]],
                          writes=[B_bout[0]])
        tk.dma(entbb[:, l, :], dentb[:, l, :], reads=[B_entbb],
               writes=[B_entbb], ce="g")
        m_dma_pw(l, 0)
        m_dma_wt(l, 0)
        m_dma_wt(l, 1)
        for u, (j, pc) in enumerate(munits):
            iw = u % 2
            js = j % 2
            jb = [B_xcbf[js * NKC + kc] for kc in range(NKC)]
            ptile, bpt = pbk[2 + iw], B_pb[2 + iw]
            pieces = [(entwTbM[:, iw, kc * 128:(kc + 1) * 128],
                       xcbfp[:, js * NKC + kc, :]) for kc in range(NKC)]
            mm_group(bpt, ptile[:], pieces, [B_entwTbM[iw]] + jb)
            if u + 2 < len(munits):
                m_dma_wt(l, u + 2)
            if pc == NKC and j + 1 < 4:
                m_dma_pw(l, j + 1)
            tk.op("s", lambda e, ptile=ptile:
                  e.copy(dtrawb[:], ptile[:]),
                  reads=[bpt], writes=[B_dtrawb])
            tk.dma(dmin[l][:, j, pc, :], dtrawb[:],
                   reads=[B_dtrawb], writes=[B_min[l]])
            if pc == NDT - 1:
                bt, bbt = pst[j % 2], B_pt[j % 2]
                bpieces = [(entbb[:, l, kc:kc + 1],
                            xcbfp[:, js * NKC + kc, :]) for kc in range(NKC)]
                mm_group(bbt, bt, bpieces, [B_entbb] + jb)
                tk.op("s", lambda e, bt=bt: e.copy(dtrawb[0:1, :], bt),
                      reads=[bbt], writes=[B_dtrawb])
                tk.dma(dbin[l][0:1, j * MMF:(j + 1) * MMF], dtrawb[0:1, :],
                       reads=[B_dtrawb], writes=[B_bin[l]])
            yield

    mgen = [m_product_gen(0), m_product_gen(1)]

    B_dtc = [tk.buf() for _ in range(NDT)]
    B_yc = [tk.buf() for _ in range(NDT)]
    B_dtp = [B_dtc, B_dtc]
    B_yp = [B_yc, B_yc]
    B_cdtl = tk.buf()
    B_spdt, B_spy = tk.bufs(2)
    B_lt = [tk.buf() for _ in range(NDT)]
    B_lthalo = [tk.buf() for _ in range(NDT)]
    (B_halosave, B_xpwb, B_dtpwb, B_dtrawb, B_bcsb, B_onesTC) = tk.bufs(6)
    B_brep, B_hb, B_yl = (tk.bufs(2) for _ in range(3))
    B_bb = tk.bufs(3)
    B_crep = tk.bufs(2)
    B_ab, B_ub = tk.bufs(3), tk.bufs(3)
    B_xpl = tk.bufs(2)
    B_xcbf = [tk.buf() for _ in range(NDT)]
    cacc = scr[:, 1024:1024 + TC]
    B_cacc = B_scr

    tk.op("v", lambda e: e.memset(onesTC[:], 1.0), writes=[B_onesTC])
    tk.dma(xpwb[:], dxpwT[:], writes=[B_xpwb], ce="g")
    tk.dma(dtpwb[:], ddtpwT[:], writes=[B_dtpwb], ce="g")

    for h in range(NH):
        t0 = h * TC
        if True:
            if h == 0:
                with nc.sbuf_tensor("haloT", [3, D], F32,
                                    side="right") as haloT:
                    B_halo = tk.buf()
                    tk.dma(haloT[:], dxhalo[:], writes=[B_halo])
                    stats_cols(haloT[:], 3, B_halo)
                    tk.op("v", lambda e: e.tensor_scalar(haloT[:], haloT[:],
                                                         statm[0:3, :],
                                                         statq[0:3, :], SUB, MULT),
                          reads=[B_halo, B_statm, B_statq], writes=[B_halo])
                    for dt in range(NDT):
                        ptile, bpt = pt[dt % 2], B_pt[dt % 2]
                        tk.op("t", lambda e, dt=dt, ptile=ptile:
                              e.transpose(ptile[:, 0:3],
                                          haloT[:, dt * 128:(dt + 1) * 128],
                                          ident[0:3, 0:3]),
                              reads=[B_halo, B_ident], writes=[bpt])
                        tk.op("s", lambda e, dt=dt, ptile=ptile:
                              e.copy(lt[:, dt, 0:3], ptile[:, 0:3]),
                              reads=[bpt], writes=[B_lthalo[dt]])
                    tk.retire(B_halo)
            else:
                for dt in range(NDT):
                    tk.op("v", lambda e, dt=dt:
                          e.tensor_copy(lt[:, dt, 0:3], halosave[:, dt, :]),
                          reads=[B_halosave], writes=[B_lthalo[dt]])

            for i in range(NTT):
                ix = i % 2
                tk.dma(xpl[:, ix, :], dx[t0 + i * 128:t0 + (i + 1) * 128, :],
                       reads=[B_xpl[ix]], writes=[B_xpl[ix]])
                stats_cols(xpl[:, ix, :], 128, B_xpl[ix])
                tk.op("v", lambda e, ix=ix:
                      e.tensor_scalar(xpl[:, ix, :], xpl[:, ix, :], statm[:],
                                      statq[:], SUB, MULT),
                      reads=[B_xpl[ix], B_statm, B_statq], writes=[B_xpl[ix]])
                for dt in range(NDT):
                    ptile, bpt = pt[dt % 2], B_pt[dt % 2]
                    tk.op("t", lambda e, ix=ix, dt=dt, ptile=ptile:
                          e.transpose(ptile, xpl[:, ix, dt * 128:(dt + 1) * 128],
                                      ident[:]),
                          reads=[B_xpl[ix], B_ident], writes=[bpt])
                    tk.op("s", lambda e, i=i, dt=dt, ptile=ptile:
                          e.copy(lt[:, dt, 3 + i * 128:3 + (i + 1) * 128],
                                 ptile),
                          reads=[bpt], writes=[B_lt[dt]])


        if h + 1 < NH:
            for dt in range(NDT):
                tk.op("v", lambda e, dt=dt:
                      e.tensor_copy(halosave[:, dt, :], lt[:, dt, TC:TC + 3]),
                      reads=[B_lt[dt]], writes=[B_halosave])

        if True:
            for dt in range(NDT):
                tk.op("v", lambda e, dt=dt:
                      e.tensor_scalar_mul(cacc, lt[:, dt, 0:TC],
                                          convw[:, dt, 0:1]),
                      reads=[B_lt[dt], B_lthalo[dt], B_convw], writes=[B_cacc])
                for j in range(1, 4):
                    tk.op("v", lambda e, dt=dt, j=j:
                          e.scalar_tensor_tensor(cacc, lt[:, dt, j:j + TC],
                                                 convw[:, dt, j:j + 1], cacc,
                                                 MULT, ADD),
                          reads=[B_lt[dt], B_lthalo[dt], B_convw, B_cacc],
                          writes=[B_cacc])
                tk.op("s", lambda e, dt=dt:
                      e.activation(lt[:, dt, 3:3 + TC], cacc, AF.Silu,
                                   bias=convb[:, dt:dt + 1]),
                      reads=[B_cacc, B_convb], writes=[B_lt[dt]])
                tk.op("v", lambda e, dt=dt:
                      e.tensor_copy(xcbfp[:, dt, :], lt[:, dt, 3:3 + TC]),
                      reads=[B_lt[dt]], writes=[B_xcbf[dt]])
                if dbg:
                    tk.dma(dbg_outs["dbg_xc"][:, dt, t0:t0 + TC],
                           lt[:, dt, 3:3 + TC],
                           reads=[B_lt[dt]], writes=[B_dbg["dbg_xc"]])

            ptile, bpt = pbk[0], B_pb[0]
            pieces = [(xpwb[:, c, 0:128], xcbfp[:, c, :]) for c in range(NDT)]
            mm_group(bpt, ptile[:], pieces, [B_xpwb] + B_xcbf)
            tk.op("s", lambda e, ptile=ptile: e.copy(dtrawb[:], ptile[:]),
                  reads=[bpt], writes=[B_dtrawb])
            ptile2, bpt2 = pbk[1], B_pb[1]
            pieces = [(xpwb[:, c, 128:160], xcbfp[:, c, :]) for c in range(NDT)]
            mm_group(bpt2, ptile2[0:32, :], pieces, [B_xpwb] + B_xcbf)
            tk.op("v", lambda e, ptile2=ptile2:
                  e.tensor_copy(scr[0:32, 0:TC], ptile2[0:32, :]),
                  reads=[bpt2], writes=[B_bcsb, B_scr])
            tk.dma(dbcr[:, t0:t0 + TC], scr[0:32, 0:TC],
                   reads=[B_bcsb, B_scr], writes=[B_bcd])

            for dt in range(NDT):
                ptile, bpt = pbk[2 + dt % 2], B_pb[2 + dt % 2]
                mm_group(bpt, ptile[:],
                         [(dtpwb[:, dt * 128:(dt + 1) * 128], dtrawb[:])],
                         [B_dtpwb, B_dtrawb])
                tk.op("s", lambda e, dt=dt, ptile=ptile:
                      e.activation(scr[:, 0:TC], ptile[:], AF.Exp,
                                   bias=dtpb[:, dt:dt + 1]),
                      reads=[bpt, B_dtpb], writes=[B_scr])
                tk.op("s", lambda e, dt=dt:
                      e.activation(dtp[h][:, dt, :], scr[:, 0:TC], AF.Ln,
                                   bias=onescol[:]),
                      reads=[B_scr, B_onescol], writes=[B_dtp[h][dt]])
                if dbg:
                    tk.dma(dbg_outs["dbg_dt"][:, dt, t0:t0 + TC],
                           dtp[h][:, dt, :],
                           reads=[B_dtp[h][dt]], writes=[B_dbg["dbg_dt"]])
                if h == 0:
                    tk.op("v", lambda e, dt=dt:
                          e.reduce_sum(dts[:, dt:dt + 1], dtp[h][:, dt, :],
                                       axis=AX),
                          reads=[B_dtp[h][dt]], writes=[B_dts])
                else:
                    tk.op("v", lambda e, dt=dt:
                          e.reduce_sum(statv[:], dtp[h][:, dt, :], axis=AX),
                          reads=[B_dtp[h][dt]], writes=[B_statv])
                    tk.op("v", lambda e, dt=dt:
                          e.tensor_tensor(dts[:, dt:dt + 1], dts[:, dt:dt + 1],
                                          statv[:], ADD),
                          reads=[B_dts, B_statv], writes=[B_dts])
                tk.op("v", lambda e, dt=dt, h=h:
                      e.tensor_scalar_mul(yp[h][:, dt, :], lt[:, dt, 3:3 + TC],
                                          dparam[:, dt:dt + 1]),
                      reads=[B_lt[dt], B_dparam], writes=[B_yp[h][dt]])
                tk.op("v", lambda e, dt=dt, h=h:
                      e.tensor_tensor(lt[:, dt, 3:3 + TC], lt[:, dt, 3:3 + TC],
                                      dtp[h][:, dt, :], MULT),
                      reads=[B_lt[dt], B_dtp[h][dt]], writes=[B_lt[dt]])


        def emit_reps1(s, par):
            tk.dma(bcflat[0:1, :], dbcr[s:s + 1, t0:t0 + TC],
                   reads=[B_bcd, B_bcflat], writes=[B_bcflat])
            ptile, bpt = pbk[0], B_pb[0]
            tk.op("t", lambda e, ptile=ptile:
                  e.matmul(ptile[:, 0:TC], onesrow[:], bcflat[0:1, :],
                           start=True, stop=True),
                  reads=[B_onesrow, B_bcflat], writes=[bpt])
            tk.op("s", lambda e, par=par, ptile=ptile:
                  e.copy(brep[par][:], ptile[:, 0:TC]),
                  reads=[bpt], writes=[B_brep[par]])
            tk.dma(bcflat[0:1, :], dbcr[16 + s:17 + s, t0:t0 + TC],
                   reads=[B_bcd, B_bcflat, bpt], writes=[B_bcflat])
            ptile2, bpt2 = pbk[1], B_pb[1]
            tk.op("t", lambda e, ptile2=ptile2:
                  e.matmul(ptile2[:, 0:TC], onesrow[:], bcflat[0:1, :],
                           start=True, stop=True),
                  reads=[B_onesrow, B_bcflat], writes=[bpt2])
            tk.op("s", lambda e, par=par, ptile2=ptile2:
                  e.copy(crep[par][:], ptile2[:, 0:TC]),
                  reads=[bpt2], writes=[B_crep[par]])

        def emit_a1(s, dt, ip, h=h):
            tk.op("s", lambda e, dt=dt, s=s, ip=ip, h=h:
                  e.activation(abuf[ip][:], dtp[h][:, dt, :], AF.Exp,
                               scale=Asb[:, dt, s:s + 1]),
                  reads=[B_dtp[h][dt], B_Asb], writes=[B_ab[ip]])

        iters = [(s, dt) for s in range(S) for dt in range(NDT)]

        def emit_u1(k):
            s, dt = iters[k]
            ip = k % 3
            tk.op("v", lambda e, dt=dt, ip=ip, par=s % 2:
                  e.tensor_tensor(ubuf[ip][:], lt[:, dt, 3:3 + TC],
                                  brep[par][:], MULT),
                  reads=[B_lt[dt], B_brep[s % 2]], writes=[B_ub[ip]])

        def emit_b1(k):
            ip = k % 3
            tk.op("s", lambda e, ip=ip:
                  e.activation(bbuf[ip][:], ubuf[ip][:], AF.Relu),
                  reads=[B_ub[ip]], writes=[B_bb[ip]])

        emit_reps1(0, 0)
        emit_a1(*iters[0], 0)
        emit_u1(0)
        emit_b1(0)
        emit_a1(*iters[1], 1)
        emit_u1(1)
        emit_b1(1)
        for k, (s, dt) in enumerate(iters):
            par = s % 2
            ip = k % 3
            if dt == NDT - 4 and s + 1 < S:
                emit_reps1(s + 1, (s + 1) % 2)
            if k + 2 < len(iters):
                emit_a1(*iters[k + 2], (k + 2) % 3)
                emit_u1(k + 2)
                emit_b1(k + 2)
            if h == 0:
                init, init_reads = 0.0, []
            else:
                init, init_reads = hlast[:, dt, s:s + 1], [B_hlast[dt][s]]
            hb = k % 2
            tk.op("v", lambda e, ip=ip, hb=hb, init=init:
                  e.tensor_tensor_scan(hbuf[hb][:], abuf[ip][:], bbuf[ip][:],
                                       init, MULT, ADD),
                  reads=[B_ab[ip], B_bb[ip]] + init_reads, writes=[B_hb[hb]])
            tk.op("s", lambda e, dt=dt, s=s, hb=hb:
                  e.copy(hlast[:, dt, s:s + 1], hbuf[hb][:, TC - 1:TC]),
                  reads=[B_hb[hb]], writes=[B_hlast[dt][s]])
            tk.op("v", lambda e, hb=hb, par=par:
                  e.tensor_tensor(ylbuf[hb][:], hbuf[hb][:], crep[par][:],
                                  MULT),
                  reads=[B_hb[hb], B_crep[par]], writes=[B_yl[hb]])
            tk.op("v", lambda e, dt=dt, hb=hb, h=h:
                  e.tensor_tensor(yp[h][:, dt, :], yp[h][:, dt, :],
                                  ylbuf[hb][:], ADD),
                  reads=[B_yp[h][dt], B_yl[hb]], writes=[B_yp[h][dt]])
            if k % 3 == 2 and mgen[h] is not None:
                try:
                    next(mgen[h])
                except StopIteration:
                    mgen[h] = None

        for dt in range(NDT):
            if h == 0:
                cinit, creads = 0.0, []
            else:
                cinit = cdtl[:, dt:dt + 1]
                creads = [B_cdtl]
            tk.op("v", lambda e, dt=dt, cinit=cinit, h=h:
                  e.tensor_tensor_scan(dtp[h][:, dt, :], onesTC[:],
                                       dtp[h][:, dt, :], cinit, MULT, ADD),
                  reads=[B_onesTC, B_dtp[h][dt]] + creads, writes=[B_dtp[h][dt]])

        if h == 0:
            for dt in range(NDT):
                tk.op("v", lambda e, dt=dt:
                      e.tensor_copy(cdtl[:, dt:dt + 1], dtc[:, dt, TC - 1:TC]),
                      reads=[B_dtc[dt]], writes=[B_cdtl])
            tk.dma(dspdt[:], dtc[:].rearrange("p a t -> p (a t)"),
                   reads=list(B_dtc), writes=[B_spdt])
            tk.dma(dspy[:], yc[:].rearrange("p a t -> p (a t)"),
                   reads=list(B_yc), writes=[B_spy])

    for h_ in range(2):
        while mgen[h_] is not None:
            try:
                next(mgen[h_])
            except StopIteration:
                mgen[h_] = None

    for dt in range(NDT):
        tk.op("s", lambda e, dt=dt:
              e.activation(acore[:, dt * S:(dt + 1) * S], Asb[:, dt, :], AF.Exp,
                           scale=dts[:, dt:dt + 1]),
              reads=[B_Asb, B_dts], writes=[B_acore])
        tk.op("v", lambda e, dt=dt:
              e.tensor_copy(bcore[:, dt * S:(dt + 1) * S], hlast[:, dt, :]),
              reads=[B_hlast[dt][ss] for ss in range(S)], writes=[B_bcore])
    tk.dma(dcin[:, 0:NDT * S], acore[:], reads=[B_acore], writes=[B_cin])
    tk.dma(dcin[:, NDT * S:], bcore[:], reads=[B_bcore], writes=[B_cin])
    tk.collective("AllGather", AluOpType.bypass, [dcin.ap().opt()],
                  [dcout.ap().opt()], reads=[B_cin], writes=[B_cout])
    tk.collective("AllReduce", ADD, [dmin[1].ap().opt()],
                  [dmout[1].ap().opt()], reads=[B_min[1]], writes=[B_mout[1]])
    tk.collective("AllReduce", ADD, [dbin[1].ap().opt()],
                  [dbout[1].ap().opt()], reads=[B_bin[1]], writes=[B_bout[1]])
    with nc.sbuf_tensor("agb", [128, 2, 2 * NDT * S], F32,
                        side="right") as agb:
        cmp0 = scr[:, 0:NDT * S]
        B_agb = tk.bufs(2)
        B_cmp0 = B_scr
        NS = NDT * S
        tk.dma(agb[:, 0, :], dcout[0:128, :], reads=[B_cout], writes=[B_agb[0]])
        tk.op("v", lambda e: e.memset(hinit[:], 0.0), writes=[B_hinit])
        for r in range(NCORES):
            rb = r % 2
            if r + 1 < NCORES:
                tk.dma(agb[:, 1 - rb, :], dcout[128 * (r + 1):128 * (r + 2), :],
                       reads=[B_cout, B_agb[1 - rb]], writes=[B_agb[1 - rb]])
            tk.op("v", lambda e, r=r, rb=rb:
                  e.tensor_scalar(cmp0, agb[:, rb, 0:NS], gm[:, r:r + 1],
                                  gm2[:, r:r + 1], MULT, ADD),
                  reads=[B_agb[rb], B_gm, B_gm2], writes=[B_cmp0])
            tk.op("v", lambda e: e.tensor_tensor(hinit[:], hinit[:], cmp0, MULT),
                  reads=[B_hinit, B_cmp0], writes=[B_hinit])
            tk.op("v", lambda e, r=r, rb=rb:
                  e.tensor_scalar(cmp0, agb[:, rb, NS:], gm[:, r:r + 1],
                                  None, MULT),
                  reads=[B_agb[rb], B_gm], writes=[B_cmp0])
            tk.op("v", lambda e: e.tensor_tensor(hinit[:], hinit[:], cmp0,
                                                 ADD),
                  reads=[B_hinit, B_cmp0], writes=[B_hinit])
        tk.op("s", lambda e: e.activation(lnh[:], hinit[:], AF.Ln,
                                          bias=c_tiny[:]),
              reads=[B_hinit, B_ctiny], writes=[B_lnh])
        tk.retire(*B_agb)

    for bl in (B_lt, B_lthalo, [B_halosave, B_xpwb, B_dtpwb, B_dtrawb, B_bcsb,
                                B_onesTC], B_brep, B_bb, B_hb, B_yl, B_xpl,
               B_xcbf):
        tk.retire(*bl)
    es_s1.close()

    es_p5 = ExitStack()
    es_p5t = ExitStack()
    es_pre = ExitStack()
    xnT = es_p5.enter_context(nc.sbuf_tensor("xnTP", [128, NDT, TSL], BF16, side="right"))
    B_xnT = [tk.buf() for _ in range(NDT)]
    B_yT = tk.bufs(2)
    B_x0 = tk.bufs(2)
    B_xnb = tk.bufs(2)
    B_awb = tk.bufs(2)
    B_pwbx = tk.bufs(2)

    def emit_crep2(t0, s, par):
        tk.dma(bcflat[0:1, :], dbcr[16 + s:17 + s, t0:t0 + TC],
               reads=[B_bcd, B_bcflat], writes=[B_bcflat])
        ptile2, bpt2 = pbk[1], B_pb[1]
        tk.op("t", lambda e, ptile2=ptile2:
              e.matmul(ptile2[:, 0:TC], onesrow[:], bcflat[0:1, :],
                       start=True, stop=True),
              reads=[B_onesrow, B_bcflat], writes=[bpt2])
        tk.op("s", lambda e, par=par, ptile2=ptile2:
              e.copy(crep[par][:], ptile2[:, 0:TC]),
              reads=[bpt2], writes=[B_crep[par]])

    def emit_pass2(h, crep_pre=False):
        t0 = h * TC

        def emit_w2(s, dt, ip):
            tk.op("s", lambda e, dt=dt, s=s, ip=ip:
                  e.activation(abuf[ip][:], dtc[:, dt, :], AF.Exp,
                               scale=Asb[:, dt, s:s + 1],
                               bias=lnh[:, dt * S + s:dt * S + s + 1]),
                  reads=[B_dtc[dt], B_Asb, B_lnh], writes=[B_ab[ip]])

        iters = [(s, dt) for s in range(S) for dt in range(NDT)]
        if not crep_pre:
            emit_crep2(t0, 0, 0)
        emit_w2(0, 0, 0)
        emit_w2(*iters[1], 1)
        for k, (s, dt) in enumerate(iters):
            par = s % 2
            ip = k % 3
            if k + 2 < len(iters):
                s2, dt2 = iters[k + 2]
                emit_w2(s2, dt2, (k + 2) % 3)
                if dt == NDT - 4 and s + 1 < S:
                    emit_crep2(t0, s + 1, (s + 1) % 2)
            tk.op("v", lambda e, ip=ip, par=par:
                  e.tensor_tensor(ubuf[ip][:], abuf[ip][:], crep[par][:],
                                  MULT),
                  reads=[B_ab[ip], B_crep[par]], writes=[B_ub[ip]])
            tk.op("v", lambda e, dt=dt, ip=ip:
                  e.tensor_tensor(yc[:, dt, :], yc[:, dt, :],
                                  ubuf[ip][:], ADD),
                  reads=[B_yc[dt], B_ub[ip]], writes=[B_yc[dt]])
        if dbg:
            for dt in range(NDT):
                tk.dma(dbg_outs["dbg_y"][:, dt, t0:t0 + TC], yc[:, dt, :],
                       reads=[B_yc[dt]], writes=[B_dbg["dbg_y"]])

    def emit_p5(h):
        t0 = h * TC

        def phase1(i):
            ib = i % 2
            tk.dma(x0t[:, ib, :], dx[t0 + i * 128:t0 + (i + 1) * 128, :],
                   reads=[B_x0[ib]], writes=[B_x0[ib]], ce="s")
            for dt in range(NDT):
                ptile, bpt = pt[dt % 2], B_pt[dt % 2]
                tk.op("t", lambda e, i=i, dt=dt, ptile=ptile:
                      e.transpose(ptile, yc[:, dt, i * 128:(i + 1) * 128],
                                  ident[:]),
                      reads=[B_yc[dt], B_ident], writes=[bpt])
                tk.op("s", lambda e, dt=dt, ib=ib, ptile=ptile:
                      e.copy(yTt[:, ib, dt * 128:(dt + 1) * 128], ptile),
                      reads=[bpt], writes=[B_yT[ib]])

        phase1(0)
        phase1(1)
        for i in range(NTT):
            ib = i % 2
            sm, sq, _, B_sm, B_sq, _ = statR[ib]
            stats_cols(yTt[:, ib, :], 128, B_yT[ib], par=ib)
            tk.op("v", lambda e, ib=ib, sm=sm, sq=sq:
                  e.tensor_scalar(yTt[:, ib, :], yTt[:, ib, :], sm[:],
                                  sq[:], SUB, MULT),
                  reads=[B_yT[ib], B_sm, B_sq], writes=[B_yT[ib]])
            tk.op("v", lambda e, ib=ib:
                  e.tensor_tensor(yTt[:, ib, :], yTt[:, ib, :],
                                  x0t[:, ib, :], ADD),
                  reads=[B_yT[ib], B_x0[ib]], writes=[B_yT[ib]])
            tk.dma(dx1st[t0 + i * 128:t0 + (i + 1) * 128, :], yTt[:, ib, :],
                   reads=[B_yT[ib]], writes=[B_x1st], ce="s")
            stats_cols(yTt[:, ib, :], 128, B_yT[ib], par=ib)
            tk.op("v", lambda e, ib=ib, sm=sm, sq=sq:
                  e.tensor_scalar(xnbft[:, ib, :], yTt[:, ib, :], sm[:],
                                  sq[:], SUB, MULT),
                  reads=[B_yT[ib], B_sm, B_sq], writes=[B_xnb[ib]])
            for dt in range(NDT):
                ptile, bpt = ptb[dt % 2], B_ptb[dt % 2]
                tk.op("t", lambda e, dt=dt, ib=ib, ptile=ptile:
                      e.transpose(ptile[:],
                                  xnbft[:, ib, dt * 128:(dt + 1) * 128],
                                  identb[:]),
                      reads=[B_xnb[ib], B_identb], writes=[bpt])
                tk.op("s", lambda e, i=i, dt=dt, t0=t0, ptile=ptile:
                      e.copy(xnT[:, dt, t0 + i * 128:t0 + (i + 1) * 128],
                             ptile[:]),
                      reads=[bpt], writes=[B_xnT[dt]])
            if i + 2 < NTT:
                phase1(i + 2)

    emit_pass2(1)
    awb = es_pre.enter_context(nc.sbuf_tensor("awbP", [128, 2, NDT, 128],
                                              BF16, side="right"))
    pwbx = es_pre.enter_context(nc.sbuf_tensor("pwbxP", [128, 2, NDT, 128],
                                               BF16, side="right"))
    yTt = es_p5t.enter_context(nc.sbuf_tensor("yTtP", [128, 2, D], F32,
                                              side="right"))
    x0t = es_p5t.enter_context(nc.sbuf_tensor("x0tP", [128, 2, D], F32,
                                              side="right"))
    xnbft = es_p5t.enter_context(nc.sbuf_tensor("xnbftP", [128, 2, D], BF16,
                                                side="right"))
    for pg in range(2):
        tk.dma(awb[:, pg, :, :], dampw[pg],
               reads=[B_awb[pg]], writes=[B_awb[pg]], ce="g")
        tk.dma(pwbx[:, pg, :, :], dphw[pg],
               reads=[B_pwbx[pg]], writes=[B_pwbx[pg]], ce="g")
    emit_p5(1)

    emit_crep2(0, 0, 0)
    for dt in range(NDT):
        tk.dma(dtc[:, dt, :], dspdt[:, dt * TC:(dt + 1) * TC],
               reads=[B_spdt, B_dtc[dt]], writes=[B_dtc[dt]])
        tk.dma(yc[:, dt, :], dspy[:, dt * TC:(dt + 1) * TC],
               reads=[B_spy, B_yc[dt]], writes=[B_yc[dt]])

    emit_pass2(0, crep_pre=True)
    emit_p5(0)

    tk.retire(*B_yT, *B_x0, *B_xnb)
    es_p5t.close()
    tk.retire(*B_dtc)
    for bl in (B_crep, B_ab, B_ub):
        tk.retire(*bl)
    es_s1_dummy = None
    es_s2.close()
    es_dt.close()
    tk.retire(*B_yc)
    es_y.close()

    es_ffn = ExitStack()
    h2 = es_ffn.enter_context(nc.sbuf_tensor("h2F", [128, NDT, TSL], F32))
    h2bf = es_ffn.enter_context(nc.sbuf_tensor("h2bfF", [128, NDT, TSL], BF16))
    B_h2 = [tk.bufs(2) for _ in range(NDT)]
    B_h2bf = [tk.bufs(2) for _ in range(NDT)]
    B_h2a = [b for bb in B_h2 for b in bb]
    B_h2bfa = [b for bb in B_h2bf for b in bb]

    with nc.sbuf_tensor("ampch", [128, 2, NH, MMF], F32) as ampch, \
         nc.sbuf_tensor("thch", [128, 2, NH, MMF], F32) as thch, \
         nc.sbuf_tensor("sctP", [128, MMF], F32, side="right") as sct:
        B_ach = [[tk.buf() for _ in range(NH)] for _ in range(2)]
        B_tch = [[tk.buf() for _ in range(NH)] for _ in range(2)]
        B_sct = tk.buf()
        for pq in range(8):
            for pp in range(2):
                pg = pq * 2 + pp
                wb = pg % 2
                for tc in range(NH):
                    fs = slice(tc * MMF, (tc + 1) * MMF)
                    ptile, bpt = pbk[tc % 2], B_pb[tc % 2]
                    pieces = [(awb[:, wb, c, :], xnT[:, c, fs])
                              for c in range(NDT)]
                    mm_group(bpt, ptile[:], pieces, [B_awb[wb]] + B_xnT)
                    tk.op("s", lambda e, pp=pp, tc=tc, pg=pg, ptile=ptile:
                          e.activation(ampch[:, pp, tc, :], ptile[:], AF.Sigmoid,
                                       bias=ampb[:, pg:pg + 1]),
                          reads=[bpt, B_ampb], writes=[B_ach[pp][tc]])
                    ptile2, bpt2 = pbk[2 + tc % 2], B_pb[2 + tc % 2]
                    pieces = [(pwbx[:, wb, c, :], xnT[:, c, fs])
                              for c in range(NDT)]
                    mm_group(bpt2, ptile2[:], pieces, [B_pwbx[wb]] + B_xnT)
                    tk.op("s", lambda e, pp=pp, tc=tc, pg=pg, ptile2=ptile2:
                          e.activation(thch[:, pp, tc, :], ptile2[:], AF.Tanh,
                                       bias=phb[:, pg:pg + 1]),
                          reads=[bpt2, B_phb], writes=[B_tch[pp][tc]])
                if pg + 2 < 16:
                    tk.dma(awb[:, wb, :, :], dampw[pg + 2],
                           reads=[B_awb[wb]], writes=[B_awb[wb]], ce="g")
                    tk.dma(pwbx[:, wb, :, :], dphw[pg + 2],
                           reads=[B_pwbx[wb]], writes=[B_pwbx[wb]], ce="g")
            for pp in range(2):
                pg = pq * 2 + pp
                for tc in range(NH):
                    fs = slice(tc * MMF, (tc + 1) * MMF)
                    tk.op("s", lambda e, pp=pp, tc=tc:
                          e.activation(sct[:], thch[:, pp, tc, :], AF.Abs),
                          reads=[B_tch[pp][tc]], writes=[B_sct])
                    tk.op("s", lambda e, pp=pp, tc=tc:
                          e.activation(thch[:, pp, tc, :], thch[:, pp, tc, :],
                                       AF.Sin, scale=PI),
                          reads=[B_tch[pp][tc]], writes=[B_tch[pp][tc]])
                    tk.op("s", lambda e: e.activation(sct[:], sct[:], AF.Sin,
                                                      bias=c_pihalf[:], scale=-PI),
                          reads=[B_sct, B_cpihalf], writes=[B_sct])
                    tk.op("v", lambda e, pp=pp, tc=tc:
                          e.tensor_tensor(thch[:, pp, tc, :],
                                          thch[:, pp, tc, :], sct[:], ADD),
                          reads=[B_tch[pp][tc], B_sct], writes=[B_tch[pp][tc]])
                    tk.op("v", lambda e, pp=pp, tc=tc, pg=pg, fs=fs:
                          e.tensor_tensor(h2[:, pg, fs], thch[:, pp, tc, :],
                                          ampch[:, pp, tc, :], MULT),
                          reads=[B_tch[pp][tc], B_ach[pp][tc]],
                          writes=[B_h2[pg][tc]])
                    tk.op("v", lambda e, pg=pg, fs=fs, tc=tc:
                          e.tensor_copy(h2bf[:, pg, fs], h2[:, pg, fs]),
                          reads=[B_h2[pg][tc]], writes=[B_h2bf[pg][tc]])
        for pp in range(2):
            tk.retire(*B_ach[pp], *B_tch[pp])
        tk.retire(*B_awb, *B_pwbx, B_sct)
    if dbg:
        for dt in range(NDT):
            tk.dma(dbg_outs["dbg_h2p"][:, dt, :], h2[:, dt, :],
                   reads=B_h2[dt], writes=[B_dbg["dbg_h2p"]])
    es_pre.close()
    tk.retire(*B_xnT)
    es_p5.close()

    def ffn_ln_tc(l, gname, bname, tc):
        if True:
            fs = slice(tc * MMF, (tc + 1) * MMF)
            em1 = []
            for jc in range(NDT):
                em1.append(lambda e, jc=jc, fs=fs, st=(jc == 0),
                           sp=(jc == NDT - 1):
                           e.matmul(pst[0], onescol[:], h2[:, jc, fs],
                                    start=st, stop=sp))
            tk.ops("t", em1, reads=[B_onescol] + [B_h2[jc][tc]
                                                  for jc in range(NDT)],
                   writes=[B_pt[0]])
            sq_bufs = tk.bufs(4)
            for jc in range(min(3, NDT)):
                sc = scr[:, (jc % 4) * MMF:(jc % 4) * MMF + MMF]
                tk.op("s", lambda e, jc=jc, fs=fs, sc=sc:
                      e.activation(sc, h2[:, jc, fs], AF.Square),
                      reads=[B_h2[jc][tc]], writes=[sq_bufs[jc % 4]])
            for jc in range(NDT):
                sc = scr[:, (jc % 4) * MMF:(jc % 4) * MMF + MMF]
                if jc + 3 < NDT:
                    jn = jc + 3
                    sn = scr[:, (jn % 4) * MMF:(jn % 4) * MMF + MMF]
                    tk.op("s", lambda e, jn=jn, fs=fs, sn=sn:
                          e.activation(sn, h2[:, jn, fs], AF.Square),
                          reads=[B_h2[jn][tc]], writes=[sq_bufs[jn % 4]])
                tk.op("t", lambda e, jc=jc, sc=sc, st=(jc == 0),
                      sp=(jc == NDT - 1):
                      e.matmul(pst[1], onescol[:], sc, start=st, stop=sp),
                      reads=[B_onescol, sq_bufs[jc % 4]], writes=[B_pt[1]])
            tk.op("v", lambda e: e.tensor_scalar_mul(rowm[:], pst[0], 1.0 / D),
                  reads=[B_pt[0]], writes=[B_rowm])
            tk.op("v", lambda e: e.tensor_scalar_mul(rowr[:], pst[1], 1.0 / D),
                  reads=[B_pt[1]], writes=[B_rowr])
            tk.op("v", lambda e: e.tensor_tensor(scr[0:1, 0:MMF], rowm[:],
                                                 rowm[:], MULT),
                  reads=[B_rowm], writes=[B_scr])
            tk.op("v", lambda e: e.tensor_tensor(rowr[:], rowr[:],
                                                 scr[0:1, 0:MMF], SUB),
                  reads=[B_rowr, B_scr], writes=[B_rowr])
            tk.op("s", lambda e: e.activation(rowr[:], rowr[:], AF.Ln,
                                              bias=c_eps[0:1, :]),
                  reads=[B_rowr, B_ceps], writes=[B_rowr])
            tk.op("s", lambda e: e.activation(rowr[:], rowr[:], AF.Exp,
                                              scale=-0.5),
                  reads=[B_rowr], writes=[B_rowr])
            tk.op("t", lambda e: e.matmul(pbk[0][:], onesrow[:], rowm[:],
                                          start=True, stop=True),
                  reads=[B_onesrow, B_rowm], writes=[B_pb[0]])
            tk.op("s", lambda e: e.copy(mrep[:], pbk[0][:]),
                  reads=[B_pb[0]], writes=[B_mrep])
            tk.op("t", lambda e: e.matmul(pbk[1][:], onesrow[:], rowr[:],
                                          start=True, stop=True),
                  reads=[B_onesrow, B_rowr], writes=[B_pb[1]])
            tk.op("s", lambda e: e.copy(rrep[:], pbk[1][:]),
                  reads=[B_pb[1]], writes=[B_rrep])
            nb = tk.bufs(4)
            for jc in range(NDT):
                sc = scr[:, (jc % 4) * MMF:(jc % 4) * MMF + MMF]
                bsc = nb[jc % 4]
                tk.op("v", lambda e, jc=jc, fs=fs, sc=sc:
                      e.tensor_tensor(sc, h2[:, jc, fs], mrep[:], SUB),
                      reads=[B_h2[jc][tc], B_mrep], writes=[bsc])
                tk.op("v", lambda e, sc=sc: e.tensor_tensor(sc, sc, rrep[:],
                                                            MULT),
                      reads=[bsc, B_rrep], writes=[bsc])
                tk.op("v", lambda e, jc=jc, fs=fs, sc=sc, l=l:
                      e.tensor_scalar(h2[:, jc, fs], sc,
                                      lns[gname][:, l, jc:jc + 1],
                                      lns[bname][:, l, jc:jc + 1], MULT, ADD),
                      reads=[bsc, B_lns[gname], B_lns[bname]],
                      writes=[B_h2[jc][tc]])
                tk.op("v", lambda e, jc=jc, fs=fs:
                      e.tensor_copy(h2bf[:, jc, fs], h2[:, jc, fs]),
                      reads=[B_h2[jc][tc]], writes=[B_h2bf[jc][tc]])
            tk.retire(*nb, *sq_bufs)

    with nc.sbuf_tensor("rowmF", [1, MMF], F32) as rowm, \
         nc.sbuf_tensor("rowrF", [1, MMF], F32) as rowr, \
         nc.sbuf_tensor("mrepF", [128, MMF], F32) as mrep, \
         nc.sbuf_tensor("rrepF", [128, MMF], F32) as rrep, \
         nc.sbuf_tensor("mblkF", [128, 2, NDT, MMF], BF16) as mblk, \
         nc.sbuf_tensor("enttF", [128, 2, MMF], F32) as entt, \
         nc.sbuf_tensor("browF", [1, D], BF16) as brow, \
         nc.sbuf_tensor("pwb2F", [128, PSLOT, NDT, 128], BF16) as pwb2, \
         nc.sbuf_tensor("tptF", [128, 2, MMF], F32) as tpt:
        B_mblk = tk.bufs(2)
        B_entt = tk.bufs(2)
        B_brow = tk.buf()
        B_pwb2 = tk.bufs(PSLOT)
        B_tpt = tk.bufs(2)
        for l in range(2):
            tk.dma(brow[:], dbout[l][0:1, :], reads=[B_bout[l], B_brow],
                   writes=[B_brow])
            for jc in range(NDT):
                ptile, bpt = ptb[jc % 2], B_ptb[jc % 2]
                tk.op("t", lambda e, jc=jc, ptile=ptile:
                      e.transpose(ptile[:, 0:1],
                                  brow[:, jc * 128:(jc + 1) * 128],
                                  identb[0:1, 0:1]),
                      reads=[B_brow, B_identb], writes=[bpt])
                tk.op("s", lambda e, jc=jc, l=l, ptile=ptile:
                      e.copy(betac[l][:, jc:jc + 1], ptile[:, 0:1]),
                      reads=[bpt], writes=[B_betac[l]])
            tk.op("v", lambda e, l=l:
                  e.tensor_tensor(betac[l][:], betac[l][:], entpb[:, l, :], ADD),
                  reads=[B_betac[l], B_entpb], writes=[B_betac[l]])

            for jb in range(4):
                ib = jb % 2
                tk.dma(mblk[:, ib, :, :], dmout[l][:, jb],
                       reads=[B_mout[l], B_mblk[ib]], writes=[B_mblk[ib]])
                tc = 0
                fs = slice(tc * MMF, (tc + 1) * MMF)
                for pj in range(4):
                    pjg = jb * 4 + pj
                    ptile, bpt = pbk[pj % 4], B_pb[pj % 4]
                    pieces = [(mblk[:, ib, c, pj * 128:(pj + 1) * 128],
                               h2bf[:, c, fs]) for c in range(NDT)]
                    mm_group(bpt, ptile[:], pieces,
                             [B_mblk[ib]] + [B_h2bf[c][tc]
                                             for c in range(NDT)])
                    ie = pj % 2
                    tk.op("s", lambda e, l=l, pjg=pjg, ie=ie, ptile=ptile:
                          e.activation(entt[:, ie, :], ptile[:], AF.Identity,
                                       bias=betac[l][:, pjg:pjg + 1]),
                          reads=[bpt, B_betac[l]], writes=[B_entt[ie]])
                    tk.op("v", lambda e, pjg=pjg, fs=fs, ie=ie:
                          e.tensor_tensor(h2[:, pjg, fs], h2[:, pjg, fs],
                                          entt[:, ie, :], ADD),
                          reads=[B_h2[pjg][tc], B_entt[ie]],
                          writes=[B_h2[pjg][tc]])
            if dbg and l == 1:
                for dt in range(NDT):
                    tk.dma(dbg_outs["dbg_e1"][:, dt, :], h2[:, dt, :],
                           reads=B_h2[dt], writes=[B_dbg["dbg_e1"]])
            ffn_ln_tc(l, "ln3g", "ln3b", 0)
            for jb in range(4):
                ib = jb % 2
                tk.dma(mblk[:, ib, :, :], dmout[l][:, jb],
                       reads=[B_mout[l], B_mblk[ib]], writes=[B_mblk[ib]])
                tc = 1
                fs = slice(tc * MMF, (tc + 1) * MMF)
                for pj in range(4):
                    pjg = jb * 4 + pj
                    ptile, bpt = pbk[pj % 4], B_pb[pj % 4]
                    pieces = [(mblk[:, ib, c, pj * 128:(pj + 1) * 128],
                               h2bf[:, c, fs]) for c in range(NDT)]
                    mm_group(bpt, ptile[:], pieces,
                             [B_mblk[ib]] + [B_h2bf[c][tc]
                                             for c in range(NDT)])
                    ie = pj % 2
                    tk.op("s", lambda e, l=l, pjg=pjg, ie=ie, ptile=ptile:
                          e.activation(entt[:, ie, :], ptile[:], AF.Identity,
                                       bias=betac[l][:, pjg:pjg + 1]),
                          reads=[bpt, B_betac[l]], writes=[B_entt[ie]])
                    tk.op("v", lambda e, pjg=pjg, fs=fs, ie=ie:
                          e.tensor_tensor(h2[:, pjg, fs], h2[:, pjg, fs],
                                          entt[:, ie, :], ADD),
                          reads=[B_h2[pjg][tc], B_entt[ie]],
                          writes=[B_h2[pjg][tc]])
            ffn_ln_tc(l, "ln3g", "ln3b", 1)
            if dbg and l == 1:
                for dt in range(NDT):
                    tk.dma(dbg_outs["dbg_l31"][:, dt, :], h2[:, dt, :],
                           reads=B_h2[dt], writes=[B_dbg["dbg_l31"]])
            if dbg and l == 0:
                for dt in range(NDT):
                    tk.dma(dbg_outs["dbg_l30"][:, dt, :], h2[:, dt, :],
                           reads=B_h2[dt], writes=[B_dbg["dbg_l30"]])

            steps = [(p, pjg) for p in range(2) for pjg in range(NDT)]
            gc = [0]

            def path_group(p, pjg, islot, tc, l=l):
                fs = slice(tc * MMF, (tc + 1) * MMF)
                bk, ie = gc[0] % 4, gc[0] % 2
                gc[0] += 1
                ptile, bpt = pbk[bk], B_pb[bk]
                pieces = [(pwb2[:, islot, c, :], h2bf[:, c, fs])
                          for c in range(NDT)]
                mm_group(bpt, ptile[:], pieces,
                         [B_pwb2[islot]] + [B_h2bf[c][tc]
                                            for c in range(NDT)])
                tk.op("s", lambda e, l=l, p=p, pjg=pjg, ie=ie, ptile=ptile:
                      e.activation(tpt[:, ie, :], ptile[:], AF.Identity,
                                   scale=modc[:, l, p, pjg:pjg + 1],
                                   bias=pbmod[:, l, p, pjg:pjg + 1]),
                      reads=[bpt, B_mod, B_pbmod], writes=[B_tpt[ie]])
                tk.op("v", lambda e, pjg=pjg, fs=fs, ie=ie:
                      e.tensor_tensor(h2[:, pjg, fs], h2[:, pjg, fs],
                                      tpt[:, ie, :], ADD),
                      reads=[B_h2[pjg][tc], B_tpt[ie]],
                      writes=[B_h2[pjg][tc]])

            for idx, (p, pjg) in enumerate(steps):
                if idx >= PDEF:
                    pd, pjd = steps[idx - PDEF]
                    path_group(pd, pjd, (idx - PDEF) % PSLOT, 1)
                islot = idx % PSLOT
                tk.dma(pwb2[:, islot, :, :], dpathw[l, p, pjg],
                       reads=[B_pwb2[islot]], writes=[B_pwb2[islot]], ce="g")
                path_group(p, pjg, islot, 0)
            if dbg and l == 1:
                for dt in range(NDT):
                    tk.dma(dbg_outs["dbg_p1"][:, dt, 0:MMF],
                           h2[:, dt, 0:MMF],
                           reads=[B_h2[dt][0]], writes=[B_dbg["dbg_p1"]])
            ffn_ln_tc(l, "ln4g", "ln4b", 0)
            for idx in range(len(steps) - PDEF, len(steps)):
                pd, pjd = steps[idx]
                path_group(pd, pjd, idx % PSLOT, 1)
            if dbg and l == 1:
                for dt in range(NDT):
                    tk.dma(dbg_outs["dbg_p1"][:, dt, MMF:TSL],
                           h2[:, dt, MMF:TSL],
                           reads=[B_h2[dt][1]], writes=[B_dbg["dbg_p1"]])
            ffn_ln_tc(l, "ln4g", "ln4b", 1)
            if dbg and l == 0:
                for dt in range(NDT):
                    tk.dma(dbg_outs["dbg_l40"][:, dt, :], h2[:, dt, :],
                           reads=[B_h2[dt]], writes=[B_dbg["dbg_l40"]])
            if dbg and l == 1:
                for dt in range(NDT):
                    tk.dma(dbg_outs["dbg_h2"][:, dt, :], h2[:, dt, :],
                           reads=B_h2[dt], writes=[B_dbg["dbg_h2"]])
        tk.retire(*B_mblk, *B_entt, B_brow, *B_pwb2, *B_tpt,
                  B_rowm, B_rowr, B_mrep, B_rrep)

    with nc.sbuf_tensor("h2TtF", [128, 2, D], F32) as h2Tt, \
         nc.sbuf_tensor("x1tF", [128, 2, D], F32) as x1t:
        B_h2T = tk.bufs(2)
        B_x1t = tk.bufs(2)
        def fphase1(it):
            ib = it % 2
            tk.dma(x1t[:, ib, :], dx1st[it * 128:(it + 1) * 128, :],
                   reads=[B_x1st, B_x1t[ib]], writes=[B_x1t[ib]], ce="s")
            for dt in range(NDT):
                ptile, bpt = pbk[dt % 2][:, 0:128], B_pb[dt % 2]
                tk.op("t", lambda e, it=it, dt=dt, ptile=ptile:
                      e.transpose(ptile, h2[:, dt, it * 128:(it + 1) * 128],
                                  ident[:]),
                      reads=[B_h2[dt][it // 4], B_ident], writes=[bpt])
                tk.op("s", lambda e, dt=dt, ib=ib, ptile=ptile:
                      e.copy(h2Tt[:, ib, dt * 128:(dt + 1) * 128], ptile),
                      reads=[bpt], writes=[B_h2T[ib]])

        fphase1(0)
        fphase1(1)
        for it in range(TSL // 128):
            ib = it % 2
            sm, sq, _, B_sm, B_sq, _ = statR[ib]
            stats_cols(h2Tt[:, ib, :], 128, B_h2T[ib], par=ib)
            tk.op("v", lambda e, ib=ib, sm=sm, sq=sq:
                  e.tensor_scalar(h2Tt[:, ib, :], h2Tt[:, ib, :], sm[:],
                                  sq[:], SUB, MULT),
                  reads=[B_h2T[ib], B_sm, B_sq], writes=[B_h2T[ib]])
            tk.op("v", lambda e, ib=ib:
                  e.tensor_tensor(x1t[:, ib, :], x1t[:, ib, :], h2Tt[:, ib, :],
                                  ADD),
                  reads=[B_x1t[ib], B_h2T[ib]], writes=[B_x1t[ib]])
            tk.dma(dout[it * 128:(it + 1) * 128, :], x1t[:, ib, :],
                   reads=[B_x1t[ib]])
            if it + 2 < TSL // 128:
                fphase1(it + 2)

    es_ffn.close()
    tk.finalize()
    return nc


_PROG_CACHE = {}


def _get_program(dbg=False):
    if dbg not in _PROG_CACHE:
        _PROG_CACHE[dbg] = build_program(dbg=dbg)
    return _PROG_CACHE[dbg]


def make_in_maps(inputs):
    f = np.float32
    asf = lambda a: np.ascontiguousarray(np.asarray(a, dtype=f))
    x = np.asarray(inputs["L0_attn_x"], dtype=f)[0]
    colmaj = lambda a: asf(np.asarray(a, f).reshape(NDT, 128).T)

    shared = {
        "ident": np.eye(128, dtype=f),
        "convw": asf(np.asarray(inputs["L0_attn_conv_w"], f)[:, 0, :]
                     .reshape(NDT, 128, 4).transpose(1, 0, 2)),
        "convb": colmaj(inputs["L0_attn_conv_b"]),
        "alog": asf(np.asarray(inputs["L0_attn_A_log"], f)
                    .reshape(NDT, 128, S).transpose(1, 0, 2)),
        "dparam": colmaj(inputs["L0_attn_D_param"]),
        "dtpb": colmaj(inputs["L0_attn_dt_proj_b"]),
        "xpwT": asf(np.asarray(inputs["L0_attn_x_proj_w"], f).T
                    .reshape(NDT, 128, PD).transpose(1, 0, 2)),
        "dtpwT": asf(np.asarray(inputs["L0_attn_dt_proj_w"], f).T),
        "ampw_blk": asf(np.asarray(inputs["L0_ffn_amp_w"], f)
                        .reshape(NDT, 128, NDT, 128).transpose(2, 1, 0, 3)),
        "phw_blk": asf(np.asarray(inputs["L0_ffn_ph_w"], f)
                       .reshape(NDT, 128, NDT, 128).transpose(2, 1, 0, 3)),
        "ampb": colmaj(inputs["L0_ffn_amp_b"]),
        "phb": colmaj(inputs["L0_ffn_ph_b"]),
        "entpb": asf(np.asarray(inputs["L0_ffn_ent_pb"], f)
                     .reshape(2, NDT, 128).transpose(2, 0, 1)),
        "pathw_blk": asf(np.asarray(inputs["L0_ffn_path_w"], f)
                         .reshape(2, 2, NDT, 128, NDT, 128)
                         .transpose(0, 1, 4, 3, 2, 5)),
        "pathb": asf(np.asarray(inputs["L0_ffn_path_b"], f)
                     .reshape(2, 2, NDT, 128).transpose(3, 0, 1, 2)),
        "phasec": asf(np.asarray(inputs["L0_ffn_phase"], f)
                      .reshape(2, 2, NDT, 128).transpose(3, 0, 1, 2)),
    }
    for nm, key in (("ln3g", "L0_ffn_ln3_g"), ("ln3b", "L0_ffn_ln3_b"),
                    ("ln4g", "L0_ffn_ln4_g"), ("ln4b", "L0_ffn_ln4_b")):
        shared[nm] = asf(np.asarray(inputs[key], f)
                         .reshape(2, NDT, 128).transpose(2, 0, 1))

    entw = np.asarray(inputs["L0_ffn_ent_w"], f)
    entpw = np.asarray(inputs["L0_ffn_ent_pw"], f)
    entb = np.asarray(inputs["L0_ffn_ent_b"], f)

    in_maps = []
    for c in range(NCORES):
        ks = slice(c * KSL, (c + 1) * KSL)
        g = np.zeros((128, NCORES), f)
        g[:, :c] = 1.0
        m = dict(shared)
        m["x"] = asf(x[c * TSL:(c + 1) * TSL])
        m["xhalo"] = (np.zeros((3, D), f) if c == 0
                      else asf(x[c * TSL - 3:c * TSL]))
        m["gm"] = g
        m["gm2"] = asf(1.0 - g)
        m["entwT_blk"] = asf(entw[:, :, ks].reshape(2, NDT, 128, NKC, 128)
                             .transpose(0, 1, 4, 3, 2))
        m["entpw_sl"] = asf(entpw[:, ks, :].reshape(2, NKC, 128, 4, MMF)
                            .transpose(0, 3, 2, 1, 4))
        m["entb_sl"] = asf(entb[:, ks].reshape(2, NKC, 128).transpose(2, 0, 1))
        in_maps.append(m)
    return in_maps


def kernel(**inputs):
    nc = _get_program(dbg=False)
    in_maps = make_in_maps(inputs)
    res = run_bass_kernel_spmd(nc, in_maps, core_ids=list(range(NCORES)))
    out = np.concatenate([np.asarray(res.results[c]["out"])
                          for c in range(NCORES)], axis=0)
    return out[None].astype(np.float32)
